# version 06
# baseline (speedup 1.0000x reference)
import math
from contextlib import ExitStack

import numpy as np
import concourse.bass as bass
import concourse.mybir as mybir
from concourse.bass_utils import run_bass_kernel_spmd

F32 = mybir.dt.float32
BF16 = mybir.dt.bfloat16
I32 = mybir.dt.int32
AF = mybir.ActivationFunctionType
ALU = mybir.AluOpType
AX = mybir.AxisListType
PI = float(np.pi)

SEQ, D = 2048, 1024
TT, NCH = SEQ // 128, D // 128
NEG = -1e30
EPS5 = 1e-5
EPSR = 1.1920928955078125e-07
ENGINES = ("sync", "gpsimd", "vector", "scalar", "tensor")


class Bld:
    THROTTLE = 4

    def __init__(self, nc, ctx):
        self.nc = nc
        self.q = {e: [] for e in ENGINES}
        self.sems = {e: ctx.enter_context(nc.semaphore(f"ms_{e}")) for e in ENGINES}
        self.dsems = {e: ctx.enter_context(nc.semaphore(f"dma_{e}"))
                      for e in ("sync", "gpsimd")}
        self.cnt = {e: 0 for e in ENGINES}
        self.icnt = {e: 0 for e in ENGINES}
        self.dcnt = {"sync": 0, "gpsimd": 0}
        self.lastw = {e: 0 for e in ENGINES}
        self.dlastw = {"sync": 0, "gpsimd": 0}
        self.sink = {}

    def mark(self, eng, fn):
        if self.cnt[eng] - self.lastw[eng] >= self.THROTTLE:
            self.wait(eng, eng, self.cnt[eng] - self.THROTTLE + 1)
        self.cnt[eng] += 1
        self.icnt[eng] = self.icnt.get(eng, 0) + 1
        v = self.cnt[eng]
        sem = self.sems[eng]
        self.q[eng].append(lambda _e, fn=fn, sem=sem: fn().then_inc(sem, 1))
        return v

    def run(self, eng, fn):
        self.icnt[eng] = self.icnt.get(eng, 0) + 1
        self.q[eng].append(lambda _e, fn=fn: fn())

    def wait(self, eng, on_eng, v):
        if v <= 0:
            return
        if eng == on_eng:
            if v <= self.lastw[eng]:
                return
            self.lastw[eng] = v
        sem = self.sems[on_eng]
        self.q[eng].append(lambda e, sem=sem, v=v: e.wait_ge(sem, v))

    def dma(self, queue, out, in_, **kw):
        if self.dcnt[queue] - self.dlastw[queue] >= 16 * self.THROTTLE:
            self.dwait(queue, queue, self.dcnt[queue] - 16 * (self.THROTTLE - 1))
        self.dcnt[queue] += 16
        v = self.dcnt[queue]
        sem = self.dsems[queue]
        self.q[queue].append(
            lambda e, out=out, in_=in_, kw=kw, sem=sem:
            e.dma_start(out=out, in_=in_, **kw).then_inc(sem, 16))
        return v

    def dwait(self, eng, queue, v):
        if v <= 0:
            return
        if eng == queue:
            if v <= self.dlastw[queue]:
                return
            self.dlastw[queue] = v
        sem = self.dsems[queue]
        self.q[eng].append(lambda e, sem=sem, v=v: e.wait_ge(sem, v))

    def barrier(self):
        nc = self.nc
        for e in ENGINES:
            self.cnt[e] += 1
        arrive = dict(self.cnt)
        for e in ENGINES:
            if e in self.dcnt and self.dcnt[e] > 0:
                self.dwait(e, e, self.dcnt[e])
            self.wait(e, e, arrive[e] - 1)
            sem = self.sems[e]
            snk = self.sink.get(e)
            if e == "vector":
                self.q[e].append(lambda _e, sem=sem, snk=snk:
                                 nc.vector.tensor_copy(snk[:], snk[:]).then_inc(sem, 1))
            elif e == "scalar":
                self.q[e].append(lambda _e, sem=sem, snk=snk:
                                 nc.scalar.copy(snk[:], snk[:]).then_inc(sem, 1))
            elif e == "gpsimd":
                self.q[e].append(lambda _e, sem=sem, snk=snk:
                                 nc.gpsimd.memset(snk[:], 0.0).then_inc(sem, 1))
            else:
                self.q[e].append(lambda e_, sem=sem: e_.nop().then_inc(sem, 1))
        for e in ENGINES:
            for o in ENGINES:
                if o != e:
                    self.wait(e, o, arrive[o])

    def replay(self, block):
        bq = self.q

        @block.sync
        def _(eng):
            for fn in bq["sync"]:
                fn(eng)

        @block.gpsimd
        def _(eng):
            for fn in bq["gpsimd"]:
                fn(eng)

        @block.vector
        def _(eng):
            for fn in bq["vector"]:
                fn(eng)

        @block.scalar
        def _(eng):
            for fn in bq["scalar"]:
                fn(eng)

        @block.tensor
        def _(eng):
            for fn in bq["tensor"]:
                fn(eng)


class Trk:

    def __init__(self, bld):
        self.bld = bld
        self.w = None
        self.r = {}

    def rd(self, eng):
        if self.w:
            k, v, idx = self.w
            if isinstance(k, tuple):
                self.bld.dwait(eng, k[1], v)
            elif k != eng:
                self.bld.wait(eng, k, v)
            elif eng != "tensor":
                self.bld.wait(eng, eng, v)

    def did_rd(self, key, v):
        if v > self.r.get(key, 0):
            self.r[key] = v

    def wr(self, eng):
        self.rd(eng)
        for k, v in self.r.items():
            if isinstance(k, tuple):
                self.bld.dwait(eng, k[1], v)
            elif k != eng:
                self.bld.wait(eng, k, v)

    def did_wr(self, key, v):
        idx = self.bld.icnt[key] if isinstance(key, str) else 0
        self.w = (key, v, idx)
        self.r = {}

    def dma_wr_pre(self, queue):
        if self.w:
            k, v = self.w[0], self.w[1]
            if isinstance(k, tuple):
                if k[1] != queue:
                    self.bld.dwait(queue, k[1], v)
            else:
                self.bld.wait(queue, k, v)
        for k, v in self.r.items():
            if isinstance(k, tuple):
                if k[1] != queue:
                    self.bld.dwait(queue, k[1], v)
            else:
                self.bld.wait(queue, k, v)


def cdiv(a, b):
    return (a + b - 1) // b


class Mod:
    def __init__(self, layers=(0, 1), dumps=(), parts=None, nheads=16,
                 outproj=True):
        self.layers = tuple(layers)
        self.dumps = dict(dumps)
        self.dump_aps = {}
        self.nheads = nheads
        self.outproj = outproj
        self.attn_stage = "full"
        self.parts = set(parts) if parts is not None else {
            "attn0", "inner", "moe0", "attn1", "ffn1"}

    def build(self):
        nc = bass.Bass("TRN2", target_bir_lowering=False)
        self.nc = nc
        ctx = ExitStack()
        self.declare_io(nc)
        with ctx:
            self.alloc_sbuf(nc, ctx)
            bld = Bld(nc, ctx)
            self.bld = bld
            bld.sink["vector"] = self.snkv
            bld.sink["scalar"] = self.snks
            bld.sink["gpsimd"] = self.snkg
            self.ptrk = [Trk(bld) for _ in range(8)]
            self.prr = 0
            names = ("actT sS sE harena kh vh qh sblk pblk ptblk oh opr "
                     "stats router tq tk tv lA lB t0 t1 ntbf x0 x1b small "
                     "bcast onesb iota masks ident onesf").split()
            self.trk = {n: Trk(bld) for n in names}
            self.wrtrk = [Trk(bld) for _ in range(4)]
            self._wri = 0
            self.emit_init()
            for seq in range(2):
                for layer in self.layers:
                    self.emit_layer(seq, layer)
            self.emit_final()
            block = ctx.enter_context(nc.Block())
            bld.replay(block)
        return nc

    def declare_io(self, nc):
        d = {}

        def inp(name, shape):
            d[name] = nc.dram_tensor(name, list(shape), F32, kind="ExternalInput").ap()

        inp("x", (2, SEQ, D))
        inp("WqH", (16, 128, 512)); inp("WkH", (16, 128, 512))
        inp("WvH", (16, 128, 512))
        inp("WpT", (128, NCH * D))
        for w in ("BqT", "BkT", "BvT"):
            inp(w, (128, NCH * 8))
        for w in ("AqT", "AkT", "AvT"):
            inp(w, (8, D))
        inp("g1", (1, D)); inp("g2", (1, D))
        inp("fwqH", (16, 128, 512))
        inp("fwkH", (4, 128, 512)); inp("fwvH", (4, 128, 512))
        inp("fwo", (128, NCH * D))
        inp("gateT", (4, D))
        inp("w1C", (4, 12, 128, 2048)); inp("w3C", (4, 12, 128, 2048))
        inp("w2C", (4, 12, 128, 2048))
        inp("l1wqH", (16, 128, 512)); inp("l1wkvT", (128, NCH * 128))
        inp("l1woT", (128, NCH * D))
        inp("ln_g", (1, D)); inp("ln_b", (1, D))
        inp("lno_g", (1, D)); inp("lno_b", (1, D))
        inp("WrT", (8, D))
        inp("WuC", (8, 16, 128, 2048)); inp("WgC", (8, 16, 128, 2048))
        inp("WdC", (8, 16, 128, 2048))
        self.w = d
        self.y = nc.dram_tensor("y", [2, SEQ, D], F32, kind="ExternalOutput").ap()
        self.x1 = nc.dram_tensor("x1scr", [2, SEQ, D], F32, kind="Internal").ap()
        okind = "ExternalOutput" if "oscr" in self.dumps else "Internal"
        self.oscr = nc.dram_tensor("oscr", [8, 128, SEQ], BF16, kind=okind).ap()
        self.mask_d = nc.dram_tensor("mask_d", [128, 2048], BF16, kind="Internal").ap()
        self.iota_d = nc.dram_tensor("iota_d", [128, 2048], F32, kind="Internal").ap()
        self.cs_d = nc.dram_tensor("cs_d", [32, 2048], BF16, kind="Internal").ap()
        self.sn_d = nc.dram_tensor("sn_d", [32, 2048], BF16, kind="Internal").ap()
        for name, shape in self.dumps.items():
            if shape is None:
                continue
            self.dump_aps[name] = nc.dram_tensor(
                "dump_" + name, list(shape), F32, kind="ExternalOutput").ap()

    def alloc_sbuf(self, nc, ctx):
        A = lambda name, shape, dt=BF16: ctx.enter_context(nc.sbuf_tensor(name, shape, dt))
        self.actT = A("actT", [128, NCH * SEQ])
        self.sS = A("sS", [128, TT * D])
        self.sE = A("sE", [128, TT * D])
        self.harena = A("harena", [128, 32768])
        self.wrot = [A(f"wrot{i}", [128, 2048]) for i in range(4)]
        self.bcast = A("bcast", [128, 6 * D])
        self.stats = A("stats", [128, 64], F32)
        self.router = A("router", [128, 256], F32)
        self.ident = A("ident", [128, 128])
        self.t0 = A("t0", [128, 2 * D], F32)
        self.t1 = A("t1", [128, D], F32)
        self.ntbf = A("ntbf", [128, D])
        self.small = A("small", [128, 64], F32)
        self.consts = A("consts", [128, 4], F32)
        self.onesf = A("onesf", [1, 128], F32)
        self.snkv = A("snkv", [128, 8], F32)
        self.snks = A("snks", [128, 8], F32)
        self.snkg = A("snkg", [128, 8], F32)
        self.psum = [ctx.enter_context(nc.psum_tensor(f"pb{i}", [128, 512], F32))
                     for i in range(8)]
        hbf = self.harena
        hf32 = self.harena[:].bitcast(F32)
        self.sblk = hf32[:, 0:2048]
        self.x0 = hf32[:, 0:1024]
        self.x1b = hf32[:, 1024:2048]
        self.pblk = hbf[:, 4096:6144]
        self.ptblk = hbf[:, 6144:8192]
        self.oh = hbf[:, 8192:10240]
        self.opr = hbf[:, 10240:11264]
        self.kh = hbf[:, 11264:13312]
        self.vh = hbf[:, 13312:14336]
        self.qh = hbf[:, 14336:16384]
        self.tq = hbf[0:8, 16384:18432]
        self.tk = hbf[0:8, 18432:20480]
        self.tv = hbf[0:8, 20480:22528]
        self.cs_t = hbf[0:64, 16384:18432]
        self.sn_t = hbf[0:64, 18432:20480]
        self.lA = hbf[0:8, 22528:25600]
        self.lB = hbf[:, 25600:25792]
        self.iota = hf32[:, 12896:14944]
        self.masks = hbf[:, 29888:31936]
        self.onesb = hbf[0:1, 0:512]
        self.wA = self.actT[:, 0:NCH * D]

    def _t(self, t):
        return self.trk[t] if isinstance(t, str) else t

    def pacq(self, n=1, avoid=None):
        ids = []
        for _ in range(n):
            i = self.prr
            self.prr = (self.prr + 1) % 8
            if avoid is not None and i == avoid:
                i = self.prr
                self.prr = (self.prr + 1) % 8
            self.ptrk[i].wr("tensor")
            ids.append(i)
        return ids if n > 1 else ids[0]

    def pap(self, i):
        return self.psum[i]

    def pbf(self, i):
        return self.psum[i][:].bitcast(BF16)

    def mmg(self, items, reads=()):
        bld = self.bld
        for t in reads:
            self._t(t).rd("tensor")
        for (bank, out, lhsT, rhs, st, sp) in items[:-1]:
            bld.run("tensor", lambda o=out, l=lhsT, r=rhs, s=st, p=sp:
                    self.nc.tensor.matmul(o, l, r, start=s, stop=p))
        bank, out, lhsT, rhs, st, sp = items[-1]
        v = bld.mark("tensor", lambda o=out, l=lhsT, r=rhs, s=st, p=sp:
                     self.nc.tensor.matmul(o, l, r, start=s, stop=p))
        for b in {it[0] for it in items}:
            self.ptrk[b].did_wr("tensor", v)
        for t in reads:
            self._t(t).did_rd("tensor", v)
        return v

    def op(self, eng, fn, rpsum=(), reads=(), writes=()):
        bld = self.bld
        for b in rpsum:
            self.ptrk[b].rd(eng)
        for t in reads:
            self._t(t).rd(eng)
        for t in writes:
            self._t(t).wr(eng)
        v = bld.mark(eng, fn)
        for b in rpsum:
            self.ptrk[b].did_rd(eng, v)
        for t in reads:
            self._t(t).did_rd(eng, v)
        for t in writes:
            self._t(t).did_wr(eng, v)
        return v

    def dma_into(self, queue, trkname, dst_ap, src_ap):
        t = self._t(trkname)
        t.dma_wr_pre(queue)
        v = self.bld.dma(queue, dst_ap, src_ap)
        t.did_wr(("dma", queue), v)
        return v

    def dma_from(self, queue, trkname, dst_ap, src_ap):
        t = self._t(trkname)
        if t.w:
            k, v = t.w[0], t.w[1]
            if isinstance(k, tuple):
                self.bld.dwait(queue, k[1], v)
            else:
                self.bld.wait(queue, k, v)
        v = self.bld.dma(queue, dst_ap, src_ap)
        t.did_rd(("dma", queue), v)
        return v

    def wload(self, src_ap, ncols=4096):
        i = self._wri
        self._wri = (i + 1) % 4
        t = self.wrtrk[i]
        t.dma_wr_pre("gpsimd")
        v = self.bld.dma("gpsimd", self.wrot[i][:, 0:ncols], src_ap)
        t.did_wr(("dma", "gpsimd"), v)
        return i

    def wuse(self, i):
        self.wrtrk[i].rd("tensor")
        return self.wrot[i]

    def wused(self, i, v):
        self.wrtrk[i].did_rd("tensor", v)

    def maybe_dump(self, name, sbuf_ap, trkname):
        if name not in self.dumps:
            return
        self.bld.barrier()
        self.dma_from("gpsimd", trkname, self.dump_aps[name], sbuf_ap)
        self.bld.barrier()

    def emit_init(self):
        nc, bld = self.nc, self.bld
        bld.mark("vector", lambda: nc.vector.memset(self.snkv[:], 0.0))
        bld.mark("scalar", lambda: nc.scalar.memzero(self.snks[:]))
        bld.mark("gpsimd", lambda: nc.gpsimd.memset(self.snkg[:], 0.0))
        self.op("gpsimd", lambda: nc.gpsimd.memset(self.ident[:], 0.0),
                writes=("ident",))
        self.op("gpsimd", lambda: nc.gpsimd.affine_select(
            out=self.ident[:], in_=self.ident[:], compare_op=ALU.not_equal,
            fill=1.0, base=0, pattern=[[-1, 128]], channel_multiplier=1),
            reads=("ident",), writes=("ident",))
        for m in range(4):
            sl = self.masks[:, m * 512:(m + 1) * 512]
            self.op("gpsimd", lambda sl=sl: nc.gpsimd.memset(sl, 0.0),
                    writes=("masks",))
            self.op("gpsimd", lambda sl=sl, m=m: nc.gpsimd.affine_select(
                out=sl, in_=sl, compare_op=ALU.is_ge, fill=NEG,
                base=128 * m, pattern=[[-1, 512]], channel_multiplier=1),
                reads=("masks",), writes=("masks",))
        self.dma_from("sync", "masks", self.mask_d, self.masks)
        self.op("gpsimd", lambda: nc.gpsimd.iota(
            self.iota, pattern=[[-1, 2048]], base=0, channel_multiplier=1,
            allow_small_or_imprecise_dtypes=True), writes=("iota",))
        self.dma_from("sync", "iota", self.iota_d, self.iota)
        vo = self.op("vector", lambda: nc.vector.memset(self.onesb[:], 1.0),
                     writes=("onesb",))
        bld.mark("vector", lambda: nc.vector.memset(self.consts[:, 0:1], EPS5))
        bld.mark("vector", lambda: nc.vector.memset(self.consts[:, 1:2], EPSR))
        bld.mark("vector", lambda: nc.vector.memset(self.consts[:, 2:3], 0.0))
        bld.mark("vector", lambda: nc.vector.memset(self.onesf[:], 1.0))
        import os as _os
        _vecs = [] if _os.environ.get("SKIP_BCAST") else ["g1", "g2", "ln_g", "ln_b", "lno_g", "lno_b"]
        for i, name in enumerate(_vecs):
            self.dma_into("sync", "t0", self.t0[0:1, 0:D], self.w[name])
            self.op("vector", lambda: nc.vector.tensor_copy(
                self.ntbf[0:1, :], self.t0[0:1, 0:D]), reads=("t0",), writes=("ntbf",))
            for half in range(2):
                b = self.pacq()
                self.mmg([(b, self.pap(b)[:, 0:512], self.onesb[0:1, 0:128],
                           self.ntbf[0:1, half * 512:(half + 1) * 512], True, True)],
                         reads=("ntbf", "onesb"))
                self.op("vector", lambda b=b, i=i, half=half: nc.vector.tensor_copy(
                    self.bcast[:, i * D + half * 512:i * D + (half + 1) * 512],
                    self.pap(b)[:, 0:512]), rpsum=(b,), writes=("bcast",))
        if not _os.environ.get("SKIP_ROPE"):
            self.emit_rope_tables()
            self.dma_from("sync", "tq", self.cs_d, self.cs_t[0:32, :])
            self.dma_from("sync", "tk", self.sn_d, self.sn_t[0:32, :])
        bld.barrier()

    def emit_rope_tables(self):
        nc, bld = self.nc, self.bld
        sm, t0, t1 = self.small, self.t0, self.t1
        self.op("gpsimd", lambda: nc.gpsimd.iota(
            sm[0:32, 0:1], pattern=[[0, 1]], base=0, channel_multiplier=1,
            allow_small_or_imprecise_dtypes=True), writes=("small",))
        self.op("scalar", lambda: nc.scalar.activation(
            sm[0:32, 1:2], sm[0:32, 0:1], AF.Exp,
            scale=-math.log(10000.0) / 32.0), reads=("small",), writes=("small",))
        self.op("gpsimd", lambda: nc.gpsimd.iota(
            t0[0:32, 0:2048], pattern=[[1, 2048]], base=0, channel_multiplier=0,
            allow_small_or_imprecise_dtypes=True), writes=("t0",))
        self.op("vector", lambda: nc.vector.tensor_scalar_mul(
            t0[0:32, 0:2048], t0[0:32, 0:2048], sm[0:32, 1:2]),
            reads=("small", "t0"), writes=("t0",))
        for half in range(4):
            F = t0[0:32, half * 512:(half + 1) * 512]
            self.op("vector", lambda F=F: nc.vector.tensor_scalar_mul(
                t1[0:32, 0:512], F, 1.0 / (2 * PI)), reads=("t0",), writes=("t1",))
            self.op("vector", lambda: nc.vector.tensor_copy(
                t1[0:32, 512:1024].bitcast(I32)[:, 0:512], t1[0:32, 0:512]),
                reads=("t1",), writes=("t1",))
            self.op("vector", lambda: nc.vector.tensor_copy(
                t1[0:32, 0:512], t1[0:32, 512:1024].bitcast(I32)[:, 0:512]),
                reads=("t1",), writes=("t1",))
            self.op("vector", lambda F=F: nc.vector.scalar_tensor_tensor(
                t1[0:32, 512:1024], t1[0:32, 0:512], -2 * PI, F,
                ALU.mult, ALU.add), reads=("t1", "t0"), writes=("t1",))
            self.op("vector", lambda: nc.vector.tensor_scalar(
                t1[0:32, 0:512], t1[0:32, 512:1024], PI, None, ALU.is_gt),
                reads=("t1",), writes=("t1",))
            self.op("vector", lambda F=F: nc.vector.scalar_tensor_tensor(
                F, t1[0:32, 0:512], -2 * PI, t1[0:32, 512:1024],
                ALU.mult, ALU.add), reads=("t1",), writes=("t0",))
            self.op("scalar", lambda F=F, half=half: nc.scalar.activation(
                self.sn_t[0:32, half * 512:(half + 1) * 512], F, AF.Sin),
                reads=("t0",), writes=("tk",))
            self.op("vector", lambda: nc.vector.tensor_scalar_add(
                t1[0:32, 0:512], t1[0:32, 512:1024], PI / 2),
                reads=("t1",), writes=("t1",))
            self.op("vector", lambda: nc.vector.tensor_scalar(
                t1[0:32, 512:1024], t1[0:32, 0:512], PI, None, ALU.is_gt),
                reads=("t1",), writes=("t1",))
            self.op("vector", lambda: nc.vector.scalar_tensor_tensor(
                t1[0:32, 0:512], t1[0:32, 512:1024], -2 * PI, t1[0:32, 0:512],
                ALU.mult, ALU.add), reads=("t1",), writes=("t1",))
            self.op("scalar", lambda half=half: nc.scalar.activation(
                self.cs_t[0:32, half * 512:(half + 1) * 512],
                t1[0:32, 0:512], AF.Sin), reads=("t1",), writes=("tq",))

    def tile_stats(self, u, utrk, eps, rms, store_i=None):
        nc, bld = self.nc, self.bld
        sm = self.small
        if not rms:
            self.op("vector", lambda: nc.vector.reduce_sum(
                sm[:, 0:1], u[:, 0:D], axis=AX.X, negate=True),
                reads=(utrk,), writes=("small",))
            vv = self.op("vector", lambda: nc.vector.tensor_scalar_mul(
                sm[:, 0:1], sm[:, 0:1], 1.0 / D), writes=("small",))
            bld.wait("scalar", "vector", vv)
        self.op("scalar", lambda rms=rms: nc.scalar.activation(
            self.t1[:, 0:D], u[:, 0:D], AF.Square,
            bias=self.consts[:, 2:3] if rms else sm[:, 0:1],
            accum_out=sm[:, 1:2]), reads=(utrk, "small"), writes=("t1", "small"))
        epscol = self.consts[:, 0:1] if eps == EPS5 else self.consts[:, 1:2]
        bld.run("scalar", lambda epscol=epscol: nc.scalar.activation(
            sm[:, 2:3], sm[:, 1:2], AF.Ln, bias=epscol, scale=1.0 / D))
        s2 = bld.mark("scalar", lambda: nc.scalar.activation(
            sm[:, 3:4], sm[:, 2:3], AF.Exp, scale=-0.5))
        self.trk["small"].did_wr("scalar", s2)
        if rms:
            z = self.op("vector", lambda: nc.vector.memset(sm[:, 4:5], 0.0),
                        writes=("small",))
            bld.wait("scalar", "vector", z)
        else:
            bv = self.op("vector", lambda: nc.vector.tensor_mul(
                sm[:, 4:5], sm[:, 0:1], sm[:, 3:4]), reads=("small",), writes=("small",))
            bld.wait("scalar", "vector", bv)
        if store_i is not None:
            self.op("vector", lambda i=store_i: nc.vector.tensor_copy(
                self.stats[:, 2 * i:2 * i + 2], self.small[:, 3:5]),
                reads=("small",), writes=("stats",))

    def transpose_tile(self, i):
        nc = self.nc
        for c in range(NCH):
            b = self.pacq()
            self.trk["ntbf"].rd("tensor")
            v = self.bld.mark("tensor", lambda b=b, c=c: nc.tensor.transpose(
                self.pbf(b)[:, 0:128], self.ntbf[:, c * 128:(c + 1) * 128],
                self.ident[:]))
            self.ptrk[b].did_wr("tensor", v)
            self.trk["ntbf"].did_rd("tensor", v)
            self.op("vector", lambda b=b, c=c, i=i: nc.vector.tensor_copy(
                self.actT[:, c * SEQ + i * 128:c * SEQ + (i + 1) * 128],
                self.pbf(b)[:, 0:128]), rpsum=(b,), writes=("actT",))

    def emit_actT(self, seq, mode, src, router=None):
        nc, bld = self.nc, self.bld
        xb_ = [self.x0, self.x1b]
        xn_ = ["x0", "x1b"]
        if router is not None:
            nE = 4 if router == "l0" else 8
            hf32 = self.harena[:].bitcast(F32)
            gg = hf32[:, 2048:2048 + nE * 1024]
            g2b = hf32[:, 2048 + nE * 1024:2048 + (nE + 1) * 1024]
            ggt = self.trk.setdefault("gg", Trk(bld))
            rows = self.w["gateT"] if router == "l0" else self.w["WrT"]
            if router == "l0":
                self.dma_into("sync", "t1", self.t1[0:1, :], self.w["g2"])
                for half in range(2):
                    b = self.pacq()
                    self.mmg([(b, self.pap(b)[:, :], self.onesf[:],
                               self.t1[0:1, half * 512:(half + 1) * 512],
                               True, True)], reads=("onesf", "t1"))
                    self.op("vector", lambda b=b, half=half: nc.vector.tensor_copy(
                        g2b[:, half * 512:(half + 1) * 512], self.pap(b)[:, :]),
                        rpsum=(b,), writes=(ggt,))
            for e in range(nE):
                self.dma_into("sync", "t1", self.t1[0:1, :], rows[e])
                for half in range(2):
                    b = self.pacq()
                    self.mmg([(b, self.pap(b)[:, :], self.onesf[:],
                               self.t1[0:1, half * 512:(half + 1) * 512],
                               True, True)], reads=("onesf", "t1"))
                    sl = gg[:, e * 1024 + half * 512:e * 1024 + (half + 1) * 512]
                    self.op("vector", lambda b=b, sl=sl: nc.vector.tensor_copy(
                        sl, self.pap(b)[:, :]), rpsum=(b,), writes=(ggt,))
                if router == "l0":
                    self.op("vector", lambda e=e: nc.vector.tensor_mul(
                        gg[:, e * 1024:(e + 1) * 1024],
                        gg[:, e * 1024:(e + 1) * 1024], g2b),
                        reads=(ggt,), writes=(ggt,))
        for i in range(TT):
            xb, xnm = xb_[i % 2], xn_[i % 2]
            self.dma_into("sync", xnm, xb[:], src[i * 128:(i + 1) * 128, :])
            if mode == "ln":
                self.tile_stats(xb, xnm, EPS5, rms=False, store_i=i)
                self.op("scalar", lambda xb=xb: nc.scalar.activation(
                    self.ntbf[:], xb[:], AF.Identity, scale=self.small[:, 3:4],
                    bias=self.small[:, 4:5]), reads=(xnm, "small"), writes=("ntbf",))
            elif mode == "ln_affine":
                self.tile_stats(xb, xnm, EPS5, rms=False, store_i=i)
                self.op("scalar", lambda xb=xb: nc.scalar.activation(
                    self.t0[:, 0:D], xb[:], AF.Identity, scale=self.small[:, 3:4],
                    bias=self.small[:, 4:5]), reads=(xnm, "small"), writes=("t0",))
                self.tile_stats(self.t0, "t0", EPS5, rms=False)
                self.op("scalar", lambda: nc.scalar.activation(
                    self.t0[:, D:2 * D], self.t0[:, 0:D], AF.Identity,
                    scale=self.small[:, 3:4], bias=self.small[:, 4:5]),
                    reads=("t0", "small"), writes=("t0",))
                self.op("vector", lambda: nc.vector.tensor_mul(
                    self.t0[:, D:2 * D], self.t0[:, D:2 * D],
                    self.bcast[:, 2 * D:3 * D]), reads=("bcast", "t0"), writes=("t0",))
                self.op("vector", lambda: nc.vector.tensor_add(
                    self.ntbf[:], self.t0[:, D:2 * D], self.bcast[:, 3 * D:4 * D]),
                    reads=("bcast", "t0"), writes=("ntbf",))
            elif mode == "recompute_ln":
                self.trk["stats"].rd("scalar")
                if router is None:
                    self.op("scalar", lambda xb=xb, i=i: nc.scalar.activation(
                        self.ntbf[:], xb[:], AF.Identity,
                        scale=self.stats[:, 2 * i:2 * i + 1],
                        bias=self.stats[:, 2 * i + 1:2 * i + 2]),
                        reads=(xnm, "stats"), writes=("ntbf",))
                else:
                    self.op("scalar", lambda xb=xb, i=i: nc.scalar.activation(
                        self.t0[:, 0:D], xb[:], AF.Identity,
                        scale=self.stats[:, 2 * i:2 * i + 1],
                        bias=self.stats[:, 2 * i + 1:2 * i + 2]),
                        reads=(xnm, "stats"), writes=("t0",))
                    self.op("vector", lambda: nc.vector.tensor_copy(
                        self.ntbf[:], self.t0[:, 0:D]), reads=("t0",),
                        writes=("ntbf",))
            elif mode in ("rms1", "rms2"):
                self.trk["stats"].rd("scalar")
                self.op("scalar", lambda xb=xb, i=i: nc.scalar.activation(
                    self.t0[:, 0:D], xb[:], AF.Identity,
                    scale=self.stats[:, 2 * i:2 * i + 1],
                    bias=self.stats[:, 2 * i + 1:2 * i + 2]),
                    reads=(xnm, "stats"), writes=("t0",))
                if mode == "rms2":
                    self.op("vector", lambda i=i: nc.vector.tensor_add(
                        self.t0[:, 0:D], self.t0[:, 0:D],
                        self.sE[:, i * D:(i + 1) * D]),
                        reads=("sE", "t0"), writes=("t0",))
                self.tile_stats(self.t0, "t0", EPSR, rms=True)
                gofs = 0 if mode == "rms1" else D
                self.op("scalar", lambda: nc.scalar.activation(
                    self.t0[:, D:2 * D], self.t0[:, 0:D], AF.Identity,
                    scale=self.small[:, 3:4]), reads=("t0", "small"), writes=("t0",))
                self.op("vector", lambda gofs=gofs: nc.vector.tensor_mul(
                    self.ntbf[:], self.t0[:, D:2 * D], self.bcast[:, gofs:gofs + D]),
                    reads=("bcast", "t0"), writes=("ntbf",))
            else:
                raise ValueError(mode)
            if router is not None:
                u = self.t0[:, D:2 * D] if mode == "rms2" else self.t0[:, 0:D]
                for e in range(nE):
                    self.op("vector", lambda u=u, e=e, i=i:
                            nc.vector.scalar_tensor_tensor(
                                self.t1[:, 0:D], u, 1.0,
                                gg[:, e * 1024:(e + 1) * 1024],
                                ALU.mult, ALU.mult,
                                accum_out=self.router[:, i * 16 + e:i * 16 + e + 1]),
                            reads=("t0", ggt), writes=("t1", "router"))
            self.transpose_tile(i)
        bld.barrier()

    def emit_attention(self, seq, kind):
        nc, bld = self.nc, self.bld
        H = 16
        self.dma_into("sync", "masks", self.masks, self.mask_d)
        if kind == "l0":
            qsrc, ksrc, vsrc = self.w["WqH"], self.w["WkH"], self.w["WvH"]
            qscale, kscale = 0.125, None
            self.dma_into("sync", "iota", self.iota, self.iota_d)
        elif kind == "l0i":
            qsrc, ksrc, vsrc = self.w["fwqH"], self.w["fwkH"], self.w["fwvH"]
            qscale, kscale = 0.125, None
            self.dma_into("sync", "tq", self.cs_t[0:32, :], self.cs_d)
            self.dma_into("sync", "tk", self.sn_t[0:32, :], self.sn_d)
        else:
            qsrc = self.w["l1wqH"]
            sqs = math.sqrt(64.0 ** -0.5)
            qscale, kscale = (64.0 ** -0.5) * sqs, sqs

        if self.attn_stage == "init":
            bld.barrier()
            return
        if kind == "l0":
            self.dma_into("gpsimd", "lA", self.lA[:, 0:D], self.w["AqT"])
            self.dma_into("gpsimd", "lA", self.lA[:, D:2 * D], self.w["AkT"])
            self.dma_into("gpsimd", "lA", self.lA[:, 2 * D:3 * D], self.w["AvT"])
            self.dma_into("gpsimd", "lB", self.lB[:, 0:64], self.w["BqT"])
            self.dma_into("gpsimd", "lB", self.lB[:, 64:128], self.w["BkT"])
            self.dma_into("gpsimd", "lB", self.lB[:, 128:192], self.w["BvT"])
            if self.attn_stage == "lora_dma":
                bld.barrier()
                return
            if self.attn_stage.startswith("lora_mm"):
                import os as _os
                var = self.attn_stage
                b = self.pacq()
                if var == "lora_mm_bare":
                    self.mmg([(b, self.pap(b)[0:8, :], self.lB[:, 0:8],
                               self.actT[:, 0:512], True, True)],
                             reads=("lB", "actT"))
                    bld.barrier()
                    return
                if var == "lora_mm_nt2":
                    self.mmg([(b, self.pap(b)[0:8, :], self.ntbf[:, 0:8],
                               self.actT[:, 0:512], True, True)],
                             reads=("ntbf", "actT"))
                    bld.barrier()
                    return
                if var == "lora_mm_rhs":
                    self.mmg([(b, self.pap(b)[0:8, :], self.lB[:, 0:8],
                               self.ntbf[:, 0:512], True, True)],
                             reads=("lB", "ntbf"))
                    bld.barrier()
                    return
                nmm = 1 if var == "lora_mm1" else NCH
                items = [(b, self.pap(b)[0:8, :],
                          self.lB[:, c * 8:(c + 1) * 8],
                          self.actT[:, c * SEQ + 0:c * SEQ + 512],
                          c == 0, c == nmm - 1) for c in range(nmm)]
                self.mmg(items, reads=("lB", "actT"))
                if var == "lora_mm_v":
                    self.op("vector", lambda b=b: nc.vector.tensor_copy(
                        self.tq[0:8, 0:512], self.pap(b)[0:8, :]),
                        rpsum=(b,), writes=("tq",))
                elif var == "lora_mm_nt":
                    self.op("scalar", lambda b=b: nc.scalar.activation(
                        self.ntbf[0:8, 0:512], self.pap(b)[0:8, :],
                        AF.Copy, scale=2.0), rpsum=(b,), writes=("ntbf",))
                else:
                    self.op("scalar", lambda b=b: nc.scalar.activation(
                        self.tq[0:8, 0:512], self.pap(b)[0:8, :],
                        AF.Copy, scale=2.0), rpsum=(b,), writes=("tq",))
                bld.barrier()
                return
            for ti, (tbuf, tn, bofs) in enumerate(
                    [(self.tq, "tq", 0), (self.tk, "tk", 64), (self.tv, "tv", 128)]):
                for tb in range(4):
                    b = self.pacq()
                    items = [(b, self.pap(b)[0:8, :],
                              self.lB[:, bofs + c * 8:bofs + (c + 1) * 8],
                              self.actT[:, c * SEQ + tb * 512:c * SEQ + (tb + 1) * 512],
                              c == 0, c == NCH - 1) for c in range(NCH)]
                    self.mmg(items, reads=("lB", "actT"))
                    self.op("scalar", lambda b=b, tbuf=tbuf, tb=tb: nc.scalar.activation(
                        tbuf[0:8, tb * 512:(tb + 1) * 512], self.pap(b)[0:8, :],
                        AF.Copy, scale=2.0), rpsum=(b,), writes=(tn,))

        if self.attn_stage == "lora":
            bld.barrier()
            return
        slopes = [2.0 ** (-0.5 * (h + 1)) for h in range(H)]

        for h in range(self.nheads):
            newkv = (kind == "l0") or (kind == "l0i" and h % 4 == 0) or \
                    (kind == "l1" and h == 0)
            if newkv:
                kv = {"l0": h, "l0i": h // 4, "l1": 0}[kind]
                if kind == "l1":
                    wik = self.wload(self.w["l1wkvT"], ncols=NCH * 128)
                    wiv = wik
                    ksl = lambda wt, c: wt[:, c * 128:c * 128 + 64]
                    vsl = lambda wt, c: wt[:, c * 128 + 64:c * 128 + 128]
                else:
                    wik = self.wload(ksrc[kv], ncols=512)
                    wiv = self.wload(vsrc[kv], ncols=512)
                    ksl = lambda wt, c: wt[:, c * 64:(c + 1) * 64]
                    vsl = ksl
                wkt = self.wuse(wik)
                for tb in range(4):
                    b = self.pacq()
                    items = [(b, self.pap(b)[0:64, :], ksl(wkt, c),
                              self.actT[:, c * SEQ + tb * 512:c * SEQ + (tb + 1) * 512],
                              c == 0, False) for c in range(NCH)]
                    if kind == "l0":
                        items.append((b, self.pap(b)[0:64, :],
                                      self.lA[:, D + h * 64:D + (h + 1) * 64],
                                      self.tk[:, tb * 512:(tb + 1) * 512],
                                      False, True))
                    else:
                        items[-1] = items[-1][:5] + (True,)
                    v = self.mmg(items, reads=("actT", "lA", "tk"))
                    self.wused(wik, v)
                    if kscale is None:
                        self.op("vector", lambda b=b, tb=tb: nc.vector.tensor_copy(
                            self.kh[0:64, tb * 512:(tb + 1) * 512],
                            self.pap(b)[0:64, :]), rpsum=(b,), writes=("kh",))
                    else:
                        self.op("scalar", lambda b=b, tb=tb: nc.scalar.activation(
                            self.kh[0:64, tb * 512:(tb + 1) * 512],
                            self.pap(b)[0:64, :], AF.Copy, scale=kscale),
                            rpsum=(b,), writes=("kh",))
                wvt = self.wuse(wiv)
                for tt in range(TT):
                    b = self.pacq()
                    items = [(b, self.pap(b)[:, 0:64],
                              self.actT[:, c * SEQ + tt * 128:c * SEQ + (tt + 1) * 128],
                              vsl(wvt, c), c == 0, False) for c in range(NCH)]
                    if kind == "l0":
                        items.append((b, self.pap(b)[:, 0:64],
                                      self.tv[:, tt * 128:(tt + 1) * 128],
                                      self.lA[:, 2 * D + h * 64:2 * D + (h + 1) * 64],
                                      False, True))
                    else:
                        items[-1] = items[-1][:5] + (True,)
                    v = self.mmg(items, reads=("actT", "lA", "tv"))
                    self.wused(wiv, v)
                    self.op("vector", lambda b=b, tt=tt: nc.vector.tensor_copy(
                        self.vh[:, tt * 64:(tt + 1) * 64], self.pap(b)[:, 0:64]),
                        rpsum=(b,), writes=("vh",))
                if kind == "l0i":
                    self.rope("kh")
            wiq = self.wload(qsrc[h], ncols=512)
            wqt = self.wuse(wiq)
            for tb in range(4):
                b = self.pacq()
                items = [(b, self.pap(b)[0:64, :],
                          wqt[:, c * 64:(c + 1) * 64],
                          self.actT[:, c * SEQ + tb * 512:c * SEQ + (tb + 1) * 512],
                          c == 0, False) for c in range(NCH)]
                if kind == "l0":
                    items.append((b, self.pap(b)[0:64, :],
                                  self.lA[:, h * 64:(h + 1) * 64],
                                  self.tq[:, tb * 512:(tb + 1) * 512], False, True))
                else:
                    items[-1] = items[-1][:5] + (True,)
                v = self.mmg(items, reads=("actT", "lA", "tq"))
                self.wused(wiq, v)
                self.op("scalar", lambda b=b, tb=tb: nc.scalar.activation(
                    self.qh[0:64, tb * 512:(tb + 1) * 512], self.pap(b)[0:64, :],
                    AF.Copy, scale=qscale), rpsum=(b,), writes=("qh",))
            if kind == "l0i":
                self.rope("qh")
            if self.attn_stage == "proj":
                continue

            for qt in range(TT):
                nk512 = qt // 4 + 1
                banks = [self.pacq() for _ in range(nk512)]
                items = [(banks[ks], self.pap(banks[ks])[:, :],
                          self.qh[0:64, qt * 128:(qt + 1) * 128],
                          self.kh[0:64, ks * 512:(ks + 1) * 512], True, True)
                         for ks in range(nk512)]
                self.mmg(items, reads=("qh", "kh"))
                for ks in range(nk512):
                    b = banks[ks]
                    if kind == "l0":
                        self.op("vector", lambda b=b, ks=ks, h=h:
                                nc.vector.scalar_tensor_tensor(
                                    self.sblk[:, ks * 512:(ks + 1) * 512],
                                    self.iota[:, ks * 512:(ks + 1) * 512],
                                    slopes[h], self.pap(b)[:, :], ALU.mult, ALU.add),
                                rpsum=(b,), writes=("sblk",))
                    else:
                        self.op("vector", lambda b=b, ks=ks: nc.vector.tensor_copy(
                            self.sblk[:, ks * 512:(ks + 1) * 512], self.pap(b)[:, :]),
                            rpsum=(b,), writes=("sblk",))
                ksp = qt // 4
                mm_ = qt % 4
                self.op("vector", lambda ksp=ksp, mm_=mm_: nc.vector.tensor_add(
                    self.sblk[:, ksp * 512:(ksp + 1) * 512],
                    self.sblk[:, ksp * 512:(ksp + 1) * 512],
                    self.masks[:, mm_ * 512:(mm_ + 1) * 512]),
                    reads=("sblk", "masks"), writes=("sblk",))
                if self.attn_stage == "passA":
                    continue
                nkk = nk512 * 512
                mv = self.op("vector", lambda nkk=nkk: nc.vector.reduce_max(
                    self.small[:, 0:1], self.sblk[:, 0:nkk], axis=AX.X, negate=True),
                    reads=("sblk",), writes=("small",))
                bld.wait("scalar", "vector", mv)
                ev = self.op("scalar", lambda nkk=nkk: nc.scalar.activation(
                    self.pblk[:, 0:nkk], self.sblk[:, 0:nkk], AF.Exp,
                    bias=self.small[:, 0:1], accum_out=self.small[:, 1:2]),
                    reads=("sblk", "small"), writes=("pblk", "small"))
                bld.wait("vector", "scalar", ev)
                self.op("vector", lambda: nc.vector.reciprocal(
                    self.small[:, 2:3], self.small[:, 1:2]), reads=("small",),
                    writes=("small",))
                self.op("vector", lambda nkk=nkk: nc.vector.tensor_scalar_mul(
                    self.pblk[:, 0:nkk], self.pblk[:, 0:nkk], self.small[:, 2:3]),
                    reads=("small", "pblk"), writes=("pblk",))
                if self.attn_stage == "noPV":
                    continue
                bo = self.pacq()

                def t_ev(kc):
                    b = self.pacq(avoid=bo)
                    self.trk["pblk"].rd("tensor")
                    v = bld.mark("tensor", lambda b=b, kc=kc: nc.tensor.transpose(
                        self.pbf(b)[:, 0:128],
                        self.pblk[:, kc * 128:(kc + 1) * 128], self.ident[:]))
                    self.ptrk[b].did_wr("tensor", v)
                    self.trk["pblk"].did_rd("tensor", v)
                    self.op("vector", lambda b=b, kc=kc: nc.vector.tensor_copy(
                        self.ptblk[:, kc * 128:(kc + 1) * 128],
                        self.pbf(b)[:, 0:128]), rpsum=(b,), writes=("ptblk",))

                def pv(kc):
                    self.mmg([(bo, self.pap(bo)[0:64, 0:128],
                               self.vh[:, kc * 64:(kc + 1) * 64],
                               self.ptblk[:, kc * 128:(kc + 1) * 128],
                               kc == 0, kc == qt)], reads=("vh", "ptblk"))

                t_ev(0)
                for kc in range(qt + 1):
                    if kc + 1 <= qt:
                        t_ev(kc + 1)
                    pv(kc)
                self.op("vector", lambda b=bo, h=h, qt=qt: nc.vector.tensor_copy(
                    self.oh[(h % 2) * 64:(h % 2) * 64 + 64, qt * 128:(qt + 1) * 128],
                    self.pap(b)[0:64, 0:128]), rpsum=(bo,), writes=("oh",))
            if h % 2 == 1 and self.attn_stage == "full":
                self.dma_from("sync", "oh", self.oscr[h // 2], self.oh[:])

        if not self.outproj:
            bld.barrier()
            return
        wname = {"l0": "WpT", "l0i": "fwo", "l1": "l1woT"}[kind]
        self.dma_into("gpsimd", "actT", self.wA, self.w[wname])
        groups = [[qt] for qt in range(TT)]
        for grp in groups:
            g0, gl = grp[0], len(grp)
            for pair in range(8):
                self.dma_into("sync", "opr",
                              self.opr[:, pair * 128:pair * 128 + gl * 128],
                              self.oscr[pair][:, g0 * 128:(g0 + gl) * 128])
            for qt in grp:
                for do in range(2):
                    b = self.pacq()
                    items = [(b, self.pap(b)[:, :],
                              self.opr[:, pair * 128 + (qt - g0) * 128:
                                        pair * 128 + (qt - g0 + 1) * 128],
                              self.wA[:, pair * D + do * 512:pair * D + do * 512 + 512],
                              pair == 0, pair == 7) for pair in range(8)]
                    self.mmg(items, reads=("opr", "actT"))
                    if kind == "l0":
                        self.op("scalar", lambda b=b, qt=qt, do=do: nc.scalar.activation(
                            self.sS[:, qt * D + do * 512:qt * D + (do + 1) * 512],
                            self.pap(b)[:, :], AF.Copy), rpsum=(b,), writes=("sS",))
                    elif kind == "l0i":
                        self.op("scalar", lambda b=b, qt=qt, do=do: nc.scalar.activation(
                            self.sE[:, qt * D + do * 512:qt * D + (do + 1) * 512],
                            self.pap(b)[:, :], AF.Copy), rpsum=(b,), writes=("sE",))
                    else:
                        self.op("vector", lambda b=b, do=do: nc.vector.tensor_copy(
                            self.t0[:, do * 512:(do + 1) * 512], self.pap(b)[:, :]),
                            rpsum=(b,), writes=("t0",))
                if kind == "l1":
                    self.tile_stats(self.t0, "t0", EPS5, rms=False)
                    self.op("scalar", lambda: nc.scalar.activation(
                        self.t1[:, 0:D], self.t0[:, 0:D], AF.Identity,
                        scale=self.small[:, 3:4], bias=self.small[:, 4:5]),
                        reads=("t0", "small"), writes=("t1",))
                    self.op("vector", lambda: nc.vector.tensor_mul(
                        self.t1[:, 0:D], self.t1[:, 0:D], self.bcast[:, 4 * D:5 * D]),
                        reads=("bcast", "t1"), writes=("t1",))
                    self.op("vector", lambda qt=qt: nc.vector.tensor_add(
                        self.sS[:, qt * D:(qt + 1) * D], self.t1[:, 0:D],
                        self.bcast[:, 5 * D:6 * D]),
                        reads=("bcast", "t1"), writes=("sS",))
        bld.barrier()

    def rope(self, bufname):
        nc = self.nc
        buf = getattr(self, bufname)
        hbf = self.harena
        x2c = self.pblk[0:32, :]
        p2 = hbf[0:32, 0:2048]
        p3 = hbf[0:32, 2048:4096]
        p4 = self.ptblk[0:32, :]
        cs, sn = self.cs_t[0:32, :], self.sn_t[0:32, :]
        self.op("vector", lambda: nc.vector.tensor_copy(x2c, buf[32:64, :]),
                reads=(bufname,), writes=("pblk",))
        self.op("vector", lambda: nc.vector.tensor_mul(p2, x2c, sn),
                reads=("pblk", "tk"), writes=("sblk",))
        self.op("vector", lambda: nc.vector.tensor_mul(p3, x2c, cs),
                reads=("pblk", "tq"), writes=("sblk",))
        self.op("vector", lambda: nc.vector.tensor_mul(p4, buf[0:32, :], sn),
                reads=(bufname, "tk"), writes=("ptblk",))
        self.op("vector", lambda: nc.vector.tensor_mul(x2c, buf[0:32, :], cs),
                reads=(bufname, "tq"), writes=("pblk",))
        self.op("vector", lambda: nc.vector.tensor_sub(buf[0:32, :], x2c, p2),
                reads=("pblk", "sblk"), writes=(bufname,))
        self.op("vector", lambda: nc.vector.tensor_add(buf[32:64, :], p3, p4),
                reads=("sblk", "ptblk"), writes=(bufname,))

    def emit_router(self, kind):
        nc, bld = self.nc, self.bld
        nE = 4 if kind == "l0" else 8
        for i in range(TT):
            base = i * 16
            pr = self.router[:, base:base + 8]
            cb = self.router[:, base + 8:base + 16]
            sm = self.small
            mv = self.op("vector", lambda pr=pr, nE=nE: nc.vector.reduce_max(
                sm[:, 0:1], pr[:, 0:nE], axis=AX.X, negate=True),
                reads=("router",), writes=("small",))
            bld.wait("scalar", "vector", mv)
            ev = self.op("scalar", lambda nE=nE, pr=pr: nc.scalar.activation(
                pr[:, 0:nE], pr[:, 0:nE], AF.Exp,
                bias=sm[:, 0:1], accum_out=sm[:, 1:2]),
                reads=("router", "small"), writes=("router", "small"))
            bld.wait("vector", "scalar", ev)
            self.op("vector", lambda: nc.vector.reciprocal(
                sm[:, 2:3], sm[:, 1:2]), reads=("small",), writes=("small",))
            self.op("vector", lambda pr=pr, nE=nE: nc.vector.tensor_scalar_mul(
                pr[:, 0:nE], pr[:, 0:nE], sm[:, 2:3]),
                reads=("small", "router"), writes=("router",))
            if kind == "l0":
                self.op("vector", lambda pr=pr: nc.vector.reduce_max(
                    sm[:, 0:1], pr[:, 0:4], axis=AX.X), reads=("router",),
                    writes=("small",))
                self.op("vector", lambda pr=pr: nc.vector.tensor_scalar(
                    sm[:, 8:12], pr[:, 0:4], sm[:, 0:1], None, ALU.is_ge),
                    reads=("router", "small"), writes=("small",))
                self.cumsum_mask(sm, 8, 4, 16)
                self.op("vector", lambda pr=pr: nc.vector.scalar_tensor_tensor(
                    sm[:, 20:24], sm[:, 16:20], -1e38, pr[:, 0:4], ALU.mult, ALU.add),
                    reads=("router",), writes=("small",))
                self.op("vector", lambda: nc.vector.reduce_max(
                    sm[:, 1:2], sm[:, 20:24], axis=AX.X), writes=("small",))
                self.op("vector", lambda: nc.vector.tensor_scalar(
                    sm[:, 8:12], sm[:, 20:24], sm[:, 1:2], None, ALU.is_ge),
                    writes=("small",))
                self.cumsum_mask(sm, 8, 4, 24)
                self.op("vector", lambda: nc.vector.tensor_add(
                    sm[:, 16:20], sm[:, 16:20], sm[:, 24:28]), writes=("small",))
                self.op("vector", lambda: nc.vector.tensor_add(
                    sm[:, 2:3], sm[:, 0:1], sm[:, 1:2]), writes=("small",))
                self.op("vector", lambda: nc.vector.reciprocal(
                    sm[:, 3:4], sm[:, 2:3]), writes=("small",))
                self.op("vector", lambda pr=pr: nc.vector.tensor_mul(
                    sm[:, 20:24], pr[:, 0:4], sm[:, 16:20]),
                    reads=("router",), writes=("small",))
                self.op("vector", lambda cb=cb: nc.vector.tensor_scalar_mul(
                    cb[:, 0:4], sm[:, 20:24], sm[:, 3:4]),
                    reads=("small",), writes=("router",))
            else:
                self.op("vector", lambda pr=pr: nc.vector.reduce_max(
                    sm[:, 0:1], pr[:, 0:8], axis=AX.X), reads=("router",),
                    writes=("small",))
                self.op("vector", lambda pr=pr: nc.vector.tensor_scalar(
                    sm[:, 8:16], pr[:, 0:8], sm[:, 0:1], None, ALU.is_ge),
                    reads=("router", "small"), writes=("small",))
                self.cumsum_mask(sm, 8, 8, 16)
                self.op("vector", lambda cb=cb: nc.vector.tensor_scalar(
                    cb[:, 0:8], sm[:, 16:24], sm[:, 0:1], 2.0, ALU.mult, ALU.mult),
                    reads=("small",), writes=("router",))
                self.op("vector", lambda cb=cb: nc.vector.memset(cb[:, 7:8], 1.0),
                        writes=("router",))

    def cumsum_mask(self, sm, src, n, dst):
        nc = self.nc
        self.op("vector", lambda: nc.vector.tensor_copy(
            sm[:, 32:32 + n], sm[:, src:src + n]), writes=("small",))
        cur = 32
        oth = 40
        sh = 1
        while sh < n:
            self.op("vector", lambda cur=cur, oth=oth, sh=sh: nc.vector.tensor_copy(
                sm[:, oth:oth + sh], sm[:, cur:cur + sh]), writes=("small",))
            self.op("vector", lambda cur=cur, oth=oth, sh=sh: nc.vector.tensor_add(
                sm[:, oth + sh:oth + n], sm[:, cur + sh:cur + n], sm[:, cur:cur + n - sh]),
                writes=("small",))
            cur, oth = oth, cur
            sh *= 2
        self.op("vector", lambda cur=cur: nc.vector.tensor_scalar(
            sm[:, 48:48 + n], sm[:, cur:cur + n], 1.0, None, ALU.is_equal),
            writes=("small",))
        self.op("vector", lambda dst=dst: nc.vector.tensor_mul(
            sm[:, dst:dst + n], sm[:, src:src + n], sm[:, 48:48 + n]),
            writes=("small",))

    def emit_moe(self, seq, kind):
        nc, bld = self.nc, self.bld
        self.emit_router(kind)
        if kind == "l0":
            nE, HC, NG = 4, 24, 12
            wup, wgt, wdn = self.w["w1C"], self.w["w3C"], self.w["w2C"]
        else:
            nE, HC, NG = 8, 32, 16
            wup, wgt, wdn = self.w["WgC"], self.w["WuC"], self.w["WdC"]
            z = self.op("gpsimd", lambda: nc.gpsimd.memset(self.sE[:], 0.0),
                        writes=("sE",))
        ones_col = self.small[:, 63:64]
        oz = self.op("vector", lambda: nc.vector.memset(ones_col, 1.0),
                     writes=("small",))
        for e in range(nE):
            for blk in range(2):
                for g in range(NG):
                    wi1 = self.wload(wup[e, g], ncols=2048)
                    wi3 = self.wload(wgt[e, g], ncols=2048)
                    w1t = self.wrot[wi1]
                    w3t = self.wrot[wi3]
                    for hcl in range(2):
                        hc = g * 2 + hcl
                        for half in range(2):
                            tok0 = blk * 1024 + half * 512
                            bA = self.pacq()
                            self.wrtrk[wi1].rd("tensor")
                            itemsA = [(bA, self.pap(bA)[:, :],
                                       w1t[:, hcl * 1024 + c * 128:hcl * 1024 + (c + 1) * 128],
                                       self.actT[:, c * SEQ + tok0:c * SEQ + tok0 + 512],
                                       c == 0, c == NCH - 1) for c in range(NCH)]
                            vA = self.mmg(itemsA, reads=("actT",))
                            self.wused(wi1, vA)
                            bB = self.pacq()
                            self.wrtrk[wi3].rd("tensor")
                            itemsB = [(bB, self.pap(bB)[:, :],
                                       w3t[:, hcl * 1024 + c * 128:hcl * 1024 + (c + 1) * 128],
                                       self.actT[:, c * SEQ + tok0:c * SEQ + tok0 + 512],
                                       c == 0, c == NCH - 1) for c in range(NCH)]
                            vB = self.mmg(itemsB, reads=("actT",))
                            self.wused(wi3, vB)
                            sv = self.op("scalar", lambda bA=bA: nc.scalar.activation(
                                self.t1[:, 0:512], self.pap(bA)[:, :], AF.Silu),
                                rpsum=(bA,), writes=("t1",))
                            bld.wait("vector", "scalar", sv)
                            self.op("vector", lambda bB=bB, hc=hc, half=half:
                                    nc.vector.tensor_mul(
                                        self.harena[:, hc * 1024 + half * 512:
                                                    hc * 1024 + (half + 1) * 512],
                                        self.t1[:, 0:512], self.pap(bB)[:, :]),
                                    rpsum=(bB,), reads=("t1",), writes=("harena",))
                for tokblk in range(2):
                    tok0 = blk * 1024 + tokblk * 512
                    tile0 = tok0 // 128
                    banks = self.pacq(8)
                    for g in range(NG):
                        wi2 = self.wload(wdn[e, g], ncols=2048)
                        w2t = self.wrot[wi2]
                        items = []
                        for hcl in range(2):
                            hc = g * 2 + hcl
                            for st in range(4):
                                for do in range(2):
                                    items.append((
                                        banks[st * 2 + do],
                                        self.pap(banks[st * 2 + do])[:, :],
                                        self.harena[:, hc * 1024 + tokblk * 512 +
                                                    st * 128:hc * 1024 + tokblk * 512 +
                                                    (st + 1) * 128],
                                        w2t[:, hcl * 1024 + do * 512:
                                            hcl * 1024 + (do + 1) * 512],
                                        hc == 0, hc == HC - 1))
                        self.wrtrk[wi2].rd("tensor")
                        v = self.mmg(items, reads=("harena",))
                        self.wused(wi2, v)
                    for st in range(4):
                        for do in range(2):
                            b = banks[st * 2 + do]
                            tile = tile0 + st
                            if kind == "l0":
                                comb = self.router[:, tile * 16 + 8 + e:tile * 16 + 9 + e]
                            else:
                                comb = (ones_col if e == 7 else
                                        self.router[:, tile * 16 + 8 + e:tile * 16 + 9 + e])
                            self.op("vector", lambda b=b, tile=tile, do=do, comb=comb:
                                    nc.vector.scalar_tensor_tensor(
                                        self.sE[:, tile * D + do * 512:
                                                tile * D + (do + 1) * 512],
                                        self.pap(b)[:, :], comb,
                                        self.sE[:, tile * D + do * 512:
                                                tile * D + (do + 1) * 512],
                                        ALU.mult, ALU.add),
                                    rpsum=(b,), reads=("router", "small"),
                                    writes=("sE",))
        bld.barrier()

    def emit_combine(self, seq, layer):
        nc, bld = self.nc, self.bld
        src = self.w["x"][seq] if layer == 0 else self.x1[seq]
        dst = self.x1[seq] if layer == 0 else self.y[seq]
        xb_ = [self.x0, self.x1b]
        xn_ = ["x0", "x1b"]
        for i in range(TT):
            xb, xnm = xb_[i % 2], xn_[i % 2]
            self.dma_into("sync", xnm, xb[:], src[i * 128:(i + 1) * 128, :])
            if layer == 0:
                self.trk["stats"].rd("scalar")
                self.op("scalar", lambda xb=xb, i=i: nc.scalar.activation(
                    self.t0[:, 0:D], xb[:], AF.Identity,
                    scale=self.stats[:, 2 * i:2 * i + 1],
                    bias=self.stats[:, 2 * i + 1:2 * i + 2]),
                    reads=(xnm, "stats"), writes=("t0",))
                self.op("vector", lambda i=i: nc.vector.tensor_add(
                    self.t0[:, 0:D], self.t0[:, 0:D], self.sE[:, i * D:(i + 1) * D]),
                    reads=("sE", "t0"), writes=("t0",))
                gv = self.op("scalar", lambda: nc.scalar.activation(
                    self.t1[:, 0:D], self.t0[:, 0:D], AF.Silu),
                    reads=("t0",), writes=("t1",))
            else:
                gv = self.op("scalar", lambda i=i: nc.scalar.activation(
                    self.t1[:, 0:D], self.sE[:, i * D:(i + 1) * D], AF.Silu),
                    reads=("sE",), writes=("t1",))
            self.op("vector", lambda i=i: nc.vector.tensor_mul(
                self.t1[:, 0:D], self.t1[:, 0:D], self.sS[:, i * D:(i + 1) * D]),
                reads=("sS", "t1"), writes=("t1",))
            ov = self.op("vector", lambda xb=xb: nc.vector.tensor_add(
                self.t0[:, 0:D], xb[:], self.t1[:, 0:D]),
                reads=(xnm, "t1"), writes=("t0",))
            self.dma_from("sync", "t0", dst[i * 128:(i + 1) * 128, :], self.t0[:, 0:D])
        bld.barrier()

    def emit_layer(self, seq, layer):
        x = self.w["x"]
        if layer == 0:
            self.emit_actT(seq, "ln", x[seq])
            self.maybe_dump(f"actT_{seq}_0", self.actT[:], "actT")
            if "attn0" in self.parts:
                self.emit_attention(seq, "l0")
                self.maybe_dump(f"sS_{seq}_0", self.sS[:], "sS")
            self.emit_actT(seq, "rms1", x[seq])
            if "inner" in self.parts:
                self.emit_attention(seq, "l0i")
                self.maybe_dump(f"sE_{seq}_0", self.sE[:], "sE")
            self.emit_actT(seq, "rms2", x[seq], router="l0" if "moe0" in self.parts else None)
            if "moe0" in self.parts:
                self.emit_moe(seq, "l0")
                self.maybe_dump(f"moe_{seq}_0", self.sE[:], "sE")
            self.emit_combine(seq, 0)
        else:
            self.emit_actT(seq, "ln_affine", self.x1[seq])
            if "attn1" in self.parts:
                self.emit_attention(seq, "l1")
                self.maybe_dump(f"sS_{seq}_1", self.sS[:], "sS")
            self.emit_actT(seq, "recompute_ln", self.x1[seq], router="l1" if "ffn1" in self.parts else None)
            if "ffn1" in self.parts:
                self.emit_moe(seq, "l1")
                self.maybe_dump(f"moe_{seq}_1", self.sE[:], "sE")
            self.emit_combine(seq, 1)

    def emit_final(self):
        bld = self.bld
        bld.barrier()
        for qn in ("sync", "gpsimd"):
            if bld.dcnt[qn]:
                bld.dwait(qn, qn, bld.dcnt[qn])


def _chunk(M):
    Din, F = M.shape
    c = Din // 128
    return np.ascontiguousarray(
        M.reshape(c, 128, F).transpose(1, 0, 2).reshape(128, c * F))


def _heads(M):
    H = M.shape[1] // 64
    return np.ascontiguousarray(
        np.stack([_chunk(M[:, h * 64:(h + 1) * 64]) for h in range(H)]))


def prep_inputs(inputs):
    f = np.float32
    T = lambda a: np.ascontiguousarray(np.asarray(a, f).T)
    w = {}
    w["WqH"] = _heads(T(inputs["L0_attn_Wq"]))
    w["WkH"] = _heads(T(inputs["L0_attn_Wk"]))
    w["WvH"] = _heads(T(inputs["L0_attn_Wv"]))
    w["WpT"] = _chunk(T(inputs["L0_attn_Wp"]))
    w["BqT"] = _chunk(T(inputs["L0_attn_Bq"]))
    w["BkT"] = _chunk(T(inputs["L0_attn_Bk"]))
    w["BvT"] = _chunk(T(inputs["L0_attn_Bv"]))
    w["AqT"] = T(inputs["L0_attn_Aq"])
    w["AkT"] = T(inputs["L0_attn_Ak"])
    w["AvT"] = T(inputs["L0_attn_Av"])
    w["g1"] = np.asarray(inputs["L0_ffn_g1"], f).reshape(1, D)
    w["g2"] = np.asarray(inputs["L0_ffn_g2"], f).reshape(1, D)
    w["fwqH"] = _heads(np.asarray(inputs["L0_ffn_wq"], f))
    w["fwkH"] = _heads(np.asarray(inputs["L0_ffn_wk"], f))
    w["fwvH"] = _heads(np.asarray(inputs["L0_ffn_wv"], f))
    w["fwo"] = _chunk(np.asarray(inputs["L0_ffn_wo"], f))
    w["gateT"] = np.ascontiguousarray(np.asarray(inputs["L0_ffn_gate"], f).T)
    w1 = np.asarray(inputs["L0_ffn_w1"], f)
    w3 = np.asarray(inputs["L0_ffn_w3"], f)
    w2 = np.asarray(inputs["L0_ffn_w2"], f)
    w["w1C"] = np.ascontiguousarray(
        w1.reshape(4, 8, 128, 12, 2, 128).transpose(0, 3, 2, 4, 1, 5).reshape(4, 12, 128, 2048))
    w["w3C"] = np.ascontiguousarray(
        w3.reshape(4, 8, 128, 12, 2, 128).transpose(0, 3, 2, 4, 1, 5).reshape(4, 12, 128, 2048))
    w["w2C"] = np.ascontiguousarray(
        w2.reshape(4, 12, 2, 128, 1024).transpose(0, 1, 3, 2, 4).reshape(4, 12, 128, 2048))
    w["l1wqH"] = _heads(T(inputs["L1_attn_w_q"]))
    w["l1wkvT"] = _chunk(T(inputs["L1_attn_w_kv"]))
    w["l1woT"] = _chunk(T(inputs["L1_attn_w_out"]))
    w["ln_g"] = np.asarray(inputs["L1_attn_ln_g"], f).reshape(1, D)
    w["ln_b"] = np.asarray(inputs["L1_attn_ln_b"], f).reshape(1, D)
    w["lno_g"] = np.asarray(inputs["L1_attn_lno_g"], f).reshape(1, D)
    w["lno_b"] = np.asarray(inputs["L1_attn_lno_b"], f).reshape(1, D)
    w["WrT"] = np.ascontiguousarray(np.asarray(inputs["L1_ffn_Wr"], f).T)
    Wu = np.concatenate([np.asarray(inputs["L1_ffn_Wu"], f),
                         np.asarray(inputs["L1_ffn_Wu_s"], f)[None]], 0)
    Wg = np.concatenate([np.asarray(inputs["L1_ffn_Wg"], f),
                         np.asarray(inputs["L1_ffn_Wg_s"], f)[None]], 0)
    Wd = np.concatenate([np.asarray(inputs["L1_ffn_Wd"], f),
                         np.asarray(inputs["L1_ffn_Wd_s"], f)[None]], 0)
    w["WuC"] = np.ascontiguousarray(
        Wu.reshape(8, 8, 128, 16, 2, 128).transpose(0, 3, 2, 4, 1, 5).reshape(8, 16, 128, 2048))
    w["WgC"] = np.ascontiguousarray(
        Wg.reshape(8, 8, 128, 16, 2, 128).transpose(0, 3, 2, 4, 1, 5).reshape(8, 16, 128, 2048))
    w["WdC"] = np.ascontiguousarray(
        Wd.reshape(8, 16, 2, 128, 1024).transpose(0, 1, 3, 2, 4).reshape(8, 16, 128, 2048))
    return w


def kernel(**inputs):
    x = np.asarray(inputs["L0_attn_x"], np.float32)
    w = prep_inputs(inputs)
    mod = Mod()
    nc = mod.build()
    in_maps = []
    for core in range(8):
        m = dict(w)
        m["x"] = np.ascontiguousarray(x[2 * core:2 * core + 2])
        in_maps.append(m)
    res = run_bass_kernel_spmd(nc, in_maps, core_ids=list(range(8)))
    return np.concatenate([r["y"] for r in res.results], axis=0)


# version 08
# speedup vs baseline: 1.2674x; 1.2674x over previous
import math
from contextlib import ExitStack

import numpy as np
import concourse.bass as bass
import concourse.mybir as mybir
from concourse.bass_utils import run_bass_kernel_spmd

F32 = mybir.dt.float32
BF16 = mybir.dt.bfloat16
I32 = mybir.dt.int32
AF = mybir.ActivationFunctionType
ALU = mybir.AluOpType
AX = mybir.AxisListType
PI = float(np.pi)

SEQ, D = 2048, 1024
TT, NCH = SEQ // 128, D // 128
NEG = -1e30
EPS5 = 1e-5
EPSR = 1.1920928955078125e-07
ENGINES = ("sync", "gpsimd", "vector", "scalar", "tensor")


class Bld:
    THROTTLE = 4

    def __init__(self, nc, ctx):
        self.nc = nc
        self.q = {e: [] for e in ENGINES}
        self.sems = {e: ctx.enter_context(nc.semaphore(f"ms_{e}")) for e in ENGINES}
        self.dsems = {e: ctx.enter_context(nc.semaphore(f"dma_{e}"))
                      for e in ("sync", "gpsimd")}
        self.cnt = {e: 0 for e in ENGINES}
        self.icnt = {e: 0 for e in ENGINES}
        self.dcnt = {"sync": 0, "gpsimd": 0}
        self.lastw = {e: 0 for e in ENGINES}
        self.dlastw = {"sync": 0, "gpsimd": 0}
        self.sink = {}

    def mark(self, eng, fn):
        if self.cnt[eng] - self.lastw[eng] >= self.THROTTLE:
            self.wait(eng, eng, self.cnt[eng] - self.THROTTLE + 1)
        self.cnt[eng] += 1
        self.icnt[eng] = self.icnt.get(eng, 0) + 1
        v = self.cnt[eng]
        sem = self.sems[eng]
        self.q[eng].append(lambda _e, fn=fn, sem=sem: fn().then_inc(sem, 1))
        return v

    def run(self, eng, fn):
        self.icnt[eng] = self.icnt.get(eng, 0) + 1
        self.q[eng].append(lambda _e, fn=fn: fn())

    def wait(self, eng, on_eng, v):
        if v <= 0:
            return
        if eng == on_eng:
            if v <= self.lastw[eng]:
                return
            self.lastw[eng] = v
        sem = self.sems[on_eng]
        self.q[eng].append(lambda e, sem=sem, v=v: e.wait_ge(sem, v))

    def dma(self, queue, out, in_, **kw):
        if self.dcnt[queue] - self.dlastw[queue] >= 16 * self.THROTTLE:
            self.dwait(queue, queue, self.dcnt[queue] - 16 * (self.THROTTLE - 1))
        self.dcnt[queue] += 16
        v = self.dcnt[queue]
        sem = self.dsems[queue]
        self.q[queue].append(
            lambda e, out=out, in_=in_, kw=kw, sem=sem:
            e.dma_start(out=out, in_=in_, **kw).then_inc(sem, 16))
        return v

    def dwait(self, eng, queue, v):
        if v <= 0:
            return
        if eng == queue:
            if v <= self.dlastw[queue]:
                return
            self.dlastw[queue] = v
        sem = self.dsems[queue]
        self.q[eng].append(lambda e, sem=sem, v=v: e.wait_ge(sem, v))

    def barrier(self):
        nc = self.nc
        for e in ENGINES:
            self.cnt[e] += 1
        arrive = dict(self.cnt)
        for e in ENGINES:
            if e in self.dcnt and self.dcnt[e] > 0:
                self.dwait(e, e, self.dcnt[e])
            self.wait(e, e, arrive[e] - 1)
            sem = self.sems[e]
            snk = self.sink.get(e)
            if e == "vector":
                self.q[e].append(lambda _e, sem=sem, snk=snk:
                                 nc.vector.tensor_copy(snk[:], snk[:]).then_inc(sem, 1))
            elif e == "scalar":
                self.q[e].append(lambda _e, sem=sem, snk=snk:
                                 nc.scalar.copy(snk[:], snk[:]).then_inc(sem, 1))
            elif e == "gpsimd":
                self.q[e].append(lambda _e, sem=sem, snk=snk:
                                 nc.gpsimd.memset(snk[:], 0.0).then_inc(sem, 1))
            else:
                self.q[e].append(lambda e_, sem=sem: e_.nop().then_inc(sem, 1))
        for e in ENGINES:
            for o in ENGINES:
                if o != e:
                    self.wait(e, o, arrive[o])

    def replay(self, block):
        bq = self.q

        @block.sync
        def _(eng):
            for fn in bq["sync"]:
                fn(eng)

        @block.gpsimd
        def _(eng):
            for fn in bq["gpsimd"]:
                fn(eng)

        @block.vector
        def _(eng):
            for fn in bq["vector"]:
                fn(eng)

        @block.scalar
        def _(eng):
            for fn in bq["scalar"]:
                fn(eng)

        @block.tensor
        def _(eng):
            for fn in bq["tensor"]:
                fn(eng)


class Trk:

    def __init__(self, bld):
        self.bld = bld
        self.w = None
        self.r = {}

    def rd(self, eng):
        if self.w:
            k, v, idx = self.w
            if isinstance(k, tuple):
                self.bld.dwait(eng, k[1], v)
            elif k != eng:
                self.bld.wait(eng, k, v)
            elif eng != "tensor":
                self.bld.wait(eng, eng, v)

    def did_rd(self, key, v):
        if v > self.r.get(key, 0):
            self.r[key] = v

    def wr(self, eng):
        self.rd(eng)
        for k, v in self.r.items():
            if isinstance(k, tuple):
                self.bld.dwait(eng, k[1], v)
            elif k != eng:
                self.bld.wait(eng, k, v)

    def did_wr(self, key, v):
        idx = self.bld.icnt[key] if isinstance(key, str) else 0
        self.w = (key, v, idx)
        self.r = {}

    def dma_wr_pre(self, queue):
        if self.w:
            k, v = self.w[0], self.w[1]
            if isinstance(k, tuple):
                if k[1] != queue:
                    self.bld.dwait(queue, k[1], v)
            else:
                self.bld.wait(queue, k, v)
        for k, v in self.r.items():
            if isinstance(k, tuple):
                if k[1] != queue:
                    self.bld.dwait(queue, k[1], v)
            else:
                self.bld.wait(queue, k, v)


def cdiv(a, b):
    return (a + b - 1) // b


class Mod:
    def __init__(self, layers=(0, 1), dumps=(), parts=None, nheads=16,
                 outproj=True):
        self.layers = tuple(layers)
        self.dumps = dict(dumps)
        self.dump_aps = {}
        self.nheads = nheads
        self.outproj = outproj
        self.attn_stage = "full"
        self.parts = set(parts) if parts is not None else {
            "attn0", "inner", "moe0", "attn1", "ffn1"}

    def build(self):
        nc = bass.Bass("TRN2", target_bir_lowering=False)
        self.nc = nc
        ctx = ExitStack()
        self.declare_io(nc)
        with ctx:
            self.alloc_sbuf(nc, ctx)
            bld = Bld(nc, ctx)
            self.bld = bld
            bld.sink["vector"] = self.snkv
            bld.sink["scalar"] = self.snks
            bld.sink["gpsimd"] = self.snkg
            self.ptrk = [Trk(bld) for _ in range(8)]
            self.prr = 0
            names = ("actT sS sE harena kh vh qh sblk pblk ptblk oh opr "
                     "stats router tq tk tv lA lB t0 t1 ntbf x0 x1b small "
                     "bcast onesb iota masks ident onesf").split()
            self.trk = {n: Trk(bld) for n in names}
            self.wrtrk = [Trk(bld) for _ in range(4)]
            self._wri = 0
            self.emit_init()
            for seq in range(2):
                for layer in self.layers:
                    self.emit_layer(seq, layer)
            self.emit_final()
            block = ctx.enter_context(nc.Block())
            bld.replay(block)
        return nc

    def declare_io(self, nc):
        d = {}

        def inp(name, shape):
            d[name] = nc.dram_tensor(name, list(shape), F32, kind="ExternalInput").ap()

        inp("x", (2, SEQ, D))
        inp("WqH", (16, 128, 512)); inp("WkH", (16, 128, 512))
        inp("WvH", (16, 128, 512))
        inp("WpT", (128, NCH * D))
        for w in ("BqT", "BkT", "BvT"):
            inp(w, (128, NCH * 8))
        for w in ("AqT", "AkT", "AvT"):
            inp(w, (8, D))
        inp("g1", (1, D)); inp("g2", (1, D))
        inp("fwqH", (16, 128, 512))
        inp("fwkH", (4, 128, 512)); inp("fwvH", (4, 128, 512))
        inp("fwo", (128, NCH * D))
        inp("gateT", (4, D))
        inp("w1C", (4, 12, 128, 2048)); inp("w3C", (4, 12, 128, 2048))
        inp("w2C", (4, 12, 128, 2048))
        inp("l1wqH", (16, 128, 512)); inp("l1wkvT", (128, NCH * 128))
        inp("l1woT", (128, NCH * D))
        inp("ln_g", (1, D)); inp("ln_b", (1, D))
        inp("lno_g", (1, D)); inp("lno_b", (1, D))
        inp("WrT", (8, D))
        inp("WuC", (8, 16, 128, 2048)); inp("WgC", (8, 16, 128, 2048))
        inp("WdC", (8, 16, 128, 2048))
        self.w = d
        self.y = nc.dram_tensor("y", [2, SEQ, D], F32, kind="ExternalOutput").ap()
        self.x1 = nc.dram_tensor("x1scr", [2, SEQ, D], F32, kind="Internal").ap()
        okind = "ExternalOutput" if "oscr" in self.dumps else "Internal"
        self.oscr = nc.dram_tensor("oscr", [8, 128, SEQ], BF16, kind=okind).ap()
        self.mask_d = nc.dram_tensor("mask_d", [128, 4096], BF16, kind="Internal").ap()
        self.ones_d = nc.dram_tensor("ones_d", [1, 2048], BF16, kind="Internal").ap()
        self.iota_d = nc.dram_tensor("iota_d", [128, 2048], F32, kind="Internal").ap()
        self.cs_d = nc.dram_tensor("cs_d", [32, 2048], BF16, kind="Internal").ap()
        self.sn_d = nc.dram_tensor("sn_d", [32, 2048], BF16, kind="Internal").ap()
        for name, shape in self.dumps.items():
            if shape is None:
                continue
            self.dump_aps[name] = nc.dram_tensor(
                "dump_" + name, list(shape), F32, kind="ExternalOutput").ap()

    def alloc_sbuf(self, nc, ctx):
        A = lambda name, shape, dt=BF16: ctx.enter_context(nc.sbuf_tensor(name, shape, dt))
        self.actT = A("actT", [128, NCH * SEQ])
        self.sS = A("sS", [128, TT * D])
        self.sE = A("sE", [128, TT * D])
        self.harena = A("harena", [128, 32768])
        self.wrot = [A(f"wrot{i}", [128, 2048]) for i in range(4)]
        self.bcast = A("bcast", [128, 6 * D])
        self.stats = A("stats", [128, 64], F32)
        self.router = A("router", [128, 256], F32)
        self.ident = A("ident", [128, 128])
        self.identf = A("identf", [128, 128], F32)
        self.t0 = A("t0", [128, 2 * D], F32)
        self.t1 = A("t1", [128, D], F32)
        self.ntbf = A("ntbf", [128, D])
        self.small = A("small", [128, 64], F32)
        self.consts = A("consts", [128, 4], F32)
        self.onesf = A("onesf", [1, 128], F32)
        self.snkv = A("snkv", [128, 8], F32)
        self.snks = A("snks", [128, 8], F32)
        self.snkg = A("snkg", [128, 8], F32)
        self.psum = [ctx.enter_context(nc.psum_tensor(f"pb{i}", [128, 512], F32))
                     for i in range(8)]
        hbf = self.harena
        hf32 = self.harena[:].bitcast(F32)
        self.sblk = hf32[:, 0:2048]
        self.x0 = hf32[:, 0:1024]
        self.x1b = hf32[:, 1024:2048]
        self.pblk = hbf[:, 4096:6144]
        self.ptblk = hbf[:, 6144:8192]
        self.oh = hbf[:, 8192:10240]
        self.opr = hbf[:, 10240:11264]
        self.KH0, self.QH0 = 11264, 14336
        self.kh = hbf[:, 11264:13312]
        self.vh = hbf[:, 13312:14336]
        self.qh = hbf[:, 14336:16384]
        self.tq = hbf[0:8, 16384:18432]
        self.tk = hbf[0:8, 18432:20480]
        self.tv = hbf[0:8, 20480:22528]
        self.cs_t = hbf[0:64, 16384:18432]
        self.sn_t = hbf[0:64, 18432:20480]
        self.lA = hbf[0:8, 22528:25600]
        self.lB = hbf[:, 25600:25792]
        self.iota = hf32[:, 12896:14944]
        self.masks = hbf[:, 29888:31936]
        self.masksB = hbf[:, 4096:6144]
        self.onesb = hbf[0:1, 0:512]
        self.wA = self.actT[:, 0:NCH * D]

    def _t(self, t):
        return self.trk[t] if isinstance(t, str) else t

    def pacq(self, n=1, avoid=None):
        ids = []
        for _ in range(n):
            i = self.prr
            self.prr = (self.prr + 1) % 8
            if avoid is not None and i == avoid:
                i = self.prr
                self.prr = (self.prr + 1) % 8
            self.ptrk[i].wr("tensor")
            ids.append(i)
        return ids if n > 1 else ids[0]

    def pap(self, i):
        return self.psum[i]

    def pbf(self, i):
        return self.psum[i][:].bitcast(BF16)

    def mmg(self, items, reads=()):
        bld = self.bld
        for t in reads:
            self._t(t).rd("tensor")
        for (bank, out, lhsT, rhs, st, sp) in items[:-1]:
            bld.run("tensor", lambda o=out, l=lhsT, r=rhs, s=st, p=sp:
                    self.nc.tensor.matmul(o, l, r, start=s, stop=p))
        bank, out, lhsT, rhs, st, sp = items[-1]
        v = bld.mark("tensor", lambda o=out, l=lhsT, r=rhs, s=st, p=sp:
                     self.nc.tensor.matmul(o, l, r, start=s, stop=p))
        for b in {it[0] for it in items}:
            self.ptrk[b].did_wr("tensor", v)
        for t in reads:
            self._t(t).did_rd("tensor", v)
        return v

    def op(self, eng, fn, rpsum=(), reads=(), writes=()):
        bld = self.bld
        for b in rpsum:
            self.ptrk[b].rd(eng)
        for t in reads:
            self._t(t).rd(eng)
        for t in writes:
            self._t(t).wr(eng)
        v = bld.mark(eng, fn)
        for b in rpsum:
            self.ptrk[b].did_rd(eng, v)
        for t in reads:
            self._t(t).did_rd(eng, v)
        for t in writes:
            self._t(t).did_wr(eng, v)
        return v

    def dma_into(self, queue, trkname, dst_ap, src_ap):
        t = self._t(trkname)
        t.dma_wr_pre(queue)
        v = self.bld.dma(queue, dst_ap, src_ap)
        t.did_wr(("dma", queue), v)
        return v

    def dma_from(self, queue, trkname, dst_ap, src_ap):
        t = self._t(trkname)
        if t.w:
            k, v = t.w[0], t.w[1]
            if isinstance(k, tuple):
                self.bld.dwait(queue, k[1], v)
            else:
                self.bld.wait(queue, k, v)
        v = self.bld.dma(queue, dst_ap, src_ap)
        t.did_rd(("dma", queue), v)
        return v

    def wload(self, src_ap, ncols=4096):
        i = self._wri
        self._wri = (i + 1) % 4
        t = self.wrtrk[i]
        t.dma_wr_pre("gpsimd")
        v = self.bld.dma("gpsimd", self.wrot[i][:, 0:ncols], src_ap)
        t.did_wr(("dma", "gpsimd"), v)
        return i

    def wuse(self, i):
        self.wrtrk[i].rd("tensor")
        return self.wrot[i]

    def wused(self, i, v):
        self.wrtrk[i].did_rd("tensor", v)

    def maybe_dump(self, name, sbuf_ap, trkname):
        if name not in self.dumps:
            return
        self.bld.barrier()
        self.dma_from("gpsimd", trkname, self.dump_aps[name], sbuf_ap)
        self.bld.barrier()

    def emit_init(self):
        nc, bld = self.nc, self.bld
        bld.mark("vector", lambda: nc.vector.memset(self.snkv[:], 0.0))
        bld.mark("scalar", lambda: nc.scalar.memzero(self.snks[:]))
        bld.mark("gpsimd", lambda: nc.gpsimd.memset(self.snkg[:], 0.0))
        self.op("gpsimd", lambda: nc.gpsimd.memset(self.ident[:], 0.0),
                writes=("ident",))
        self.op("gpsimd", lambda: nc.gpsimd.affine_select(
            out=self.ident[:], in_=self.ident[:], compare_op=ALU.not_equal,
            fill=1.0, base=0, pattern=[[-1, 128]], channel_multiplier=1),
            reads=("ident",), writes=("ident",))
        self.op("gpsimd", lambda: nc.gpsimd.memset(self.identf[:], 0.0),
                writes=("onesf",))
        self.op("gpsimd", lambda: nc.gpsimd.affine_select(
            out=self.identf[:], in_=self.identf[:], compare_op=ALU.not_equal,
            fill=1.0, base=0, pattern=[[-1, 128]], channel_multiplier=1),
            reads=("onesf",), writes=("onesf",))
        self.op("gpsimd", lambda: nc.gpsimd.memset(self.harena[0:1, 0:2048], 1.0),
                writes=("kh",))
        self.dma_from("sync", "kh", self.ones_d, self.harena[0:1, 0:2048])
        for m in range(4):
            sl = self.masks[:, m * 512:(m + 1) * 512]
            self.op("gpsimd", lambda sl=sl: nc.gpsimd.memset(sl, 0.0),
                    writes=("masks",))
            self.op("gpsimd", lambda sl=sl, m=m: nc.gpsimd.affine_select(
                out=sl, in_=sl, compare_op=ALU.is_ge, fill=NEG,
                base=128 * m, pattern=[[-1, 512]], channel_multiplier=1),
                reads=("masks",), writes=("masks",))
            s2 = self.masksB[:, m * 512:(m + 1) * 512]
            self.op("gpsimd", lambda s2=s2: nc.gpsimd.memset(s2, 0.0),
                    writes=("pblk",))
            self.op("gpsimd", lambda s2=s2, m=m: nc.gpsimd.affine_select(
                out=s2, in_=s2, compare_op=ALU.is_ge, fill=NEG,
                base=-128 * m, pattern=[[1, 512]], channel_multiplier=-1),
                reads=("pblk",), writes=("pblk",))
        self.dma_from("sync", "masks", self.mask_d[:, 0:2048], self.masks)
        self.dma_from("sync", "pblk", self.mask_d[:, 2048:4096], self.masksB)
        self.op("gpsimd", lambda: nc.gpsimd.iota(
            self.iota, pattern=[[-1, 2048]], base=0, channel_multiplier=1,
            allow_small_or_imprecise_dtypes=True), writes=("iota",))
        self.dma_from("sync", "iota", self.iota_d, self.iota)
        vo = self.op("vector", lambda: nc.vector.memset(self.onesb[:], 1.0),
                     writes=("onesb",))
        bld.mark("vector", lambda: nc.vector.memset(self.consts[:, 0:1], EPS5))
        bld.mark("vector", lambda: nc.vector.memset(self.consts[:, 1:2], EPSR))
        bld.mark("vector", lambda: nc.vector.memset(self.consts[:, 2:3], 0.0))
        bld.mark("vector", lambda: nc.vector.memset(self.onesf[:], 1.0))
        import os as _os
        _vecs = [] if _os.environ.get("SKIP_BCAST") else ["g1", "g2", "ln_g", "ln_b", "lno_g", "lno_b"]
        for i, name in enumerate(_vecs):
            self.dma_into("sync", "t0", self.t0[0:1, 0:D], self.w[name])
            self.op("vector", lambda: nc.vector.tensor_copy(
                self.ntbf[0:1, :], self.t0[0:1, 0:D]), reads=("t0",), writes=("ntbf",))
            for half in range(2):
                b = self.pacq()
                self.mmg([(b, self.pap(b)[:, 0:512], self.onesb[0:1, 0:128],
                           self.ntbf[0:1, half * 512:(half + 1) * 512], True, True)],
                         reads=("ntbf", "onesb"))
                self.op("vector", lambda b=b, i=i, half=half: nc.vector.tensor_copy(
                    self.bcast[:, i * D + half * 512:i * D + (half + 1) * 512],
                    self.pap(b)[:, 0:512]), rpsum=(b,), writes=("bcast",))
        if not _os.environ.get("SKIP_ROPE"):
            self.emit_rope_tables()
            self.dma_from("sync", "tq", self.cs_d, self.cs_t[0:32, :])
            self.dma_from("sync", "tk", self.sn_d, self.sn_t[0:32, :])
        bld.barrier()

    def emit_rope_tables(self):
        nc, bld = self.nc, self.bld
        sm, t0, t1 = self.small, self.t0, self.t1
        self.op("gpsimd", lambda: nc.gpsimd.iota(
            sm[0:32, 0:1], pattern=[[0, 1]], base=0, channel_multiplier=1,
            allow_small_or_imprecise_dtypes=True), writes=("small",))
        self.op("scalar", lambda: nc.scalar.activation(
            sm[0:32, 1:2], sm[0:32, 0:1], AF.Exp,
            scale=-math.log(10000.0) / 32.0), reads=("small",), writes=("small",))
        self.op("gpsimd", lambda: nc.gpsimd.iota(
            t0[0:32, 0:2048], pattern=[[1, 2048]], base=0, channel_multiplier=0,
            allow_small_or_imprecise_dtypes=True), writes=("t0",))
        self.op("vector", lambda: nc.vector.tensor_scalar_mul(
            t0[0:32, 0:2048], t0[0:32, 0:2048], sm[0:32, 1:2]),
            reads=("small", "t0"), writes=("t0",))
        for half in range(4):
            F = t0[0:32, half * 512:(half + 1) * 512]
            self.op("vector", lambda F=F: nc.vector.tensor_scalar_mul(
                t1[0:32, 0:512], F, 1.0 / (2 * PI)), reads=("t0",), writes=("t1",))
            self.op("vector", lambda: nc.vector.tensor_copy(
                t1[0:32, 512:1024].bitcast(I32)[:, 0:512], t1[0:32, 0:512]),
                reads=("t1",), writes=("t1",))
            self.op("vector", lambda: nc.vector.tensor_copy(
                t1[0:32, 0:512], t1[0:32, 512:1024].bitcast(I32)[:, 0:512]),
                reads=("t1",), writes=("t1",))
            self.op("vector", lambda F=F: nc.vector.scalar_tensor_tensor(
                t1[0:32, 512:1024], t1[0:32, 0:512], -2 * PI, F,
                ALU.mult, ALU.add), reads=("t1", "t0"), writes=("t1",))
            self.op("vector", lambda: nc.vector.tensor_scalar(
                t1[0:32, 0:512], t1[0:32, 512:1024], PI, None, ALU.is_gt),
                reads=("t1",), writes=("t1",))
            self.op("vector", lambda F=F: nc.vector.scalar_tensor_tensor(
                F, t1[0:32, 0:512], -2 * PI, t1[0:32, 512:1024],
                ALU.mult, ALU.add), reads=("t1",), writes=("t0",))
            self.op("scalar", lambda F=F, half=half: nc.scalar.activation(
                self.sn_t[0:32, half * 512:(half + 1) * 512], F, AF.Sin),
                reads=("t0",), writes=("tk",))
            self.op("vector", lambda: nc.vector.tensor_scalar_add(
                t1[0:32, 0:512], t1[0:32, 512:1024], PI / 2),
                reads=("t1",), writes=("t1",))
            self.op("vector", lambda: nc.vector.tensor_scalar(
                t1[0:32, 512:1024], t1[0:32, 0:512], PI, None, ALU.is_gt),
                reads=("t1",), writes=("t1",))
            self.op("vector", lambda: nc.vector.scalar_tensor_tensor(
                t1[0:32, 0:512], t1[0:32, 512:1024], -2 * PI, t1[0:32, 0:512],
                ALU.mult, ALU.add), reads=("t1",), writes=("t1",))
            self.op("scalar", lambda half=half: nc.scalar.activation(
                self.cs_t[0:32, half * 512:(half + 1) * 512],
                t1[0:32, 0:512], AF.Sin), reads=("t1",), writes=("tq",))

    def tile_stats(self, u, utrk, eps, rms, store_i=None):
        nc, bld = self.nc, self.bld
        sm = self.small
        if not rms:
            self.op("vector", lambda: nc.vector.reduce_sum(
                sm[:, 0:1], u[:, 0:D], axis=AX.X, negate=True),
                reads=(utrk,), writes=("small",))
            vv = self.op("vector", lambda: nc.vector.tensor_scalar_mul(
                sm[:, 0:1], sm[:, 0:1], 1.0 / D), writes=("small",))
            bld.wait("scalar", "vector", vv)
        self.op("scalar", lambda rms=rms: nc.scalar.activation(
            self.t1[:, 0:D], u[:, 0:D], AF.Square,
            bias=self.consts[:, 2:3] if rms else sm[:, 0:1],
            accum_out=sm[:, 1:2]), reads=(utrk, "small"), writes=("t1", "small"))
        epscol = self.consts[:, 0:1] if eps == EPS5 else self.consts[:, 1:2]
        bld.run("scalar", lambda epscol=epscol: nc.scalar.activation(
            sm[:, 2:3], sm[:, 1:2], AF.Ln, bias=epscol, scale=1.0 / D))
        s2 = bld.mark("scalar", lambda: nc.scalar.activation(
            sm[:, 3:4], sm[:, 2:3], AF.Exp, scale=-0.5))
        self.trk["small"].did_wr("scalar", s2)
        if rms:
            z = self.op("vector", lambda: nc.vector.memset(sm[:, 4:5], 0.0),
                        writes=("small",))
            bld.wait("scalar", "vector", z)
        else:
            bv = self.op("vector", lambda: nc.vector.tensor_mul(
                sm[:, 4:5], sm[:, 0:1], sm[:, 3:4]), reads=("small",), writes=("small",))
            bld.wait("scalar", "vector", bv)
        if store_i is not None:
            self.op("vector", lambda i=store_i: nc.vector.tensor_copy(
                self.stats[:, 2 * i:2 * i + 2], self.small[:, 3:5]),
                reads=("small",), writes=("stats",))

    def transpose_tile(self, i):
        nc = self.nc
        for c in range(NCH):
            b = self.pacq()
            self.trk["ntbf"].rd("tensor")
            v = self.bld.mark("tensor", lambda b=b, c=c: nc.tensor.transpose(
                self.pbf(b)[:, 0:128], self.ntbf[:, c * 128:(c + 1) * 128],
                self.ident[:]))
            self.ptrk[b].did_wr("tensor", v)
            self.trk["ntbf"].did_rd("tensor", v)
            self.op("vector", lambda b=b, c=c, i=i: nc.vector.tensor_copy(
                self.actT[:, c * SEQ + i * 128:c * SEQ + (i + 1) * 128],
                self.pbf(b)[:, 0:128]), rpsum=(b,), writes=("actT",))

    def emit_actT(self, seq, mode, src, router=None):
        nc, bld = self.nc, self.bld
        xb_ = [self.x0, self.x1b]
        xn_ = ["x0", "x1b"]
        if router is not None:
            nE = 4 if router == "l0" else 8
            hf32 = self.harena[:].bitcast(F32)
            gg = hf32[:, 2048:2048 + nE * 1024]
            g2b = hf32[:, 2048 + nE * 1024:2048 + (nE + 1) * 1024]
            ggt = self.trk.setdefault("gg", Trk(bld))
            rows = self.w["gateT"] if router == "l0" else self.w["WrT"]
            if router == "l0":
                self.dma_into("sync", "t1", self.t1[0:1, :], self.w["g2"])
                for half in range(2):
                    b = self.pacq()
                    self.mmg([(b, self.pap(b)[:, :], self.onesf[:],
                               self.t1[0:1, half * 512:(half + 1) * 512],
                               True, True)], reads=("onesf", "t1"))
                    self.op("vector", lambda b=b, half=half: nc.vector.tensor_copy(
                        g2b[:, half * 512:(half + 1) * 512], self.pap(b)[:, :]),
                        rpsum=(b,), writes=(ggt,))
            for e in range(nE):
                self.dma_into("sync", "t1", self.t1[0:1, :], rows[e])
                for half in range(2):
                    b = self.pacq()
                    self.mmg([(b, self.pap(b)[:, :], self.onesf[:],
                               self.t1[0:1, half * 512:(half + 1) * 512],
                               True, True)], reads=("onesf", "t1"))
                    sl = gg[:, e * 1024 + half * 512:e * 1024 + (half + 1) * 512]
                    self.op("vector", lambda b=b, sl=sl: nc.vector.tensor_copy(
                        sl, self.pap(b)[:, :]), rpsum=(b,), writes=(ggt,))
                if router == "l0":
                    self.op("vector", lambda e=e: nc.vector.tensor_mul(
                        gg[:, e * 1024:(e + 1) * 1024],
                        gg[:, e * 1024:(e + 1) * 1024], g2b),
                        reads=(ggt,), writes=(ggt,))
        for i in range(TT):
            xb, xnm = xb_[i % 2], xn_[i % 2]
            self.dma_into("sync", xnm, xb[:], src[i * 128:(i + 1) * 128, :])
            if mode == "ln":
                self.tile_stats(xb, xnm, EPS5, rms=False, store_i=i)
                self.op("scalar", lambda xb=xb: nc.scalar.activation(
                    self.ntbf[:], xb[:], AF.Identity, scale=self.small[:, 3:4],
                    bias=self.small[:, 4:5]), reads=(xnm, "small"), writes=("ntbf",))
            elif mode == "ln_affine":
                self.tile_stats(xb, xnm, EPS5, rms=False, store_i=i)
                self.op("scalar", lambda xb=xb: nc.scalar.activation(
                    self.t0[:, 0:D], xb[:], AF.Identity, scale=self.small[:, 3:4],
                    bias=self.small[:, 4:5]), reads=(xnm, "small"), writes=("t0",))
                self.tile_stats(self.t0, "t0", EPS5, rms=False)
                self.op("scalar", lambda: nc.scalar.activation(
                    self.t0[:, D:2 * D], self.t0[:, 0:D], AF.Identity,
                    scale=self.small[:, 3:4], bias=self.small[:, 4:5]),
                    reads=("t0", "small"), writes=("t0",))
                self.op("vector", lambda: nc.vector.tensor_mul(
                    self.t0[:, D:2 * D], self.t0[:, D:2 * D],
                    self.bcast[:, 2 * D:3 * D]), reads=("bcast", "t0"), writes=("t0",))
                self.op("vector", lambda: nc.vector.tensor_add(
                    self.ntbf[:], self.t0[:, D:2 * D], self.bcast[:, 3 * D:4 * D]),
                    reads=("bcast", "t0"), writes=("ntbf",))
            elif mode == "recompute_ln":
                self.trk["stats"].rd("scalar")
                if router is None:
                    self.op("scalar", lambda xb=xb, i=i: nc.scalar.activation(
                        self.ntbf[:], xb[:], AF.Identity,
                        scale=self.stats[:, 2 * i:2 * i + 1],
                        bias=self.stats[:, 2 * i + 1:2 * i + 2]),
                        reads=(xnm, "stats"), writes=("ntbf",))
                else:
                    self.op("scalar", lambda xb=xb, i=i: nc.scalar.activation(
                        self.t0[:, 0:D], xb[:], AF.Identity,
                        scale=self.stats[:, 2 * i:2 * i + 1],
                        bias=self.stats[:, 2 * i + 1:2 * i + 2]),
                        reads=(xnm, "stats"), writes=("t0",))
                    self.op("vector", lambda: nc.vector.tensor_copy(
                        self.ntbf[:], self.t0[:, 0:D]), reads=("t0",),
                        writes=("ntbf",))
            elif mode in ("rms1", "rms2"):
                self.trk["stats"].rd("scalar")
                self.op("scalar", lambda xb=xb, i=i: nc.scalar.activation(
                    self.t0[:, 0:D], xb[:], AF.Identity,
                    scale=self.stats[:, 2 * i:2 * i + 1],
                    bias=self.stats[:, 2 * i + 1:2 * i + 2]),
                    reads=(xnm, "stats"), writes=("t0",))
                if mode == "rms2":
                    self.op("vector", lambda i=i: nc.vector.tensor_add(
                        self.t0[:, 0:D], self.t0[:, 0:D],
                        self.sE[:, i * D:(i + 1) * D]),
                        reads=("sE", "t0"), writes=("t0",))
                self.tile_stats(self.t0, "t0", EPSR, rms=True)
                gofs = 0 if mode == "rms1" else D
                self.op("scalar", lambda: nc.scalar.activation(
                    self.t0[:, D:2 * D], self.t0[:, 0:D], AF.Identity,
                    scale=self.small[:, 3:4]), reads=("t0", "small"), writes=("t0",))
                self.op("vector", lambda gofs=gofs: nc.vector.tensor_mul(
                    self.ntbf[:], self.t0[:, D:2 * D], self.bcast[:, gofs:gofs + D]),
                    reads=("bcast", "t0"), writes=("ntbf",))
            else:
                raise ValueError(mode)
            if router is not None:
                u = self.t0[:, D:2 * D] if mode == "rms2" else self.t0[:, 0:D]
                for e in range(nE):
                    self.op("vector", lambda u=u, e=e, i=i:
                            nc.vector.scalar_tensor_tensor(
                                self.t1[:, 0:D], u, 1.0,
                                gg[:, e * 1024:(e + 1) * 1024],
                                ALU.mult, ALU.mult,
                                accum_out=self.router[:, i * 16 + e:i * 16 + e + 1]),
                            reads=("t0", ggt), writes=("t1", "router"))
            self.transpose_tile(i)
        bld.barrier()

    def emit_attention(self, seq, kind):
        nc, bld = self.nc, self.bld
        H = 16
        self.dma_into("sync", "masks", self.masks, self.mask_d[:, 0:2048])
        self.dma_into("sync", "pblk", self.masksB, self.mask_d[:, 2048:4096])
        KB = 70 if kind == "l0" else 67
        hbf = self.harena
        if kind == "l0":
            for r in range(3):
                self.dma_into("gpsimd", "kh",
                              hbf[67 + r:68 + r, self.KH0:self.KH0 + 2048],
                              self.ones_d)
                self.dma_into("gpsimd", "qh",
                              hbf[64 + r:65 + r, self.QH0:self.QH0 + 2048],
                              self.ones_d)
        else:
            for r in range(3):
                self.dma_into("gpsimd", "kh",
                              hbf[64 + r:65 + r, self.KH0:self.KH0 + 2048],
                              self.ones_d)
        if kind == "l0":
            qsrc, ksrc, vsrc = self.w["WqH"], self.w["WkH"], self.w["WvH"]
            qscale, kscale = 0.125, None
            self.dma_into("sync", "iota", self.iota, self.iota_d)
        elif kind == "l0i":
            qsrc, ksrc, vsrc = self.w["fwqH"], self.w["fwkH"], self.w["fwvH"]
            qscale, kscale = 0.125, None
            self.dma_into("sync", "tq", self.cs_t[0:32, :], self.cs_d)
            self.dma_into("sync", "tk", self.sn_t[0:32, :], self.sn_d)
        else:
            qsrc = self.w["l1wqH"]
            sqs = math.sqrt(64.0 ** -0.5)
            qscale, kscale = (64.0 ** -0.5) * sqs, sqs

        if self.attn_stage == "init":
            bld.barrier()
            return
        if kind == "l0":
            self.dma_into("gpsimd", "lA", self.lA[:, 0:D], self.w["AqT"])
            self.dma_into("gpsimd", "lA", self.lA[:, D:2 * D], self.w["AkT"])
            self.dma_into("gpsimd", "lA", self.lA[:, 2 * D:3 * D], self.w["AvT"])
            self.dma_into("gpsimd", "lB", self.lB[:, 0:64], self.w["BqT"])
            self.dma_into("gpsimd", "lB", self.lB[:, 64:128], self.w["BkT"])
            self.dma_into("gpsimd", "lB", self.lB[:, 128:192], self.w["BvT"])
            if self.attn_stage == "lora_dma":
                bld.barrier()
                return
            if self.attn_stage.startswith("lora_mm"):
                import os as _os
                var = self.attn_stage
                b = self.pacq()
                if var == "lora_mm_bare":
                    self.mmg([(b, self.pap(b)[0:8, :], self.lB[:, 0:8],
                               self.actT[:, 0:512], True, True)],
                             reads=("lB", "actT"))
                    bld.barrier()
                    return
                if var == "lora_mm_nt2":
                    self.mmg([(b, self.pap(b)[0:8, :], self.ntbf[:, 0:8],
                               self.actT[:, 0:512], True, True)],
                             reads=("ntbf", "actT"))
                    bld.barrier()
                    return
                if var == "lora_mm_rhs":
                    self.mmg([(b, self.pap(b)[0:8, :], self.lB[:, 0:8],
                               self.ntbf[:, 0:512], True, True)],
                             reads=("lB", "ntbf"))
                    bld.barrier()
                    return
                nmm = 1 if var == "lora_mm1" else NCH
                items = [(b, self.pap(b)[0:8, :],
                          self.lB[:, c * 8:(c + 1) * 8],
                          self.actT[:, c * SEQ + 0:c * SEQ + 512],
                          c == 0, c == nmm - 1) for c in range(nmm)]
                self.mmg(items, reads=("lB", "actT"))
                if var == "lora_mm_v":
                    self.op("vector", lambda b=b: nc.vector.tensor_copy(
                        self.tq[0:8, 0:512], self.pap(b)[0:8, :]),
                        rpsum=(b,), writes=("tq",))
                elif var == "lora_mm_nt":
                    self.op("scalar", lambda b=b: nc.scalar.activation(
                        self.ntbf[0:8, 0:512], self.pap(b)[0:8, :],
                        AF.Copy, scale=2.0), rpsum=(b,), writes=("ntbf",))
                else:
                    self.op("scalar", lambda b=b: nc.scalar.activation(
                        self.tq[0:8, 0:512], self.pap(b)[0:8, :],
                        AF.Copy, scale=2.0), rpsum=(b,), writes=("tq",))
                bld.barrier()
                return
            for ti, (tbuf, tn, bofs) in enumerate(
                    [(self.tq, "tq", 0), (self.tk, "tk", 64), (self.tv, "tv", 128)]):
                for tb in range(4):
                    b = self.pacq()
                    items = [(b, self.pap(b)[0:8, :],
                              self.lB[:, bofs + c * 8:bofs + (c + 1) * 8],
                              self.actT[:, c * SEQ + tb * 512:c * SEQ + (tb + 1) * 512],
                              c == 0, c == NCH - 1) for c in range(NCH)]
                    self.mmg(items, reads=("lB", "actT"))
                    self.op("scalar", lambda b=b, tbuf=tbuf, tb=tb: nc.scalar.activation(
                        tbuf[0:8, tb * 512:(tb + 1) * 512], self.pap(b)[0:8, :],
                        AF.Copy, scale=2.0), rpsum=(b,), writes=(tn,))

        if self.attn_stage == "lora":
            bld.barrier()
            return
        slopes = [2.0 ** (-0.5 * (h + 1)) for h in range(H)]

        for h in range(self.nheads):
            newkv = (kind == "l0") or (kind == "l0i" and h % 4 == 0) or \
                    (kind == "l1" and h == 0)
            if newkv:
                kv = {"l0": h, "l0i": h // 4, "l1": 0}[kind]
                if kind == "l1":
                    wik = self.wload(self.w["l1wkvT"], ncols=NCH * 128)
                    wiv = wik
                    ksl = lambda wt, c: wt[:, c * 128:c * 128 + 64]
                    vsl = lambda wt, c: wt[:, c * 128 + 64:c * 128 + 128]
                else:
                    wik = self.wload(ksrc[kv], ncols=512)
                    wiv = self.wload(vsrc[kv], ncols=512)
                    ksl = lambda wt, c: wt[:, c * 64:(c + 1) * 64]
                    vsl = ksl
                wkt = self.wuse(wik)
                for tb in range(4):
                    b = self.pacq()
                    items = [(b, self.pap(b)[0:64, :], ksl(wkt, c),
                              self.actT[:, c * SEQ + tb * 512:c * SEQ + (tb + 1) * 512],
                              c == 0, False) for c in range(NCH)]
                    if kind == "l0":
                        items.append((b, self.pap(b)[0:64, :],
                                      self.lA[:, D + h * 64:D + (h + 1) * 64],
                                      self.tk[:, tb * 512:(tb + 1) * 512],
                                      False, True))
                    else:
                        items[-1] = items[-1][:5] + (True,)
                    v = self.mmg(items, reads=("actT", "lA", "tk"))
                    self.wused(wik, v)
                    if kscale is None:
                        self.op("vector", lambda b=b, tb=tb: nc.vector.tensor_copy(
                            self.kh[0:64, tb * 512:(tb + 1) * 512],
                            self.pap(b)[0:64, :]), rpsum=(b,), writes=("kh",))
                    else:
                        self.op("scalar", lambda b=b, tb=tb: nc.scalar.activation(
                            self.kh[0:64, tb * 512:(tb + 1) * 512],
                            self.pap(b)[0:64, :], AF.Copy, scale=kscale),
                            rpsum=(b,), writes=("kh",))
                wvt = self.wuse(wiv)
                for tt in range(TT):
                    b = self.pacq()
                    items = [(b, self.pap(b)[:, 0:64],
                              self.actT[:, c * SEQ + tt * 128:c * SEQ + (tt + 1) * 128],
                              vsl(wvt, c), c == 0, False) for c in range(NCH)]
                    if kind == "l0":
                        items.append((b, self.pap(b)[:, 0:64],
                                      self.tv[:, tt * 128:(tt + 1) * 128],
                                      self.lA[:, 2 * D + h * 64:2 * D + (h + 1) * 64],
                                      False, True))
                    else:
                        items[-1] = items[-1][:5] + (True,)
                    v = self.mmg(items, reads=("actT", "lA", "tv"))
                    self.wused(wiv, v)
                    self.op("vector", lambda b=b, tt=tt: nc.vector.tensor_copy(
                        self.vh[:, tt * 64:(tt + 1) * 64], self.pap(b)[:, 0:64]),
                        rpsum=(b,), writes=("vh",))
                if kind == "l0i":
                    self.rope("kh")
            wiq = self.wload(qsrc[h], ncols=512)
            wqt = self.wuse(wiq)
            for tb in range(4):
                b = self.pacq()
                items = [(b, self.pap(b)[0:64, :],
                          wqt[:, c * 64:(c + 1) * 64],
                          self.actT[:, c * SEQ + tb * 512:c * SEQ + (tb + 1) * 512],
                          c == 0, False) for c in range(NCH)]
                if kind == "l0":
                    items.append((b, self.pap(b)[0:64, :],
                                  self.lA[:, h * 64:(h + 1) * 64],
                                  self.tq[:, tb * 512:(tb + 1) * 512], False, True))
                else:
                    items[-1] = items[-1][:5] + (True,)
                v = self.mmg(items, reads=("actT", "lA", "tq"))
                self.wused(wiq, v)
                self.op("scalar", lambda b=b, tb=tb: nc.scalar.activation(
                    self.qh[0:64, tb * 512:(tb + 1) * 512], self.pap(b)[0:64, :],
                    AF.Copy, scale=qscale), rpsum=(b,), writes=("qh",))
            if kind == "l0i":
                self.rope("qh")
            if self.attn_stage == "proj":
                continue

            hbf = self.harena
            sl0 = self.harena[:].bitcast(F32)[0:1, 0:2048]
            if kind == "l0":
                slope = slopes[h]
                self.op("scalar", lambda slope=slope: nc.scalar.activation(
                    self.t0[0:1, 0:2048], self.iota[0:1, :], AF.Identity,
                    scale=slope), reads=("iota",), writes=("t0",))
                self.op("vector", lambda: nc.vector.tensor_copy(
                    self.ptblk[0:1, :], self.t0[0:1, 0:2048]),
                    reads=("t0",), writes=("ptblk",))
                self.dma_from("gpsimd", "ptblk",
                              self.harena[64:65, self.KH0:self.KH0 + 2048].bitcast(F32).bitcast(BF16)
                              if False else hbf[64:65, self.KH0:self.KH0 + 2048],
                              self.ptblk[0:1, :])
                self.op("vector", lambda: nc.vector.tensor_sub(
                    sl0, self.t0[0:1, 0:2048], self.ptblk[0:1, :]),
                    reads=("t0", "ptblk"), writes=("sblk",))
                self.op("vector", lambda: nc.vector.tensor_copy(
                    self.ptblk[0:1, :], sl0), reads=("sblk",), writes=("ptblk",))
                self.dma_from("gpsimd", "ptblk",
                              hbf[65:66, self.KH0:self.KH0 + 2048], self.ptblk[0:1, :])
                self.op("vector", lambda: nc.vector.tensor_sub(
                    self.t0[0:1, 0:2048], sl0, self.ptblk[0:1, :]),
                    reads=("sblk", "ptblk"), writes=("t0",))
                self.op("vector", lambda: nc.vector.tensor_copy(
                    self.ptblk[0:1, :], self.t0[0:1, 0:2048]),
                    reads=("t0",), writes=("ptblk",))
                self.dma_from("gpsimd", "ptblk",
                              hbf[66:67, self.KH0:self.KH0 + 2048], self.ptblk[0:1, :])
                self.trk["kh"].did_wr(("dma", "gpsimd"), self.bld.dcnt["gpsimd"])

            KA = 67 if kind == "l0" else 64
            sm = self.small
            for qt in range(TT):
                nk512 = qt // 4 + 1
                banks = [self.pacq() for _ in range(nk512)]
                items = [(banks[ks], self.pap(banks[ks])[:, :],
                          self.qh[0:KA, qt * 128:(qt + 1) * 128],
                          self.kh[0:KA, ks * 512:(ks + 1) * 512], True, True)
                         for ks in range(nk512)]
                self.mmg(items, reads=("qh", "kh"))
                ksp, mm_ = qt // 4, qt % 4
                b = banks[ksp]
                self.op("vector", lambda b=b, mm_=mm_: nc.vector.tensor_add(
                    self.pap(b)[:, :], self.pap(b)[:, :],
                    self.masks[:, mm_ * 512:(mm_ + 1) * 512]),
                    rpsum=(b,), reads=("masks",), writes=())
                self.ptrk[b].did_wr("vector", self.bld.cnt["vector"])
                for ks in range(nk512):
                    self.op("vector", lambda b=banks[ks], ks=ks:
                            nc.vector.reduce_max(sm[:, 24 + ks:25 + ks],
                                                 self.pap(b)[:, :], axis=AX.X,
                                                 negate=True),
                            rpsum=(banks[ks],), writes=("small",))
                if nk512 > 1:
                    self.op("vector", lambda n=nk512: nc.vector.tensor_reduce(
                        sm[:, 0:1], sm[:, 24:24 + n], axis=AX.X, op=ALU.min),
                        reads=("small",), writes=("small",))
                else:
                    self.op("vector", lambda: nc.vector.tensor_copy(
                        sm[:, 0:1], sm[:, 24:25]), reads=("small",),
                        writes=("small",))
                for ks in range(nk512):
                    b = banks[ks]
                    ev = self.op("scalar", lambda b=b, ks=ks: nc.scalar.activation(
                        self.pap(b)[:, :], self.pap(b)[:, :], AF.Exp,
                        bias=sm[:, 0:1], accum_out=sm[:, 28 + ks:29 + ks]),
                        rpsum=(b,), reads=("small",), writes=("small",))
                    self.ptrk[b].did_wr("scalar", ev)
                if nk512 > 1:
                    self.op("vector", lambda n=nk512: nc.vector.tensor_reduce(
                        sm[:, 1:2], sm[:, 28:28 + n], axis=AX.X, op=ALU.add),
                        reads=("small",), writes=("small",))
                else:
                    self.op("vector", lambda: nc.vector.tensor_copy(
                        sm[:, 1:2], sm[:, 28:29]), reads=("small",),
                        writes=("small",))
                lv = self.op("scalar", lambda: nc.scalar.activation(
                    sm[:, 2:3], sm[:, 1:2], AF.Ln), reads=("small",),
                    writes=("small",))
                self.op("vector", lambda qt=qt: nc.vector.tensor_sub(
                    sm[:, 8 + qt:9 + qt], sm[:, 0:1], sm[:, 2:3]),
                    reads=("small",), writes=("small",))
            if self.attn_stage == "passA":
                continue

            bt = self.pacq()
            self.trk["small"].rd("tensor")
            vtr = bld.mark("tensor", lambda bt=bt: nc.tensor.transpose(
                self.pap(bt)[0:16, 0:128], sm[:, 8:24], self.identf[:]))
            self.ptrk[bt].did_wr("tensor", vtr)
            self.trk["small"].did_rd("tensor", vtr)
            t1b = self.t1[:].bitcast(BF16)
            self.op("vector", lambda bt=bt: nc.vector.tensor_copy(
                self.t1[0:16, 0:128], self.pap(bt)[0:16, 0:128]),
                rpsum=(bt,), writes=("t1",))
            self.op("vector", lambda: nc.vector.tensor_copy(
                t1b[0:16, 1024:1152], self.t1[0:16, 0:128]),
                reads=("t1",), writes=("t1",))
            self.op("vector", lambda: nc.vector.tensor_sub(
                self.t1[0:16, 128:256], self.t1[0:16, 0:128], t1b[0:16, 1024:1152]),
                reads=("t1",), writes=("t1",))
            self.op("vector", lambda: nc.vector.tensor_copy(
                t1b[0:16, 1152:1280], self.t1[0:16, 128:256]),
                reads=("t1",), writes=("t1",))
            self.op("vector", lambda: nc.vector.tensor_sub(
                self.t1[0:16, 256:384], self.t1[0:16, 128:256], t1b[0:16, 1152:1280]),
                reads=("t1",), writes=("t1",))
            self.op("vector", lambda: nc.vector.tensor_copy(
                t1b[0:16, 1280:1408], self.t1[0:16, 256:384]),
                reads=("t1",), writes=("t1",))
            qrow0 = 67 if kind == "l0" else 64
            for r, c0 in ((0, 1024), (1, 1152), (2, 1280)):
                self.dma_from("gpsimd", "t1",
                              hbf[qrow0 + r:qrow0 + r + 1, self.QH0:self.QH0 + 2048],
                              t1b[0:16, c0:c0 + 128])
            self.trk["qh"].did_wr(("dma", "gpsimd"), self.bld.dcnt["gpsimd"])

            for qb in range(4):
                kmax = 4 * (qb + 1)
                bo = self.pacq()
                prev = None
                for kc in range(kmax):
                    b = self.pacq(avoid=bo)
                    self.mmg([(b, self.pap(b)[:, :],
                               self.kh[0:KB, kc * 128:(kc + 1) * 128],
                               self.qh[0:KB, qb * 512:(qb + 1) * 512],
                               True, True)], reads=("qh", "kh"))
                    i = kc - 4 * qb
                    if i >= 0:
                        self.op("vector", lambda b=b, i=i: nc.vector.tensor_add(
                            self.pap(b)[:, :], self.pap(b)[:, :],
                            self.masksB[:, i * 512:(i + 1) * 512]),
                            rpsum=(b,), reads=("pblk",), writes=())
                        self.ptrk[b].did_wr("vector", self.bld.cnt["vector"])
                    pts = self.ptblk[:, (kc % 4) * 512:(kc % 4 + 1) * 512]
                    ev = self.op("scalar", lambda b=b, pts=pts: nc.scalar.activation(
                        pts, self.pap(b)[:, :], AF.Exp), rpsum=(b,),
                        writes=("ptblk",))
                    if prev is not None:
                        self.mmg([prev], reads=("vh", "ptblk"))
                    prev = (bo, self.pap(bo)[0:64, :],
                            self.vh[:, kc * 64:(kc + 1) * 64],
                            pts, kc == 0, kc == kmax - 1)
                self.mmg([prev], reads=("vh", "ptblk"))
                self.op("vector", lambda b=bo, h=h, qb=qb: nc.vector.tensor_copy(
                    self.oh[(h % 2) * 64:(h % 2) * 64 + 64,
                            qb * 512:(qb + 1) * 512],
                    self.pap(b)[0:64, :]), rpsum=(bo,), writes=("oh",))
            if h % 2 == 1 and self.attn_stage == "full":
                self.dma_from("sync", "oh", self.oscr[h // 2], self.oh[:])

        if not self.outproj:
            bld.barrier()
            return
        wname = {"l0": "WpT", "l0i": "fwo", "l1": "l1woT"}[kind]
        self.dma_into("gpsimd", "actT", self.wA, self.w[wname])
        groups = [[qt] for qt in range(TT)]
        for grp in groups:
            g0, gl = grp[0], len(grp)
            for pair in range(8):
                self.dma_into("sync", "opr",
                              self.opr[:, pair * 128:pair * 128 + gl * 128],
                              self.oscr[pair][:, g0 * 128:(g0 + gl) * 128])
            for qt in grp:
                for do in range(2):
                    b = self.pacq()
                    items = [(b, self.pap(b)[:, :],
                              self.opr[:, pair * 128 + (qt - g0) * 128:
                                        pair * 128 + (qt - g0 + 1) * 128],
                              self.wA[:, pair * D + do * 512:pair * D + do * 512 + 512],
                              pair == 0, pair == 7) for pair in range(8)]
                    self.mmg(items, reads=("opr", "actT"))
                    if kind == "l0":
                        self.op("scalar", lambda b=b, qt=qt, do=do: nc.scalar.activation(
                            self.sS[:, qt * D + do * 512:qt * D + (do + 1) * 512],
                            self.pap(b)[:, :], AF.Copy), rpsum=(b,), writes=("sS",))
                    elif kind == "l0i":
                        self.op("scalar", lambda b=b, qt=qt, do=do: nc.scalar.activation(
                            self.sE[:, qt * D + do * 512:qt * D + (do + 1) * 512],
                            self.pap(b)[:, :], AF.Copy), rpsum=(b,), writes=("sE",))
                    else:
                        self.op("vector", lambda b=b, do=do: nc.vector.tensor_copy(
                            self.t0[:, do * 512:(do + 1) * 512], self.pap(b)[:, :]),
                            rpsum=(b,), writes=("t0",))
                if kind == "l1":
                    self.tile_stats(self.t0, "t0", EPS5, rms=False)
                    self.op("scalar", lambda: nc.scalar.activation(
                        self.t1[:, 0:D], self.t0[:, 0:D], AF.Identity,
                        scale=self.small[:, 3:4], bias=self.small[:, 4:5]),
                        reads=("t0", "small"), writes=("t1",))
                    self.op("vector", lambda: nc.vector.tensor_mul(
                        self.t1[:, 0:D], self.t1[:, 0:D], self.bcast[:, 4 * D:5 * D]),
                        reads=("bcast", "t1"), writes=("t1",))
                    self.op("vector", lambda qt=qt: nc.vector.tensor_add(
                        self.sS[:, qt * D:(qt + 1) * D], self.t1[:, 0:D],
                        self.bcast[:, 5 * D:6 * D]),
                        reads=("bcast", "t1"), writes=("sS",))
        bld.barrier()

    def rope(self, bufname):
        nc = self.nc
        buf = getattr(self, bufname)
        hbf = self.harena
        x2c = self.ptblk[0:32, :]
        p2 = hbf[0:32, 0:2048]
        p3 = hbf[0:32, 2048:4096]
        p4 = self.t0[:].bitcast(BF16)[0:32, 0:2048]
        cs, sn = self.cs_t[0:32, :], self.sn_t[0:32, :]
        self.op("vector", lambda: nc.vector.tensor_copy(x2c, buf[32:64, :]),
                reads=(bufname,), writes=("ptblk",))
        self.op("vector", lambda: nc.vector.tensor_mul(p2, x2c, sn),
                reads=("ptblk", "tk"), writes=("sblk",))
        self.op("vector", lambda: nc.vector.tensor_mul(p3, x2c, cs),
                reads=("ptblk", "tq"), writes=("sblk",))
        self.op("vector", lambda: nc.vector.tensor_mul(p4, buf[0:32, :], sn),
                reads=(bufname, "tk"), writes=("t0",))
        self.op("vector", lambda: nc.vector.tensor_mul(x2c, buf[0:32, :], cs),
                reads=(bufname, "tq"), writes=("ptblk",))
        self.op("vector", lambda: nc.vector.tensor_sub(buf[0:32, :], x2c, p2),
                reads=("ptblk", "sblk"), writes=(bufname,))
        self.op("vector", lambda: nc.vector.tensor_add(buf[32:64, :], p3, p4),
                reads=("sblk", "t0"), writes=(bufname,))

    def emit_router(self, kind):
        nc, bld = self.nc, self.bld
        nE = 4 if kind == "l0" else 8
        for i in range(TT):
            base = i * 16
            pr = self.router[:, base:base + 8]
            cb = self.router[:, base + 8:base + 16]
            sm = self.small
            mv = self.op("vector", lambda pr=pr, nE=nE: nc.vector.reduce_max(
                sm[:, 0:1], pr[:, 0:nE], axis=AX.X, negate=True),
                reads=("router",), writes=("small",))
            bld.wait("scalar", "vector", mv)
            ev = self.op("scalar", lambda nE=nE, pr=pr: nc.scalar.activation(
                pr[:, 0:nE], pr[:, 0:nE], AF.Exp,
                bias=sm[:, 0:1], accum_out=sm[:, 1:2]),
                reads=("router", "small"), writes=("router", "small"))
            bld.wait("vector", "scalar", ev)
            self.op("vector", lambda: nc.vector.reciprocal(
                sm[:, 2:3], sm[:, 1:2]), reads=("small",), writes=("small",))
            self.op("vector", lambda pr=pr, nE=nE: nc.vector.tensor_scalar_mul(
                pr[:, 0:nE], pr[:, 0:nE], sm[:, 2:3]),
                reads=("small", "router"), writes=("router",))
            if kind == "l0":
                self.op("vector", lambda pr=pr: nc.vector.reduce_max(
                    sm[:, 0:1], pr[:, 0:4], axis=AX.X), reads=("router",),
                    writes=("small",))
                self.op("vector", lambda pr=pr: nc.vector.tensor_scalar(
                    sm[:, 8:12], pr[:, 0:4], sm[:, 0:1], None, ALU.is_ge),
                    reads=("router", "small"), writes=("small",))
                self.cumsum_mask(sm, 8, 4, 16)
                self.op("vector", lambda pr=pr: nc.vector.scalar_tensor_tensor(
                    sm[:, 20:24], sm[:, 16:20], -1e38, pr[:, 0:4], ALU.mult, ALU.add),
                    reads=("router",), writes=("small",))
                self.op("vector", lambda: nc.vector.reduce_max(
                    sm[:, 1:2], sm[:, 20:24], axis=AX.X), writes=("small",))
                self.op("vector", lambda: nc.vector.tensor_scalar(
                    sm[:, 8:12], sm[:, 20:24], sm[:, 1:2], None, ALU.is_ge),
                    writes=("small",))
                self.cumsum_mask(sm, 8, 4, 24)
                self.op("vector", lambda: nc.vector.tensor_add(
                    sm[:, 16:20], sm[:, 16:20], sm[:, 24:28]), writes=("small",))
                self.op("vector", lambda: nc.vector.tensor_add(
                    sm[:, 2:3], sm[:, 0:1], sm[:, 1:2]), writes=("small",))
                self.op("vector", lambda: nc.vector.reciprocal(
                    sm[:, 3:4], sm[:, 2:3]), writes=("small",))
                self.op("vector", lambda pr=pr: nc.vector.tensor_mul(
                    sm[:, 20:24], pr[:, 0:4], sm[:, 16:20]),
                    reads=("router",), writes=("small",))
                self.op("vector", lambda cb=cb: nc.vector.tensor_scalar_mul(
                    cb[:, 0:4], sm[:, 20:24], sm[:, 3:4]),
                    reads=("small",), writes=("router",))
            else:
                self.op("vector", lambda pr=pr: nc.vector.reduce_max(
                    sm[:, 0:1], pr[:, 0:8], axis=AX.X), reads=("router",),
                    writes=("small",))
                self.op("vector", lambda pr=pr: nc.vector.tensor_scalar(
                    sm[:, 8:16], pr[:, 0:8], sm[:, 0:1], None, ALU.is_ge),
                    reads=("router", "small"), writes=("small",))
                self.cumsum_mask(sm, 8, 8, 16)
                self.op("vector", lambda cb=cb: nc.vector.tensor_scalar(
                    cb[:, 0:8], sm[:, 16:24], sm[:, 0:1], 2.0, ALU.mult, ALU.mult),
                    reads=("small",), writes=("router",))
                self.op("vector", lambda cb=cb: nc.vector.memset(cb[:, 7:8], 1.0),
                        writes=("router",))

    def cumsum_mask(self, sm, src, n, dst):
        nc = self.nc
        self.op("vector", lambda: nc.vector.tensor_copy(
            sm[:, 32:32 + n], sm[:, src:src + n]), writes=("small",))
        cur = 32
        oth = 40
        sh = 1
        while sh < n:
            self.op("vector", lambda cur=cur, oth=oth, sh=sh: nc.vector.tensor_copy(
                sm[:, oth:oth + sh], sm[:, cur:cur + sh]), writes=("small",))
            self.op("vector", lambda cur=cur, oth=oth, sh=sh: nc.vector.tensor_add(
                sm[:, oth + sh:oth + n], sm[:, cur + sh:cur + n], sm[:, cur:cur + n - sh]),
                writes=("small",))
            cur, oth = oth, cur
            sh *= 2
        self.op("vector", lambda cur=cur: nc.vector.tensor_scalar(
            sm[:, 48:48 + n], sm[:, cur:cur + n], 1.0, None, ALU.is_equal),
            writes=("small",))
        self.op("vector", lambda dst=dst: nc.vector.tensor_mul(
            sm[:, dst:dst + n], sm[:, src:src + n], sm[:, 48:48 + n]),
            writes=("small",))

    def emit_moe(self, seq, kind):
        nc, bld = self.nc, self.bld
        self.emit_router(kind)
        if kind == "l0":
            nE, HC, NG = 4, 24, 12
            wup, wgt, wdn = self.w["w1C"], self.w["w3C"], self.w["w2C"]
        else:
            nE, HC, NG = 8, 32, 16
            wup, wgt, wdn = self.w["WgC"], self.w["WuC"], self.w["WdC"]
            z = self.op("gpsimd", lambda: nc.gpsimd.memset(self.sE[:], 0.0),
                        writes=("sE",))
        ones_col = self.small[:, 63:64]
        oz = self.op("vector", lambda: nc.vector.memset(ones_col, 1.0),
                     writes=("small",))
        for e in range(nE):
            for blk in range(2):
                for g in range(NG):
                    wi1 = self.wload(wup[e, g], ncols=2048)
                    wi3 = self.wload(wgt[e, g], ncols=2048)
                    w1t = self.wrot[wi1]
                    w3t = self.wrot[wi3]
                    for hcl in range(2):
                        hc = g * 2 + hcl
                        for half in range(2):
                            tok0 = blk * 1024 + half * 512
                            bA = self.pacq()
                            self.wrtrk[wi1].rd("tensor")
                            itemsA = [(bA, self.pap(bA)[:, :],
                                       w1t[:, hcl * 1024 + c * 128:hcl * 1024 + (c + 1) * 128],
                                       self.actT[:, c * SEQ + tok0:c * SEQ + tok0 + 512],
                                       c == 0, c == NCH - 1) for c in range(NCH)]
                            vA = self.mmg(itemsA, reads=("actT",))
                            self.wused(wi1, vA)
                            bB = self.pacq()
                            self.wrtrk[wi3].rd("tensor")
                            itemsB = [(bB, self.pap(bB)[:, :],
                                       w3t[:, hcl * 1024 + c * 128:hcl * 1024 + (c + 1) * 128],
                                       self.actT[:, c * SEQ + tok0:c * SEQ + tok0 + 512],
                                       c == 0, c == NCH - 1) for c in range(NCH)]
                            vB = self.mmg(itemsB, reads=("actT",))
                            self.wused(wi3, vB)
                            sv = self.op("scalar", lambda bA=bA: nc.scalar.activation(
                                self.t1[:, 0:512], self.pap(bA)[:, :], AF.Silu),
                                rpsum=(bA,), writes=("t1",))
                            bld.wait("vector", "scalar", sv)
                            self.op("vector", lambda bB=bB, hc=hc, half=half:
                                    nc.vector.tensor_mul(
                                        self.harena[:, hc * 1024 + half * 512:
                                                    hc * 1024 + (half + 1) * 512],
                                        self.t1[:, 0:512], self.pap(bB)[:, :]),
                                    rpsum=(bB,), reads=("t1",), writes=("harena",))
                for tokblk in range(2):
                    tok0 = blk * 1024 + tokblk * 512
                    tile0 = tok0 // 128
                    banks = self.pacq(8)
                    for g in range(NG):
                        wi2 = self.wload(wdn[e, g], ncols=2048)
                        w2t = self.wrot[wi2]
                        items = []
                        for hcl in range(2):
                            hc = g * 2 + hcl
                            for st in range(4):
                                for do in range(2):
                                    items.append((
                                        banks[st * 2 + do],
                                        self.pap(banks[st * 2 + do])[:, :],
                                        self.harena[:, hc * 1024 + tokblk * 512 +
                                                    st * 128:hc * 1024 + tokblk * 512 +
                                                    (st + 1) * 128],
                                        w2t[:, hcl * 1024 + do * 512:
                                            hcl * 1024 + (do + 1) * 512],
                                        hc == 0, hc == HC - 1))
                        self.wrtrk[wi2].rd("tensor")
                        v = self.mmg(items, reads=("harena",))
                        self.wused(wi2, v)
                    for st in range(4):
                        for do in range(2):
                            b = banks[st * 2 + do]
                            tile = tile0 + st
                            if kind == "l0":
                                comb = self.router[:, tile * 16 + 8 + e:tile * 16 + 9 + e]
                            else:
                                comb = (ones_col if e == 7 else
                                        self.router[:, tile * 16 + 8 + e:tile * 16 + 9 + e])
                            self.op("vector", lambda b=b, tile=tile, do=do, comb=comb:
                                    nc.vector.scalar_tensor_tensor(
                                        self.sE[:, tile * D + do * 512:
                                                tile * D + (do + 1) * 512],
                                        self.pap(b)[:, :], comb,
                                        self.sE[:, tile * D + do * 512:
                                                tile * D + (do + 1) * 512],
                                        ALU.mult, ALU.add),
                                    rpsum=(b,), reads=("router", "small"),
                                    writes=("sE",))
        bld.barrier()

    def emit_combine(self, seq, layer):
        nc, bld = self.nc, self.bld
        src = self.w["x"][seq] if layer == 0 else self.x1[seq]
        dst = self.x1[seq] if layer == 0 else self.y[seq]
        xb_ = [self.x0, self.x1b]
        xn_ = ["x0", "x1b"]
        for i in range(TT):
            xb, xnm = xb_[i % 2], xn_[i % 2]
            self.dma_into("sync", xnm, xb[:], src[i * 128:(i + 1) * 128, :])
            if layer == 0:
                self.trk["stats"].rd("scalar")
                self.op("scalar", lambda xb=xb, i=i: nc.scalar.activation(
                    self.t0[:, 0:D], xb[:], AF.Identity,
                    scale=self.stats[:, 2 * i:2 * i + 1],
                    bias=self.stats[:, 2 * i + 1:2 * i + 2]),
                    reads=(xnm, "stats"), writes=("t0",))
                self.op("vector", lambda i=i: nc.vector.tensor_add(
                    self.t0[:, 0:D], self.t0[:, 0:D], self.sE[:, i * D:(i + 1) * D]),
                    reads=("sE", "t0"), writes=("t0",))
                gv = self.op("scalar", lambda: nc.scalar.activation(
                    self.t1[:, 0:D], self.t0[:, 0:D], AF.Silu),
                    reads=("t0",), writes=("t1",))
            else:
                gv = self.op("scalar", lambda i=i: nc.scalar.activation(
                    self.t1[:, 0:D], self.sE[:, i * D:(i + 1) * D], AF.Silu),
                    reads=("sE",), writes=("t1",))
            self.op("vector", lambda i=i: nc.vector.tensor_mul(
                self.t1[:, 0:D], self.t1[:, 0:D], self.sS[:, i * D:(i + 1) * D]),
                reads=("sS", "t1"), writes=("t1",))
            ov = self.op("vector", lambda xb=xb: nc.vector.tensor_add(
                self.t0[:, 0:D], xb[:], self.t1[:, 0:D]),
                reads=(xnm, "t1"), writes=("t0",))
            self.dma_from("sync", "t0", dst[i * 128:(i + 1) * 128, :], self.t0[:, 0:D])
        bld.barrier()

    def emit_layer(self, seq, layer):
        x = self.w["x"]
        if layer == 0:
            self.emit_actT(seq, "ln", x[seq])
            self.maybe_dump(f"actT_{seq}_0", self.actT[:], "actT")
            if "attn0" in self.parts:
                self.emit_attention(seq, "l0")
                self.maybe_dump(f"sS_{seq}_0", self.sS[:], "sS")
            self.emit_actT(seq, "rms1", x[seq])
            if "inner" in self.parts:
                self.emit_attention(seq, "l0i")
                self.maybe_dump(f"sE_{seq}_0", self.sE[:], "sE")
            self.emit_actT(seq, "rms2", x[seq], router="l0" if "moe0" in self.parts else None)
            if "moe0" in self.parts:
                self.emit_moe(seq, "l0")
                self.maybe_dump(f"moe_{seq}_0", self.sE[:], "sE")
            self.emit_combine(seq, 0)
        else:
            self.emit_actT(seq, "ln_affine", self.x1[seq])
            if "attn1" in self.parts:
                self.emit_attention(seq, "l1")
                self.maybe_dump(f"sS_{seq}_1", self.sS[:], "sS")
            self.emit_actT(seq, "recompute_ln", self.x1[seq], router="l1" if "ffn1" in self.parts else None)
            if "ffn1" in self.parts:
                self.emit_moe(seq, "l1")
                self.maybe_dump(f"moe_{seq}_1", self.sE[:], "sE")
            self.emit_combine(seq, 1)

    def emit_final(self):
        bld = self.bld
        bld.barrier()
        for qn in ("sync", "gpsimd"):
            if bld.dcnt[qn]:
                bld.dwait(qn, qn, bld.dcnt[qn])


def _chunk(M):
    Din, F = M.shape
    c = Din // 128
    return np.ascontiguousarray(
        M.reshape(c, 128, F).transpose(1, 0, 2).reshape(128, c * F))


def _heads(M):
    H = M.shape[1] // 64
    return np.ascontiguousarray(
        np.stack([_chunk(M[:, h * 64:(h + 1) * 64]) for h in range(H)]))


def prep_inputs(inputs):
    f = np.float32
    T = lambda a: np.ascontiguousarray(np.asarray(a, f).T)
    w = {}
    w["WqH"] = _heads(T(inputs["L0_attn_Wq"]))
    w["WkH"] = _heads(T(inputs["L0_attn_Wk"]))
    w["WvH"] = _heads(T(inputs["L0_attn_Wv"]))
    w["WpT"] = _chunk(T(inputs["L0_attn_Wp"]))
    w["BqT"] = _chunk(T(inputs["L0_attn_Bq"]))
    w["BkT"] = _chunk(T(inputs["L0_attn_Bk"]))
    w["BvT"] = _chunk(T(inputs["L0_attn_Bv"]))
    w["AqT"] = T(inputs["L0_attn_Aq"])
    w["AkT"] = T(inputs["L0_attn_Ak"])
    w["AvT"] = T(inputs["L0_attn_Av"])
    w["g1"] = np.asarray(inputs["L0_ffn_g1"], f).reshape(1, D)
    w["g2"] = np.asarray(inputs["L0_ffn_g2"], f).reshape(1, D)
    w["fwqH"] = _heads(np.asarray(inputs["L0_ffn_wq"], f))
    w["fwkH"] = _heads(np.asarray(inputs["L0_ffn_wk"], f))
    w["fwvH"] = _heads(np.asarray(inputs["L0_ffn_wv"], f))
    w["fwo"] = _chunk(np.asarray(inputs["L0_ffn_wo"], f))
    w["gateT"] = np.ascontiguousarray(np.asarray(inputs["L0_ffn_gate"], f).T)
    w1 = np.asarray(inputs["L0_ffn_w1"], f)
    w3 = np.asarray(inputs["L0_ffn_w3"], f)
    w2 = np.asarray(inputs["L0_ffn_w2"], f)
    w["w1C"] = np.ascontiguousarray(
        w1.reshape(4, 8, 128, 12, 2, 128).transpose(0, 3, 2, 4, 1, 5).reshape(4, 12, 128, 2048))
    w["w3C"] = np.ascontiguousarray(
        w3.reshape(4, 8, 128, 12, 2, 128).transpose(0, 3, 2, 4, 1, 5).reshape(4, 12, 128, 2048))
    w["w2C"] = np.ascontiguousarray(
        w2.reshape(4, 12, 2, 128, 1024).transpose(0, 1, 3, 2, 4).reshape(4, 12, 128, 2048))
    w["l1wqH"] = _heads(T(inputs["L1_attn_w_q"]))
    w["l1wkvT"] = _chunk(T(inputs["L1_attn_w_kv"]))
    w["l1woT"] = _chunk(T(inputs["L1_attn_w_out"]))
    w["ln_g"] = np.asarray(inputs["L1_attn_ln_g"], f).reshape(1, D)
    w["ln_b"] = np.asarray(inputs["L1_attn_ln_b"], f).reshape(1, D)
    w["lno_g"] = np.asarray(inputs["L1_attn_lno_g"], f).reshape(1, D)
    w["lno_b"] = np.asarray(inputs["L1_attn_lno_b"], f).reshape(1, D)
    w["WrT"] = np.ascontiguousarray(np.asarray(inputs["L1_ffn_Wr"], f).T)
    Wu = np.concatenate([np.asarray(inputs["L1_ffn_Wu"], f),
                         np.asarray(inputs["L1_ffn_Wu_s"], f)[None]], 0)
    Wg = np.concatenate([np.asarray(inputs["L1_ffn_Wg"], f),
                         np.asarray(inputs["L1_ffn_Wg_s"], f)[None]], 0)
    Wd = np.concatenate([np.asarray(inputs["L1_ffn_Wd"], f),
                         np.asarray(inputs["L1_ffn_Wd_s"], f)[None]], 0)
    w["WuC"] = np.ascontiguousarray(
        Wu.reshape(8, 8, 128, 16, 2, 128).transpose(0, 3, 2, 4, 1, 5).reshape(8, 16, 128, 2048))
    w["WgC"] = np.ascontiguousarray(
        Wg.reshape(8, 8, 128, 16, 2, 128).transpose(0, 3, 2, 4, 1, 5).reshape(8, 16, 128, 2048))
    w["WdC"] = np.ascontiguousarray(
        Wd.reshape(8, 16, 2, 128, 1024).transpose(0, 1, 3, 2, 4).reshape(8, 16, 128, 2048))
    return w


def kernel(**inputs):
    x = np.asarray(inputs["L0_attn_x"], np.float32)
    w = prep_inputs(inputs)
    mod = Mod()
    nc = mod.build()
    in_maps = []
    for core in range(8):
        m = dict(w)
        m["x"] = np.ascontiguousarray(x[2 * core:2 * core + 2])
        in_maps.append(m)
    res = run_bass_kernel_spmd(nc, in_maps, core_ids=list(range(8)))
    return np.concatenate([r["y"] for r in res.results], axis=0)


# version 09
# speedup vs baseline: 1.2688x; 1.0011x over previous
import math
from contextlib import ExitStack

import numpy as np
import concourse.bass as bass
import concourse.mybir as mybir
from concourse.bass_utils import run_bass_kernel_spmd

F32 = mybir.dt.float32
BF16 = mybir.dt.bfloat16
I32 = mybir.dt.int32
AF = mybir.ActivationFunctionType
ALU = mybir.AluOpType
AX = mybir.AxisListType
PI = float(np.pi)

SEQ, D = 2048, 1024
TT, NCH = SEQ // 128, D // 128
NEG = -1e30
EPS5 = 1e-5
EPSR = 1.1920928955078125e-07
ENGINES = ("sync", "gpsimd", "vector", "scalar", "tensor")


class Bld:
    THROTTLE = 4

    def __init__(self, nc, ctx):
        self.nc = nc
        self.q = {e: [] for e in ENGINES}
        self.sems = {e: ctx.enter_context(nc.semaphore(f"ms_{e}")) for e in ENGINES}
        self.dsems = {e: ctx.enter_context(nc.semaphore(f"dma_{e}"))
                      for e in ("sync", "gpsimd")}
        self.cnt = {e: 0 for e in ENGINES}
        self.icnt = {e: 0 for e in ENGINES}
        self.dcnt = {"sync": 0, "gpsimd": 0}
        self.lastw = {e: 0 for e in ENGINES}
        self.dlastw = {"sync": 0, "gpsimd": 0}
        self.sink = {}

    def mark(self, eng, fn):
        if self.cnt[eng] - self.lastw[eng] >= self.THROTTLE:
            self.wait(eng, eng, self.cnt[eng] - self.THROTTLE + 1)
        self.cnt[eng] += 1
        self.icnt[eng] = self.icnt.get(eng, 0) + 1
        v = self.cnt[eng]
        sem = self.sems[eng]
        self.q[eng].append(lambda _e, fn=fn, sem=sem: fn().then_inc(sem, 1))
        return v

    def run(self, eng, fn):
        self.icnt[eng] = self.icnt.get(eng, 0) + 1
        self.q[eng].append(lambda _e, fn=fn: fn())

    def wait(self, eng, on_eng, v):
        if v <= 0:
            return
        if eng == on_eng:
            if v <= self.lastw[eng]:
                return
            self.lastw[eng] = v
        sem = self.sems[on_eng]
        self.q[eng].append(lambda e, sem=sem, v=v: e.wait_ge(sem, v))

    def dma(self, queue, out, in_, **kw):
        if self.dcnt[queue] - self.dlastw[queue] >= 16 * self.THROTTLE:
            self.dwait(queue, queue, self.dcnt[queue] - 16 * (self.THROTTLE - 1))
        self.dcnt[queue] += 16
        v = self.dcnt[queue]
        sem = self.dsems[queue]
        self.q[queue].append(
            lambda e, out=out, in_=in_, kw=kw, sem=sem:
            e.dma_start(out=out, in_=in_, **kw).then_inc(sem, 16))
        return v

    def dwait(self, eng, queue, v):
        if v <= 0:
            return
        if eng == queue:
            if v <= self.dlastw[queue]:
                return
            self.dlastw[queue] = v
        sem = self.dsems[queue]
        self.q[eng].append(lambda e, sem=sem, v=v: e.wait_ge(sem, v))

    def barrier(self):
        nc = self.nc
        for e in ENGINES:
            self.cnt[e] += 1
        arrive = dict(self.cnt)
        for e in ENGINES:
            if e in self.dcnt and self.dcnt[e] > 0:
                self.dwait(e, e, self.dcnt[e])
            self.wait(e, e, arrive[e] - 1)
            sem = self.sems[e]
            snk = self.sink.get(e)
            if e == "vector":
                self.q[e].append(lambda _e, sem=sem, snk=snk:
                                 nc.vector.tensor_copy(snk[:], snk[:]).then_inc(sem, 1))
            elif e == "scalar":
                self.q[e].append(lambda _e, sem=sem, snk=snk:
                                 nc.scalar.copy(snk[:], snk[:]).then_inc(sem, 1))
            elif e == "gpsimd":
                self.q[e].append(lambda _e, sem=sem, snk=snk:
                                 nc.gpsimd.memset(snk[:], 0.0).then_inc(sem, 1))
            else:
                self.q[e].append(lambda e_, sem=sem: e_.nop().then_inc(sem, 1))
        for e in ENGINES:
            for o in ENGINES:
                if o != e:
                    self.wait(e, o, arrive[o])

    def replay(self, block):
        bq = self.q

        @block.sync
        def _(eng):
            for fn in bq["sync"]:
                fn(eng)

        @block.gpsimd
        def _(eng):
            for fn in bq["gpsimd"]:
                fn(eng)

        @block.vector
        def _(eng):
            for fn in bq["vector"]:
                fn(eng)

        @block.scalar
        def _(eng):
            for fn in bq["scalar"]:
                fn(eng)

        @block.tensor
        def _(eng):
            for fn in bq["tensor"]:
                fn(eng)


class Trk:

    def __init__(self, bld):
        self.bld = bld
        self.w = None
        self.r = {}

    def rd(self, eng):
        if self.w:
            k, v, idx = self.w
            if isinstance(k, tuple):
                self.bld.dwait(eng, k[1], v)
            elif k != eng:
                self.bld.wait(eng, k, v)
            elif eng != "tensor":
                self.bld.wait(eng, eng, v)

    def did_rd(self, key, v):
        if v > self.r.get(key, 0):
            self.r[key] = v

    def wr(self, eng):
        self.rd(eng)
        for k, v in self.r.items():
            if isinstance(k, tuple):
                self.bld.dwait(eng, k[1], v)
            elif k != eng:
                self.bld.wait(eng, k, v)

    def did_wr(self, key, v):
        idx = self.bld.icnt[key] if isinstance(key, str) else 0
        self.w = (key, v, idx)
        self.r = {}

    def dma_wr_pre(self, queue):
        if self.w:
            k, v = self.w[0], self.w[1]
            if isinstance(k, tuple):
                if k[1] != queue:
                    self.bld.dwait(queue, k[1], v)
            else:
                self.bld.wait(queue, k, v)
        for k, v in self.r.items():
            if isinstance(k, tuple):
                if k[1] != queue:
                    self.bld.dwait(queue, k[1], v)
            else:
                self.bld.wait(queue, k, v)


def cdiv(a, b):
    return (a + b - 1) // b


class Mod:
    def __init__(self, layers=(0, 1), dumps=(), parts=None, nheads=16,
                 outproj=True):
        self.layers = tuple(layers)
        self.dumps = dict(dumps)
        self.dump_aps = {}
        self.nheads = nheads
        self.outproj = outproj
        self.attn_stage = "full"
        self.parts = set(parts) if parts is not None else {
            "attn0", "inner", "moe0", "attn1", "ffn1"}

    def build(self):
        nc = bass.Bass("TRN2", target_bir_lowering=False)
        self.nc = nc
        ctx = ExitStack()
        self.declare_io(nc)
        with ctx:
            self.alloc_sbuf(nc, ctx)
            bld = Bld(nc, ctx)
            self.bld = bld
            bld.sink["vector"] = self.snkv
            bld.sink["scalar"] = self.snks
            bld.sink["gpsimd"] = self.snkg
            self.ptrk = [Trk(bld) for _ in range(8)]
            self.prr = 0
            names = ("actT sS sE harena kh vh qh sblk pblk ptblk oh opr "
                     "stats router tq tk tv lA lB t0 t1 ntbf x0 x1b small "
                     "bcast onesb iota masks ident onesf").split()
            self.trk = {n: Trk(bld) for n in names}
            self.wrtrk = [Trk(bld) for _ in range(4)]
            self.strk = [Trk(bld), Trk(bld)]
            self.trk["smallM"] = Trk(bld)
            self._wri = 0
            self.emit_init()
            for seq in range(2):
                for layer in self.layers:
                    self.emit_layer(seq, layer)
            self.emit_final()
            block = ctx.enter_context(nc.Block())
            bld.replay(block)
        return nc

    def declare_io(self, nc):
        d = {}

        def inp(name, shape):
            d[name] = nc.dram_tensor(name, list(shape), F32, kind="ExternalInput").ap()

        inp("x", (2, SEQ, D))
        inp("WqH", (16, 128, 512)); inp("WkH", (16, 128, 512))
        inp("WvH", (16, 128, 512))
        inp("WpT", (128, NCH * D))
        for w in ("BqT", "BkT", "BvT"):
            inp(w, (128, NCH * 8))
        for w in ("AqT", "AkT", "AvT"):
            inp(w, (8, D))
        inp("g1", (1, D)); inp("g2", (1, D))
        inp("fwqH", (16, 128, 512))
        inp("fwkH", (4, 128, 512)); inp("fwvH", (4, 128, 512))
        inp("fwo", (128, NCH * D))
        inp("gateT", (4, D))
        inp("w1C", (4, 12, 128, 2048)); inp("w3C", (4, 12, 128, 2048))
        inp("w2C", (4, 12, 128, 2048))
        inp("l1wqH", (16, 128, 512)); inp("l1wkvT", (128, NCH * 128))
        inp("l1woT", (128, NCH * D))
        inp("ln_g", (1, D)); inp("ln_b", (1, D))
        inp("lno_g", (1, D)); inp("lno_b", (1, D))
        inp("WrT", (8, D))
        inp("WuC", (8, 16, 128, 2048)); inp("WgC", (8, 16, 128, 2048))
        inp("WdC", (8, 16, 128, 2048))
        self.w = d
        self.y = nc.dram_tensor("y", [2, SEQ, D], F32, kind="ExternalOutput").ap()
        self.x1 = nc.dram_tensor("x1scr", [2, SEQ, D], F32, kind="Internal").ap()
        okind = "ExternalOutput" if "oscr" in self.dumps else "Internal"
        self.oscr = nc.dram_tensor("oscr", [8, 128, SEQ], BF16, kind=okind).ap()
        self.mask_d = nc.dram_tensor("mask_d", [128, 4096], BF16, kind="Internal").ap()
        self.ones_d = nc.dram_tensor("ones_d", [1, 2048], BF16, kind="Internal").ap()
        self.iota_d = nc.dram_tensor("iota_d", [128, 2048], F32, kind="Internal").ap()
        self.cs_d = nc.dram_tensor("cs_d", [32, 2048], BF16, kind="Internal").ap()
        self.sn_d = nc.dram_tensor("sn_d", [32, 2048], BF16, kind="Internal").ap()
        for name, shape in self.dumps.items():
            if shape is None:
                continue
            self.dump_aps[name] = nc.dram_tensor(
                "dump_" + name, list(shape), F32, kind="ExternalOutput").ap()

    def alloc_sbuf(self, nc, ctx):
        A = lambda name, shape, dt=BF16: ctx.enter_context(nc.sbuf_tensor(name, shape, dt))
        self.actT = A("actT", [128, NCH * SEQ])
        self.sS = A("sS", [128, TT * D])
        self.sE = A("sE", [128, TT * D])
        self.harena = A("harena", [128, 32768])
        self.wrot = [A(f"wrot{i}", [128, 2048]) for i in range(4)]
        self.bcast = A("bcast", [128, 6 * D])
        self.stats = A("stats", [128, 64], F32)
        self.router = A("router", [128, 256], F32)
        self.ident = A("ident", [128, 128])
        self.identf = A("identf", [128, 128], F32)
        self.t0 = A("t0", [128, 2 * D], F32)
        self.t1 = A("t1", [128, D], F32)
        self.ntbf = A("ntbf", [128, D])
        self.small = A("small", [128, 64], F32)
        self.consts = A("consts", [128, 4], F32)
        self.onesf = A("onesf", [1, 128], F32)
        self.snkv = A("snkv", [128, 8], F32)
        self.snks = A("snks", [128, 8], F32)
        self.snkg = A("snkg", [128, 8], F32)
        self.psum = [ctx.enter_context(nc.psum_tensor(f"pb{i}", [128, 512], F32))
                     for i in range(8)]
        hbf = self.harena
        hf32 = self.harena[:].bitcast(F32)
        self.sblk = hf32[:, 0:2048]
        self.x0 = hf32[:, 0:1024]
        self.x1b = hf32[:, 1024:2048]
        self.pblk = hbf[:, 4096:6144]
        self.ptblk = hbf[:, 6144:8192]
        self.oh = hbf[:, 8192:10240]
        self.opr = hbf[:, 10240:11264]
        self.KH0, self.QH0 = 11264, 14336
        self.kh = hbf[:, 11264:13312]
        self.vh = hbf[:, 13312:14336]
        self.qh = hbf[:, 14336:16384]
        self.tq = hbf[0:8, 16384:18432]
        self.tk = hbf[0:8, 18432:20480]
        self.tv = hbf[0:8, 20480:22528]
        self.cs_t = hbf[0:64, 16384:18432]
        self.sn_t = hbf[0:64, 18432:20480]
        self.lA = hbf[0:8, 22528:25600]
        self.lB = hbf[:, 25600:25792]
        self.iota = hf32[:, 12896:14944]
        self.masks = hbf[:, 29888:31936]
        self.masksB = hbf[:, 4096:6144]
        self.onesb = hbf[0:1, 0:512]
        self.wA = self.actT[:, 0:NCH * D]

    def _t(self, t):
        return self.trk[t] if isinstance(t, str) else t

    def pacq(self, n=1, avoid=None):
        ids = []
        for _ in range(n):
            i = self.prr
            self.prr = (self.prr + 1) % 8
            if avoid is not None and i == avoid:
                i = self.prr
                self.prr = (self.prr + 1) % 8
            self.ptrk[i].wr("tensor")
            ids.append(i)
        return ids if n > 1 else ids[0]

    def pap(self, i):
        return self.psum[i]

    def pbf(self, i):
        return self.psum[i][:].bitcast(BF16)

    def mmg(self, items, reads=()):
        bld = self.bld
        for t in reads:
            self._t(t).rd("tensor")
        for (bank, out, lhsT, rhs, st, sp) in items[:-1]:
            bld.run("tensor", lambda o=out, l=lhsT, r=rhs, s=st, p=sp:
                    self.nc.tensor.matmul(o, l, r, start=s, stop=p))
        bank, out, lhsT, rhs, st, sp = items[-1]
        v = bld.mark("tensor", lambda o=out, l=lhsT, r=rhs, s=st, p=sp:
                     self.nc.tensor.matmul(o, l, r, start=s, stop=p))
        for b in {it[0] for it in items}:
            self.ptrk[b].did_wr("tensor", v)
        for t in reads:
            self._t(t).did_rd("tensor", v)
        return v

    def op(self, eng, fn, rpsum=(), reads=(), writes=()):
        bld = self.bld
        for b in rpsum:
            self.ptrk[b].rd(eng)
        for t in reads:
            self._t(t).rd(eng)
        for t in writes:
            self._t(t).wr(eng)
        v = bld.mark(eng, fn)
        for b in rpsum:
            self.ptrk[b].did_rd(eng, v)
        for t in reads:
            self._t(t).did_rd(eng, v)
        for t in writes:
            self._t(t).did_wr(eng, v)
        return v

    def dma_into(self, queue, trkname, dst_ap, src_ap):
        t = self._t(trkname)
        t.dma_wr_pre(queue)
        v = self.bld.dma(queue, dst_ap, src_ap)
        t.did_wr(("dma", queue), v)
        return v

    def dma_from(self, queue, trkname, dst_ap, src_ap):
        t = self._t(trkname)
        if t.w:
            k, v = t.w[0], t.w[1]
            if isinstance(k, tuple):
                self.bld.dwait(queue, k[1], v)
            else:
                self.bld.wait(queue, k, v)
        v = self.bld.dma(queue, dst_ap, src_ap)
        t.did_rd(("dma", queue), v)
        return v

    def wload(self, src_ap, ncols=4096):
        i = self._wri
        self._wri = (i + 1) % 4
        t = self.wrtrk[i]
        t.dma_wr_pre("gpsimd")
        v = self.bld.dma("gpsimd", self.wrot[i][:, 0:ncols], src_ap)
        t.did_wr(("dma", "gpsimd"), v)
        return i

    def wuse(self, i):
        self.wrtrk[i].rd("tensor")
        return self.wrot[i]

    def wused(self, i, v):
        self.wrtrk[i].did_rd("tensor", v)

    def maybe_dump(self, name, sbuf_ap, trkname):
        if name not in self.dumps:
            return
        self.bld.barrier()
        self.dma_from("gpsimd", trkname, self.dump_aps[name], sbuf_ap)
        self.bld.barrier()

    def emit_init(self):
        nc, bld = self.nc, self.bld
        bld.mark("vector", lambda: nc.vector.memset(self.snkv[:], 0.0))
        bld.mark("scalar", lambda: nc.scalar.memzero(self.snks[:]))
        bld.mark("gpsimd", lambda: nc.gpsimd.memset(self.snkg[:], 0.0))
        self.op("gpsimd", lambda: nc.gpsimd.memset(self.ident[:], 0.0),
                writes=("ident",))
        self.op("gpsimd", lambda: nc.gpsimd.affine_select(
            out=self.ident[:], in_=self.ident[:], compare_op=ALU.not_equal,
            fill=1.0, base=0, pattern=[[-1, 128]], channel_multiplier=1),
            reads=("ident",), writes=("ident",))
        self.op("gpsimd", lambda: nc.gpsimd.memset(self.identf[:], 0.0),
                writes=("onesf",))
        self.op("gpsimd", lambda: nc.gpsimd.affine_select(
            out=self.identf[:], in_=self.identf[:], compare_op=ALU.not_equal,
            fill=1.0, base=0, pattern=[[-1, 128]], channel_multiplier=1),
            reads=("onesf",), writes=("onesf",))
        self.op("gpsimd", lambda: nc.gpsimd.memset(self.harena[0:1, 0:2048], 1.0),
                writes=("kh",))
        self.dma_from("sync", "kh", self.ones_d, self.harena[0:1, 0:2048])
        for m in range(4):
            sl = self.masks[:, m * 512:(m + 1) * 512]
            self.op("gpsimd", lambda sl=sl: nc.gpsimd.memset(sl, 0.0),
                    writes=("masks",))
            self.op("gpsimd", lambda sl=sl, m=m: nc.gpsimd.affine_select(
                out=sl, in_=sl, compare_op=ALU.is_ge, fill=NEG,
                base=128 * m, pattern=[[-1, 512]], channel_multiplier=1),
                reads=("masks",), writes=("masks",))
            s2 = self.masksB[:, m * 512:(m + 1) * 512]
            self.op("gpsimd", lambda s2=s2: nc.gpsimd.memset(s2, 0.0),
                    writes=("pblk",))
            self.op("gpsimd", lambda s2=s2, m=m: nc.gpsimd.affine_select(
                out=s2, in_=s2, compare_op=ALU.is_ge, fill=NEG,
                base=-128 * m, pattern=[[1, 512]], channel_multiplier=-1),
                reads=("pblk",), writes=("pblk",))
        self.dma_from("sync", "masks", self.mask_d[:, 0:2048], self.masks)
        self.dma_from("sync", "pblk", self.mask_d[:, 2048:4096], self.masksB)
        self.op("gpsimd", lambda: nc.gpsimd.iota(
            self.iota, pattern=[[-1, 2048]], base=0, channel_multiplier=1,
            allow_small_or_imprecise_dtypes=True), writes=("iota",))
        self.dma_from("sync", "iota", self.iota_d, self.iota)
        vo = self.op("vector", lambda: nc.vector.memset(self.onesb[:], 1.0),
                     writes=("onesb",))
        bld.mark("vector", lambda: nc.vector.memset(self.consts[:, 0:1], EPS5))
        bld.mark("vector", lambda: nc.vector.memset(self.consts[:, 1:2], EPSR))
        bld.mark("vector", lambda: nc.vector.memset(self.consts[:, 2:3], 0.0))
        bld.mark("vector", lambda: nc.vector.memset(self.onesf[:], 1.0))
        import os as _os
        _vecs = [] if _os.environ.get("SKIP_BCAST") else ["g1", "g2", "ln_g", "ln_b", "lno_g", "lno_b"]
        for i, name in enumerate(_vecs):
            self.dma_into("sync", "t0", self.t0[0:1, 0:D], self.w[name])
            self.op("vector", lambda: nc.vector.tensor_copy(
                self.ntbf[0:1, :], self.t0[0:1, 0:D]), reads=("t0",), writes=("ntbf",))
            for half in range(2):
                b = self.pacq()
                self.mmg([(b, self.pap(b)[:, 0:512], self.onesb[0:1, 0:128],
                           self.ntbf[0:1, half * 512:(half + 1) * 512], True, True)],
                         reads=("ntbf", "onesb"))
                self.op("vector", lambda b=b, i=i, half=half: nc.vector.tensor_copy(
                    self.bcast[:, i * D + half * 512:i * D + (half + 1) * 512],
                    self.pap(b)[:, 0:512]), rpsum=(b,), writes=("bcast",))
        if not _os.environ.get("SKIP_ROPE"):
            self.emit_rope_tables()
            self.dma_from("sync", "tq", self.cs_d, self.cs_t[0:32, :])
            self.dma_from("sync", "tk", self.sn_d, self.sn_t[0:32, :])
        bld.barrier()

    def emit_rope_tables(self):
        nc, bld = self.nc, self.bld
        sm, t0, t1 = self.small, self.t0, self.t1
        self.op("gpsimd", lambda: nc.gpsimd.iota(
            sm[0:32, 0:1], pattern=[[0, 1]], base=0, channel_multiplier=1,
            allow_small_or_imprecise_dtypes=True), writes=("small",))
        self.op("scalar", lambda: nc.scalar.activation(
            sm[0:32, 1:2], sm[0:32, 0:1], AF.Exp,
            scale=-math.log(10000.0) / 32.0), reads=("small",), writes=("small",))
        self.op("gpsimd", lambda: nc.gpsimd.iota(
            t0[0:32, 0:2048], pattern=[[1, 2048]], base=0, channel_multiplier=0,
            allow_small_or_imprecise_dtypes=True), writes=("t0",))
        self.op("vector", lambda: nc.vector.tensor_scalar_mul(
            t0[0:32, 0:2048], t0[0:32, 0:2048], sm[0:32, 1:2]),
            reads=("small", "t0"), writes=("t0",))
        for half in range(4):
            F = t0[0:32, half * 512:(half + 1) * 512]
            self.op("vector", lambda F=F: nc.vector.tensor_scalar_mul(
                t1[0:32, 0:512], F, 1.0 / (2 * PI)), reads=("t0",), writes=("t1",))
            self.op("vector", lambda: nc.vector.tensor_copy(
                t1[0:32, 512:1024].bitcast(I32)[:, 0:512], t1[0:32, 0:512]),
                reads=("t1",), writes=("t1",))
            self.op("vector", lambda: nc.vector.tensor_copy(
                t1[0:32, 0:512], t1[0:32, 512:1024].bitcast(I32)[:, 0:512]),
                reads=("t1",), writes=("t1",))
            self.op("vector", lambda F=F: nc.vector.scalar_tensor_tensor(
                t1[0:32, 512:1024], t1[0:32, 0:512], -2 * PI, F,
                ALU.mult, ALU.add), reads=("t1", "t0"), writes=("t1",))
            self.op("vector", lambda: nc.vector.tensor_scalar(
                t1[0:32, 0:512], t1[0:32, 512:1024], PI, None, ALU.is_gt),
                reads=("t1",), writes=("t1",))
            self.op("vector", lambda F=F: nc.vector.scalar_tensor_tensor(
                F, t1[0:32, 0:512], -2 * PI, t1[0:32, 512:1024],
                ALU.mult, ALU.add), reads=("t1",), writes=("t0",))
            self.op("scalar", lambda F=F, half=half: nc.scalar.activation(
                self.sn_t[0:32, half * 512:(half + 1) * 512], F, AF.Sin),
                reads=("t0",), writes=("tk",))
            self.op("vector", lambda: nc.vector.tensor_scalar_add(
                t1[0:32, 0:512], t1[0:32, 512:1024], PI / 2),
                reads=("t1",), writes=("t1",))
            self.op("vector", lambda: nc.vector.tensor_scalar(
                t1[0:32, 512:1024], t1[0:32, 0:512], PI, None, ALU.is_gt),
                reads=("t1",), writes=("t1",))
            self.op("vector", lambda: nc.vector.scalar_tensor_tensor(
                t1[0:32, 0:512], t1[0:32, 512:1024], -2 * PI, t1[0:32, 0:512],
                ALU.mult, ALU.add), reads=("t1",), writes=("t1",))
            self.op("scalar", lambda half=half: nc.scalar.activation(
                self.cs_t[0:32, half * 512:(half + 1) * 512],
                t1[0:32, 0:512], AF.Sin), reads=("t1",), writes=("tq",))

    def tile_stats(self, u, utrk, eps, rms, store_i=None):
        nc, bld = self.nc, self.bld
        sm = self.small
        if not rms:
            self.op("vector", lambda: nc.vector.reduce_sum(
                sm[:, 0:1], u[:, 0:D], axis=AX.X, negate=True),
                reads=(utrk,), writes=("small",))
            vv = self.op("vector", lambda: nc.vector.tensor_scalar_mul(
                sm[:, 0:1], sm[:, 0:1], 1.0 / D), writes=("small",))
            bld.wait("scalar", "vector", vv)
        self.op("scalar", lambda rms=rms: nc.scalar.activation(
            self.t1[:, 0:D], u[:, 0:D], AF.Square,
            bias=self.consts[:, 2:3] if rms else sm[:, 0:1],
            accum_out=sm[:, 1:2]), reads=(utrk, "small"), writes=("t1", "small"))
        epscol = self.consts[:, 0:1] if eps == EPS5 else self.consts[:, 1:2]
        bld.run("scalar", lambda epscol=epscol: nc.scalar.activation(
            sm[:, 2:3], sm[:, 1:2], AF.Ln, bias=epscol, scale=1.0 / D))
        s2 = bld.mark("scalar", lambda: nc.scalar.activation(
            sm[:, 3:4], sm[:, 2:3], AF.Exp, scale=-0.5))
        self.trk["small"].did_wr("scalar", s2)
        if rms:
            z = self.op("vector", lambda: nc.vector.memset(sm[:, 4:5], 0.0),
                        writes=("small",))
            bld.wait("scalar", "vector", z)
        else:
            bv = self.op("vector", lambda: nc.vector.tensor_mul(
                sm[:, 4:5], sm[:, 0:1], sm[:, 3:4]), reads=("small",), writes=("small",))
            bld.wait("scalar", "vector", bv)
        if store_i is not None:
            self.op("vector", lambda i=store_i: nc.vector.tensor_copy(
                self.stats[:, 2 * i:2 * i + 2], self.small[:, 3:5]),
                reads=("small",), writes=("stats",))

    def transpose_tile(self, i):
        nc = self.nc
        for c in range(NCH):
            b = self.pacq()
            self.trk["ntbf"].rd("tensor")
            v = self.bld.mark("tensor", lambda b=b, c=c: nc.tensor.transpose(
                self.pbf(b)[:, 0:128], self.ntbf[:, c * 128:(c + 1) * 128],
                self.ident[:]))
            self.ptrk[b].did_wr("tensor", v)
            self.trk["ntbf"].did_rd("tensor", v)
            self.op("vector", lambda b=b, c=c, i=i: nc.vector.tensor_copy(
                self.actT[:, c * SEQ + i * 128:c * SEQ + (i + 1) * 128],
                self.pbf(b)[:, 0:128]), rpsum=(b,), writes=("actT",))

    def emit_actT(self, seq, mode, src, router=None):
        nc, bld = self.nc, self.bld
        xb_ = [self.x0, self.x1b]
        xn_ = ["x0", "x1b"]
        if router is not None:
            nE = 4 if router == "l0" else 8
            hf32 = self.harena[:].bitcast(F32)
            gg = hf32[:, 2048:2048 + nE * 1024]
            g2b = hf32[:, 2048 + nE * 1024:2048 + (nE + 1) * 1024]
            ggt = self.trk.setdefault("gg", Trk(bld))
            rows = self.w["gateT"] if router == "l0" else self.w["WrT"]
            if router == "l0":
                self.dma_into("sync", "t1", self.t1[0:1, :], self.w["g2"])
                for half in range(2):
                    b = self.pacq()
                    self.mmg([(b, self.pap(b)[:, :], self.onesf[:],
                               self.t1[0:1, half * 512:(half + 1) * 512],
                               True, True)], reads=("onesf", "t1"))
                    self.op("vector", lambda b=b, half=half: nc.vector.tensor_copy(
                        g2b[:, half * 512:(half + 1) * 512], self.pap(b)[:, :]),
                        rpsum=(b,), writes=(ggt,))
            for e in range(nE):
                self.dma_into("sync", "t1", self.t1[0:1, :], rows[e])
                for half in range(2):
                    b = self.pacq()
                    self.mmg([(b, self.pap(b)[:, :], self.onesf[:],
                               self.t1[0:1, half * 512:(half + 1) * 512],
                               True, True)], reads=("onesf", "t1"))
                    sl = gg[:, e * 1024 + half * 512:e * 1024 + (half + 1) * 512]
                    self.op("vector", lambda b=b, sl=sl: nc.vector.tensor_copy(
                        sl, self.pap(b)[:, :]), rpsum=(b,), writes=(ggt,))
                if router == "l0":
                    self.op("vector", lambda e=e: nc.vector.tensor_mul(
                        gg[:, e * 1024:(e + 1) * 1024],
                        gg[:, e * 1024:(e + 1) * 1024], g2b),
                        reads=(ggt,), writes=(ggt,))
        for i in range(TT):
            xb, xnm = xb_[i % 2], xn_[i % 2]
            self.dma_into("sync", xnm, xb[:], src[i * 128:(i + 1) * 128, :])
            if mode == "ln":
                self.tile_stats(xb, xnm, EPS5, rms=False, store_i=i)
                self.op("scalar", lambda xb=xb: nc.scalar.activation(
                    self.ntbf[:], xb[:], AF.Identity, scale=self.small[:, 3:4],
                    bias=self.small[:, 4:5]), reads=(xnm, "small"), writes=("ntbf",))
            elif mode == "ln_affine":
                self.tile_stats(xb, xnm, EPS5, rms=False, store_i=i)
                self.op("scalar", lambda xb=xb: nc.scalar.activation(
                    self.t0[:, 0:D], xb[:], AF.Identity, scale=self.small[:, 3:4],
                    bias=self.small[:, 4:5]), reads=(xnm, "small"), writes=("t0",))
                self.tile_stats(self.t0, "t0", EPS5, rms=False)
                self.op("scalar", lambda: nc.scalar.activation(
                    self.t0[:, D:2 * D], self.t0[:, 0:D], AF.Identity,
                    scale=self.small[:, 3:4], bias=self.small[:, 4:5]),
                    reads=("t0", "small"), writes=("t0",))
                self.op("vector", lambda: nc.vector.tensor_mul(
                    self.t0[:, D:2 * D], self.t0[:, D:2 * D],
                    self.bcast[:, 2 * D:3 * D]), reads=("bcast", "t0"), writes=("t0",))
                self.op("vector", lambda: nc.vector.tensor_add(
                    self.ntbf[:], self.t0[:, D:2 * D], self.bcast[:, 3 * D:4 * D]),
                    reads=("bcast", "t0"), writes=("ntbf",))
            elif mode == "recompute_ln":
                self.trk["stats"].rd("scalar")
                if router is None:
                    self.op("scalar", lambda xb=xb, i=i: nc.scalar.activation(
                        self.ntbf[:], xb[:], AF.Identity,
                        scale=self.stats[:, 2 * i:2 * i + 1],
                        bias=self.stats[:, 2 * i + 1:2 * i + 2]),
                        reads=(xnm, "stats"), writes=("ntbf",))
                else:
                    self.op("scalar", lambda xb=xb, i=i: nc.scalar.activation(
                        self.t0[:, 0:D], xb[:], AF.Identity,
                        scale=self.stats[:, 2 * i:2 * i + 1],
                        bias=self.stats[:, 2 * i + 1:2 * i + 2]),
                        reads=(xnm, "stats"), writes=("t0",))
                    self.op("vector", lambda: nc.vector.tensor_copy(
                        self.ntbf[:], self.t0[:, 0:D]), reads=("t0",),
                        writes=("ntbf",))
            elif mode in ("rms1", "rms2"):
                self.trk["stats"].rd("scalar")
                self.op("scalar", lambda xb=xb, i=i: nc.scalar.activation(
                    self.t0[:, 0:D], xb[:], AF.Identity,
                    scale=self.stats[:, 2 * i:2 * i + 1],
                    bias=self.stats[:, 2 * i + 1:2 * i + 2]),
                    reads=(xnm, "stats"), writes=("t0",))
                if mode == "rms2":
                    self.op("vector", lambda i=i: nc.vector.tensor_add(
                        self.t0[:, 0:D], self.t0[:, 0:D],
                        self.sE[:, i * D:(i + 1) * D]),
                        reads=("sE", "t0"), writes=("t0",))
                self.tile_stats(self.t0, "t0", EPSR, rms=True)
                gofs = 0 if mode == "rms1" else D
                self.op("scalar", lambda: nc.scalar.activation(
                    self.t0[:, D:2 * D], self.t0[:, 0:D], AF.Identity,
                    scale=self.small[:, 3:4]), reads=("t0", "small"), writes=("t0",))
                self.op("vector", lambda gofs=gofs: nc.vector.tensor_mul(
                    self.ntbf[:], self.t0[:, D:2 * D], self.bcast[:, gofs:gofs + D]),
                    reads=("bcast", "t0"), writes=("ntbf",))
            else:
                raise ValueError(mode)
            if router is not None:
                u = self.t0[:, D:2 * D] if mode == "rms2" else self.t0[:, 0:D]
                for e in range(nE):
                    self.op("vector", lambda u=u, e=e, i=i:
                            nc.vector.scalar_tensor_tensor(
                                self.t1[:, 0:D], u, 1.0,
                                gg[:, e * 1024:(e + 1) * 1024],
                                ALU.mult, ALU.mult,
                                accum_out=self.router[:, i * 16 + e:i * 16 + e + 1]),
                            reads=("t0", ggt), writes=("t1", "router"))
            self.transpose_tile(i)
        bld.barrier()

    def emit_attention(self, seq, kind):
        nc, bld = self.nc, self.bld
        H = 16
        self.dma_into("sync", "masks", self.masks, self.mask_d[:, 0:2048])
        self.dma_into("sync", "pblk", self.masksB, self.mask_d[:, 2048:4096])
        KB = 70 if kind == "l0" else 67
        hbf = self.harena
        if kind == "l0":
            for r in range(3):
                self.dma_into("gpsimd", "kh",
                              hbf[67 + r:68 + r, self.KH0:self.KH0 + 2048],
                              self.ones_d)
                self.dma_into("gpsimd", "qh",
                              hbf[64 + r:65 + r, self.QH0:self.QH0 + 2048],
                              self.ones_d)
        else:
            for r in range(3):
                self.dma_into("gpsimd", "kh",
                              hbf[64 + r:65 + r, self.KH0:self.KH0 + 2048],
                              self.ones_d)
        if kind == "l0":
            qsrc, ksrc, vsrc = self.w["WqH"], self.w["WkH"], self.w["WvH"]
            qscale, kscale = 0.125, None
            self.dma_into("sync", "iota", self.iota, self.iota_d)
        elif kind == "l0i":
            qsrc, ksrc, vsrc = self.w["fwqH"], self.w["fwkH"], self.w["fwvH"]
            qscale, kscale = 0.125, None
            self.dma_into("sync", "tq", self.cs_t[0:32, :], self.cs_d)
            self.dma_into("sync", "tk", self.sn_t[0:32, :], self.sn_d)
        else:
            qsrc = self.w["l1wqH"]
            sqs = math.sqrt(64.0 ** -0.5)
            qscale, kscale = (64.0 ** -0.5) * sqs, sqs

        if self.attn_stage == "init":
            bld.barrier()
            return
        if kind == "l0":
            self.dma_into("gpsimd", "lA", self.lA[:, 0:D], self.w["AqT"])
            self.dma_into("gpsimd", "lA", self.lA[:, D:2 * D], self.w["AkT"])
            self.dma_into("gpsimd", "lA", self.lA[:, 2 * D:3 * D], self.w["AvT"])
            self.dma_into("gpsimd", "lB", self.lB[:, 0:64], self.w["BqT"])
            self.dma_into("gpsimd", "lB", self.lB[:, 64:128], self.w["BkT"])
            self.dma_into("gpsimd", "lB", self.lB[:, 128:192], self.w["BvT"])
            if self.attn_stage == "lora_dma":
                bld.barrier()
                return
            if self.attn_stage.startswith("lora_mm"):
                import os as _os
                var = self.attn_stage
                b = self.pacq()
                if var == "lora_mm_bare":
                    self.mmg([(b, self.pap(b)[0:8, :], self.lB[:, 0:8],
                               self.actT[:, 0:512], True, True)],
                             reads=("lB", "actT"))
                    bld.barrier()
                    return
                if var == "lora_mm_nt2":
                    self.mmg([(b, self.pap(b)[0:8, :], self.ntbf[:, 0:8],
                               self.actT[:, 0:512], True, True)],
                             reads=("ntbf", "actT"))
                    bld.barrier()
                    return
                if var == "lora_mm_rhs":
                    self.mmg([(b, self.pap(b)[0:8, :], self.lB[:, 0:8],
                               self.ntbf[:, 0:512], True, True)],
                             reads=("lB", "ntbf"))
                    bld.barrier()
                    return
                nmm = 1 if var == "lora_mm1" else NCH
                items = [(b, self.pap(b)[0:8, :],
                          self.lB[:, c * 8:(c + 1) * 8],
                          self.actT[:, c * SEQ + 0:c * SEQ + 512],
                          c == 0, c == nmm - 1) for c in range(nmm)]
                self.mmg(items, reads=("lB", "actT"))
                if var == "lora_mm_v":
                    self.op("vector", lambda b=b: nc.vector.tensor_copy(
                        self.tq[0:8, 0:512], self.pap(b)[0:8, :]),
                        rpsum=(b,), writes=("tq",))
                elif var == "lora_mm_nt":
                    self.op("scalar", lambda b=b: nc.scalar.activation(
                        self.ntbf[0:8, 0:512], self.pap(b)[0:8, :],
                        AF.Copy, scale=2.0), rpsum=(b,), writes=("ntbf",))
                else:
                    self.op("scalar", lambda b=b: nc.scalar.activation(
                        self.tq[0:8, 0:512], self.pap(b)[0:8, :],
                        AF.Copy, scale=2.0), rpsum=(b,), writes=("tq",))
                bld.barrier()
                return
            for ti, (tbuf, tn, bofs) in enumerate(
                    [(self.tq, "tq", 0), (self.tk, "tk", 64), (self.tv, "tv", 128)]):
                for tb in range(4):
                    b = self.pacq()
                    items = [(b, self.pap(b)[0:8, :],
                              self.lB[:, bofs + c * 8:bofs + (c + 1) * 8],
                              self.actT[:, c * SEQ + tb * 512:c * SEQ + (tb + 1) * 512],
                              c == 0, c == NCH - 1) for c in range(NCH)]
                    self.mmg(items, reads=("lB", "actT"))
                    self.op("scalar", lambda b=b, tbuf=tbuf, tb=tb: nc.scalar.activation(
                        tbuf[0:8, tb * 512:(tb + 1) * 512], self.pap(b)[0:8, :],
                        AF.Copy, scale=2.0), rpsum=(b,), writes=(tn,))

        if self.attn_stage == "lora":
            bld.barrier()
            return
        slopes = [2.0 ** (-0.5 * (h + 1)) for h in range(H)]

        for h in range(self.nheads):
            newkv = (kind == "l0") or (kind == "l0i" and h % 4 == 0) or \
                    (kind == "l1" and h == 0)
            if newkv:
                kv = {"l0": h, "l0i": h // 4, "l1": 0}[kind]
                if kind == "l1":
                    wik = self.wload(self.w["l1wkvT"], ncols=NCH * 128)
                    wiv = wik
                    ksl = lambda wt, c: wt[:, c * 128:c * 128 + 64]
                    vsl = lambda wt, c: wt[:, c * 128 + 64:c * 128 + 128]
                else:
                    wik = self.wload(ksrc[kv], ncols=512)
                    wiv = self.wload(vsrc[kv], ncols=512)
                    ksl = lambda wt, c: wt[:, c * 64:(c + 1) * 64]
                    vsl = ksl
                wkt = self.wuse(wik)
                for tb in range(4):
                    b = self.pacq()
                    items = [(b, self.pap(b)[0:64, :], ksl(wkt, c),
                              self.actT[:, c * SEQ + tb * 512:c * SEQ + (tb + 1) * 512],
                              c == 0, False) for c in range(NCH)]
                    if kind == "l0":
                        items.append((b, self.pap(b)[0:64, :],
                                      self.lA[:, D + h * 64:D + (h + 1) * 64],
                                      self.tk[:, tb * 512:(tb + 1) * 512],
                                      False, True))
                    else:
                        items[-1] = items[-1][:5] + (True,)
                    v = self.mmg(items, reads=("actT", "lA", "tk"))
                    self.wused(wik, v)
                    if kscale is None:
                        self.op("vector", lambda b=b, tb=tb: nc.vector.tensor_copy(
                            self.kh[0:64, tb * 512:(tb + 1) * 512],
                            self.pap(b)[0:64, :]), rpsum=(b,), writes=("kh",))
                    else:
                        self.op("scalar", lambda b=b, tb=tb: nc.scalar.activation(
                            self.kh[0:64, tb * 512:(tb + 1) * 512],
                            self.pap(b)[0:64, :], AF.Copy, scale=kscale),
                            rpsum=(b,), writes=("kh",))
                wvt = self.wuse(wiv)
                for tt in range(TT):
                    b = self.pacq()
                    items = [(b, self.pap(b)[:, 0:64],
                              self.actT[:, c * SEQ + tt * 128:c * SEQ + (tt + 1) * 128],
                              vsl(wvt, c), c == 0, False) for c in range(NCH)]
                    if kind == "l0":
                        items.append((b, self.pap(b)[:, 0:64],
                                      self.tv[:, tt * 128:(tt + 1) * 128],
                                      self.lA[:, 2 * D + h * 64:2 * D + (h + 1) * 64],
                                      False, True))
                    else:
                        items[-1] = items[-1][:5] + (True,)
                    v = self.mmg(items, reads=("actT", "lA", "tv"))
                    self.wused(wiv, v)
                    self.op("vector", lambda b=b, tt=tt: nc.vector.tensor_copy(
                        self.vh[:, tt * 64:(tt + 1) * 64], self.pap(b)[:, 0:64]),
                        rpsum=(b,), writes=("vh",))
                if kind == "l0i":
                    self.rope("kh")
            wiq = self.wload(qsrc[h], ncols=512)
            wqt = self.wuse(wiq)
            for tb in range(4):
                b = self.pacq()
                items = [(b, self.pap(b)[0:64, :],
                          wqt[:, c * 64:(c + 1) * 64],
                          self.actT[:, c * SEQ + tb * 512:c * SEQ + (tb + 1) * 512],
                          c == 0, False) for c in range(NCH)]
                if kind == "l0":
                    items.append((b, self.pap(b)[0:64, :],
                                  self.lA[:, h * 64:(h + 1) * 64],
                                  self.tq[:, tb * 512:(tb + 1) * 512], False, True))
                else:
                    items[-1] = items[-1][:5] + (True,)
                v = self.mmg(items, reads=("actT", "lA", "tq"))
                self.wused(wiq, v)
                self.op("scalar", lambda b=b, tb=tb: nc.scalar.activation(
                    self.qh[0:64, tb * 512:(tb + 1) * 512], self.pap(b)[0:64, :],
                    AF.Copy, scale=qscale), rpsum=(b,), writes=("qh",))
            if kind == "l0i":
                self.rope("qh")
            if self.attn_stage == "proj":
                continue

            hbf = self.harena
            sl0 = self.harena[:].bitcast(F32)[0:1, 0:2048]
            if kind == "l0":
                slope = slopes[h]
                self.op("scalar", lambda slope=slope: nc.scalar.activation(
                    self.t0[0:1, 0:2048], self.iota[0:1, :], AF.Identity,
                    scale=slope), reads=("iota",), writes=("t0",))
                self.op("vector", lambda: nc.vector.tensor_copy(
                    self.ptblk[0:1, :], self.t0[0:1, 0:2048]),
                    reads=("t0",), writes=("ptblk",))
                self.dma_from("gpsimd", "ptblk",
                              self.harena[64:65, self.KH0:self.KH0 + 2048].bitcast(F32).bitcast(BF16)
                              if False else hbf[64:65, self.KH0:self.KH0 + 2048],
                              self.ptblk[0:1, :])
                self.op("vector", lambda: nc.vector.tensor_sub(
                    sl0, self.t0[0:1, 0:2048], self.ptblk[0:1, :]),
                    reads=("t0", "ptblk"), writes=("sblk",))
                self.op("vector", lambda: nc.vector.tensor_copy(
                    self.ptblk[0:1, :], sl0), reads=("sblk",), writes=("ptblk",))
                self.dma_from("gpsimd", "ptblk",
                              hbf[65:66, self.KH0:self.KH0 + 2048], self.ptblk[0:1, :])
                self.op("vector", lambda: nc.vector.tensor_sub(
                    self.t0[0:1, 0:2048], sl0, self.ptblk[0:1, :]),
                    reads=("sblk", "ptblk"), writes=("t0",))
                self.op("vector", lambda: nc.vector.tensor_copy(
                    self.ptblk[0:1, :], self.t0[0:1, 0:2048]),
                    reads=("t0",), writes=("ptblk",))
                self.dma_from("gpsimd", "ptblk",
                              hbf[66:67, self.KH0:self.KH0 + 2048], self.ptblk[0:1, :])
                self.trk["kh"].did_wr(("dma", "gpsimd"), self.bld.dcnt["gpsimd"])

            KA = 67 if kind == "l0" else 64
            sm = self.small
            for qt in range(TT):
                nk512 = qt // 4 + 1
                pt = self.strk[qt % 2]
                c0 = 32 + (qt % 2) * 16
                banks = [self.pacq() for _ in range(nk512)]
                items = [(banks[ks], self.pap(banks[ks])[:, :],
                          self.qh[0:KA, qt * 128:(qt + 1) * 128],
                          self.kh[0:KA, ks * 512:(ks + 1) * 512], True, True)
                         for ks in range(nk512)]
                self.mmg(items, reads=("qh", "kh"))
                ksp, mm_ = qt // 4, qt % 4
                b = banks[ksp]
                self.op("vector", lambda b=b, mm_=mm_: nc.vector.tensor_add(
                    self.pap(b)[:, :], self.pap(b)[:, :],
                    self.masks[:, mm_ * 512:(mm_ + 1) * 512]),
                    rpsum=(b,), reads=("masks",), writes=())
                self.ptrk[b].did_wr("vector", self.bld.cnt["vector"])
                for ks in range(nk512):
                    self.op("vector", lambda b=banks[ks], ks=ks, c0=c0:
                            nc.vector.reduce_max(sm[:, c0 + ks:c0 + 1 + ks],
                                                 self.pap(b)[:, :], axis=AX.X,
                                                 negate=True),
                            rpsum=(banks[ks],), writes=(pt,))
                if nk512 > 1:
                    self.op("vector", lambda n=nk512, c0=c0: nc.vector.tensor_reduce(
                        sm[:, c0 + 4:c0 + 5], sm[:, c0:c0 + n], axis=AX.X,
                        op=ALU.min), reads=(pt,), writes=(pt,))
                else:
                    self.op("vector", lambda c0=c0: nc.vector.tensor_copy(
                        sm[:, c0 + 4:c0 + 5], sm[:, c0:c0 + 1]), reads=(pt,),
                        writes=(pt,))
                for ks in range(nk512):
                    b = banks[ks]
                    ev = self.op("scalar", lambda b=b, ks=ks, c0=c0:
                                 nc.scalar.activation(
                                     self.pap(b)[:, :], self.pap(b)[:, :], AF.Exp,
                                     bias=sm[:, c0 + 4:c0 + 5],
                                     accum_out=sm[:, c0 + 5 + ks:c0 + 6 + ks]),
                                 rpsum=(b,), reads=(pt,), writes=(pt,))
                    self.ptrk[b].did_wr("scalar", ev)
                if nk512 > 1:
                    self.op("vector", lambda n=nk512, c0=c0: nc.vector.tensor_reduce(
                        sm[:, c0 + 9:c0 + 10], sm[:, c0 + 5:c0 + 5 + n], axis=AX.X,
                        op=ALU.add), reads=(pt,), writes=(pt,))
                else:
                    self.op("vector", lambda c0=c0: nc.vector.tensor_copy(
                        sm[:, c0 + 9:c0 + 10], sm[:, c0 + 5:c0 + 6]), reads=(pt,),
                        writes=(pt,))
                lv = self.op("scalar", lambda c0=c0: nc.scalar.activation(
                    sm[:, c0 + 10:c0 + 11], sm[:, c0 + 9:c0 + 10], AF.Ln),
                    reads=(pt,), writes=(pt,))
                self.op("vector", lambda qt=qt, c0=c0: nc.vector.tensor_sub(
                    sm[:, 8 + qt:9 + qt], sm[:, c0 + 4:c0 + 5],
                    sm[:, c0 + 10:c0 + 11]),
                    reads=(pt,), writes=("smallM",))
            if self.attn_stage == "passA":
                continue

            bt = self.pacq()
            self.trk["smallM"].rd("tensor")
            vtr = bld.mark("tensor", lambda bt=bt: nc.tensor.transpose(
                self.pap(bt)[0:16, 0:128], sm[:, 8:24], self.identf[:]))
            self.ptrk[bt].did_wr("tensor", vtr)
            self.trk["smallM"].did_rd("tensor", vtr)
            t1b = self.t1[:].bitcast(BF16)
            self.op("vector", lambda bt=bt: nc.vector.tensor_copy(
                self.t1[0:16, 0:128], self.pap(bt)[0:16, 0:128]),
                rpsum=(bt,), writes=("t1",))
            self.op("vector", lambda: nc.vector.tensor_copy(
                t1b[0:16, 1024:1152], self.t1[0:16, 0:128]),
                reads=("t1",), writes=("t1",))
            self.op("vector", lambda: nc.vector.tensor_sub(
                self.t1[0:16, 128:256], self.t1[0:16, 0:128], t1b[0:16, 1024:1152]),
                reads=("t1",), writes=("t1",))
            self.op("vector", lambda: nc.vector.tensor_copy(
                t1b[0:16, 1152:1280], self.t1[0:16, 128:256]),
                reads=("t1",), writes=("t1",))
            self.op("vector", lambda: nc.vector.tensor_sub(
                self.t1[0:16, 256:384], self.t1[0:16, 128:256], t1b[0:16, 1152:1280]),
                reads=("t1",), writes=("t1",))
            self.op("vector", lambda: nc.vector.tensor_copy(
                t1b[0:16, 1280:1408], self.t1[0:16, 256:384]),
                reads=("t1",), writes=("t1",))
            qrow0 = 67 if kind == "l0" else 64
            for r, c0 in ((0, 1024), (1, 1152), (2, 1280)):
                self.dma_from("gpsimd", "t1",
                              hbf[qrow0 + r:qrow0 + r + 1, self.QH0:self.QH0 + 2048],
                              t1b[0:16, c0:c0 + 128])
            self.trk["qh"].did_wr(("dma", "gpsimd"), self.bld.dcnt["gpsimd"])

            for qb in range(4):
                kmax = 4 * (qb + 1)
                bo = self.pacq()
                prev = None
                for kc in range(kmax):
                    b = self.pacq(avoid=bo)
                    self.mmg([(b, self.pap(b)[:, :],
                               self.kh[0:KB, kc * 128:(kc + 1) * 128],
                               self.qh[0:KB, qb * 512:(qb + 1) * 512],
                               True, True)], reads=("qh", "kh"))
                    i = kc - 4 * qb
                    if i >= 0:
                        self.op("vector", lambda b=b, i=i: nc.vector.tensor_add(
                            self.pap(b)[:, :], self.pap(b)[:, :],
                            self.masksB[:, i * 512:(i + 1) * 512]),
                            rpsum=(b,), reads=("pblk",), writes=())
                        self.ptrk[b].did_wr("vector", self.bld.cnt["vector"])
                    pts = self.ptblk[:, (kc % 4) * 512:(kc % 4 + 1) * 512]
                    ev = self.op("scalar", lambda b=b, pts=pts: nc.scalar.activation(
                        pts, self.pap(b)[:, :], AF.Exp), rpsum=(b,),
                        writes=("ptblk",))
                    if prev is not None:
                        self.mmg([prev], reads=("vh", "ptblk"))
                    prev = (bo, self.pap(bo)[0:64, :],
                            self.vh[:, kc * 64:(kc + 1) * 64],
                            pts, kc == 0, kc == kmax - 1)
                self.mmg([prev], reads=("vh", "ptblk"))
                self.op("vector", lambda b=bo, h=h, qb=qb: nc.vector.tensor_copy(
                    self.oh[(h % 2) * 64:(h % 2) * 64 + 64,
                            qb * 512:(qb + 1) * 512],
                    self.pap(b)[0:64, :]), rpsum=(bo,), writes=("oh",))
            if h % 2 == 1 and self.attn_stage == "full":
                self.dma_from("sync", "oh", self.oscr[h // 2], self.oh[:])

        if not self.outproj:
            bld.barrier()
            return
        wname = {"l0": "WpT", "l0i": "fwo", "l1": "l1woT"}[kind]
        self.dma_into("gpsimd", "actT", self.wA, self.w[wname])
        groups = [[qt] for qt in range(TT)]
        for grp in groups:
            g0, gl = grp[0], len(grp)
            for pair in range(8):
                self.dma_into("sync", "opr",
                              self.opr[:, pair * 128:pair * 128 + gl * 128],
                              self.oscr[pair][:, g0 * 128:(g0 + gl) * 128])
            for qt in grp:
                for do in range(2):
                    b = self.pacq()
                    items = [(b, self.pap(b)[:, :],
                              self.opr[:, pair * 128 + (qt - g0) * 128:
                                        pair * 128 + (qt - g0 + 1) * 128],
                              self.wA[:, pair * D + do * 512:pair * D + do * 512 + 512],
                              pair == 0, pair == 7) for pair in range(8)]
                    self.mmg(items, reads=("opr", "actT"))
                    if kind == "l0":
                        self.op("scalar", lambda b=b, qt=qt, do=do: nc.scalar.activation(
                            self.sS[:, qt * D + do * 512:qt * D + (do + 1) * 512],
                            self.pap(b)[:, :], AF.Copy), rpsum=(b,), writes=("sS",))
                    elif kind == "l0i":
                        self.op("scalar", lambda b=b, qt=qt, do=do: nc.scalar.activation(
                            self.sE[:, qt * D + do * 512:qt * D + (do + 1) * 512],
                            self.pap(b)[:, :], AF.Copy), rpsum=(b,), writes=("sE",))
                    else:
                        self.op("vector", lambda b=b, do=do: nc.vector.tensor_copy(
                            self.t0[:, do * 512:(do + 1) * 512], self.pap(b)[:, :]),
                            rpsum=(b,), writes=("t0",))
                if kind == "l1":
                    self.tile_stats(self.t0, "t0", EPS5, rms=False)
                    self.op("scalar", lambda: nc.scalar.activation(
                        self.t1[:, 0:D], self.t0[:, 0:D], AF.Identity,
                        scale=self.small[:, 3:4], bias=self.small[:, 4:5]),
                        reads=("t0", "small"), writes=("t1",))
                    self.op("vector", lambda: nc.vector.tensor_mul(
                        self.t1[:, 0:D], self.t1[:, 0:D], self.bcast[:, 4 * D:5 * D]),
                        reads=("bcast", "t1"), writes=("t1",))
                    self.op("vector", lambda qt=qt: nc.vector.tensor_add(
                        self.sS[:, qt * D:(qt + 1) * D], self.t1[:, 0:D],
                        self.bcast[:, 5 * D:6 * D]),
                        reads=("bcast", "t1"), writes=("sS",))
        bld.barrier()

    def rope(self, bufname):
        nc = self.nc
        buf = getattr(self, bufname)
        hbf = self.harena
        x2c = self.ptblk[0:32, :]
        p2 = hbf[0:32, 0:2048]
        p3 = hbf[0:32, 2048:4096]
        p4 = self.t0[:].bitcast(BF16)[0:32, 0:2048]
        cs, sn = self.cs_t[0:32, :], self.sn_t[0:32, :]
        self.op("vector", lambda: nc.vector.tensor_copy(x2c, buf[32:64, :]),
                reads=(bufname,), writes=("ptblk",))
        self.op("vector", lambda: nc.vector.tensor_mul(p2, x2c, sn),
                reads=("ptblk", "tk"), writes=("sblk",))
        self.op("vector", lambda: nc.vector.tensor_mul(p3, x2c, cs),
                reads=("ptblk", "tq"), writes=("sblk",))
        self.op("vector", lambda: nc.vector.tensor_mul(p4, buf[0:32, :], sn),
                reads=(bufname, "tk"), writes=("t0",))
        self.op("vector", lambda: nc.vector.tensor_mul(x2c, buf[0:32, :], cs),
                reads=(bufname, "tq"), writes=("ptblk",))
        self.op("vector", lambda: nc.vector.tensor_sub(buf[0:32, :], x2c, p2),
                reads=("ptblk", "sblk"), writes=(bufname,))
        self.op("vector", lambda: nc.vector.tensor_add(buf[32:64, :], p3, p4),
                reads=("sblk", "t0"), writes=(bufname,))

    def emit_router(self, kind):
        nc, bld = self.nc, self.bld
        nE = 4 if kind == "l0" else 8
        for i in range(TT):
            base = i * 16
            pr = self.router[:, base:base + 8]
            cb = self.router[:, base + 8:base + 16]
            sm = self.small
            mv = self.op("vector", lambda pr=pr, nE=nE: nc.vector.reduce_max(
                sm[:, 0:1], pr[:, 0:nE], axis=AX.X, negate=True),
                reads=("router",), writes=("small",))
            bld.wait("scalar", "vector", mv)
            ev = self.op("scalar", lambda nE=nE, pr=pr: nc.scalar.activation(
                pr[:, 0:nE], pr[:, 0:nE], AF.Exp,
                bias=sm[:, 0:1], accum_out=sm[:, 1:2]),
                reads=("router", "small"), writes=("router", "small"))
            bld.wait("vector", "scalar", ev)
            self.op("vector", lambda: nc.vector.reciprocal(
                sm[:, 2:3], sm[:, 1:2]), reads=("small",), writes=("small",))
            self.op("vector", lambda pr=pr, nE=nE: nc.vector.tensor_scalar_mul(
                pr[:, 0:nE], pr[:, 0:nE], sm[:, 2:3]),
                reads=("small", "router"), writes=("router",))
            if kind == "l0":
                self.op("vector", lambda pr=pr: nc.vector.reduce_max(
                    sm[:, 0:1], pr[:, 0:4], axis=AX.X), reads=("router",),
                    writes=("small",))
                self.op("vector", lambda pr=pr: nc.vector.tensor_scalar(
                    sm[:, 8:12], pr[:, 0:4], sm[:, 0:1], None, ALU.is_ge),
                    reads=("router", "small"), writes=("small",))
                self.cumsum_mask(sm, 8, 4, 16)
                self.op("vector", lambda pr=pr: nc.vector.scalar_tensor_tensor(
                    sm[:, 20:24], sm[:, 16:20], -1e38, pr[:, 0:4], ALU.mult, ALU.add),
                    reads=("router",), writes=("small",))
                self.op("vector", lambda: nc.vector.reduce_max(
                    sm[:, 1:2], sm[:, 20:24], axis=AX.X), writes=("small",))
                self.op("vector", lambda: nc.vector.tensor_scalar(
                    sm[:, 8:12], sm[:, 20:24], sm[:, 1:2], None, ALU.is_ge),
                    writes=("small",))
                self.cumsum_mask(sm, 8, 4, 24)
                self.op("vector", lambda: nc.vector.tensor_add(
                    sm[:, 16:20], sm[:, 16:20], sm[:, 24:28]), writes=("small",))
                self.op("vector", lambda: nc.vector.tensor_add(
                    sm[:, 2:3], sm[:, 0:1], sm[:, 1:2]), writes=("small",))
                self.op("vector", lambda: nc.vector.reciprocal(
                    sm[:, 3:4], sm[:, 2:3]), writes=("small",))
                self.op("vector", lambda pr=pr: nc.vector.tensor_mul(
                    sm[:, 20:24], pr[:, 0:4], sm[:, 16:20]),
                    reads=("router",), writes=("small",))
                self.op("vector", lambda cb=cb: nc.vector.tensor_scalar_mul(
                    cb[:, 0:4], sm[:, 20:24], sm[:, 3:4]),
                    reads=("small",), writes=("router",))
            else:
                self.op("vector", lambda pr=pr: nc.vector.reduce_max(
                    sm[:, 0:1], pr[:, 0:8], axis=AX.X), reads=("router",),
                    writes=("small",))
                self.op("vector", lambda pr=pr: nc.vector.tensor_scalar(
                    sm[:, 8:16], pr[:, 0:8], sm[:, 0:1], None, ALU.is_ge),
                    reads=("router", "small"), writes=("small",))
                self.cumsum_mask(sm, 8, 8, 16)
                self.op("vector", lambda cb=cb: nc.vector.tensor_scalar(
                    cb[:, 0:8], sm[:, 16:24], sm[:, 0:1], 2.0, ALU.mult, ALU.mult),
                    reads=("small",), writes=("router",))
                self.op("vector", lambda cb=cb: nc.vector.memset(cb[:, 7:8], 1.0),
                        writes=("router",))

    def cumsum_mask(self, sm, src, n, dst):
        nc = self.nc
        self.op("vector", lambda: nc.vector.tensor_copy(
            sm[:, 32:32 + n], sm[:, src:src + n]), writes=("small",))
        cur = 32
        oth = 40
        sh = 1
        while sh < n:
            self.op("vector", lambda cur=cur, oth=oth, sh=sh: nc.vector.tensor_copy(
                sm[:, oth:oth + sh], sm[:, cur:cur + sh]), writes=("small",))
            self.op("vector", lambda cur=cur, oth=oth, sh=sh: nc.vector.tensor_add(
                sm[:, oth + sh:oth + n], sm[:, cur + sh:cur + n], sm[:, cur:cur + n - sh]),
                writes=("small",))
            cur, oth = oth, cur
            sh *= 2
        self.op("vector", lambda cur=cur: nc.vector.tensor_scalar(
            sm[:, 48:48 + n], sm[:, cur:cur + n], 1.0, None, ALU.is_equal),
            writes=("small",))
        self.op("vector", lambda dst=dst: nc.vector.tensor_mul(
            sm[:, dst:dst + n], sm[:, src:src + n], sm[:, 48:48 + n]),
            writes=("small",))

    def emit_moe(self, seq, kind):
        nc, bld = self.nc, self.bld
        self.emit_router(kind)
        if kind == "l0":
            nE, HC, NG = 4, 24, 12
            wup, wgt, wdn = self.w["w1C"], self.w["w3C"], self.w["w2C"]
        else:
            nE, HC, NG = 8, 32, 16
            wup, wgt, wdn = self.w["WgC"], self.w["WuC"], self.w["WdC"]
            z = self.op("gpsimd", lambda: nc.gpsimd.memset(self.sE[:], 0.0),
                        writes=("sE",))
        ones_col = self.small[:, 63:64]
        oz = self.op("vector", lambda: nc.vector.memset(ones_col, 1.0),
                     writes=("small",))
        for e in range(nE):
            for blk in range(2):
                for g in range(NG):
                    wi1 = self.wload(wup[e, g], ncols=2048)
                    wi3 = self.wload(wgt[e, g], ncols=2048)
                    w1t = self.wrot[wi1]
                    w3t = self.wrot[wi3]
                    for hcl in range(2):
                        hc = g * 2 + hcl
                        for half in range(2):
                            tok0 = blk * 1024 + half * 512
                            bA = self.pacq()
                            self.wrtrk[wi1].rd("tensor")
                            itemsA = [(bA, self.pap(bA)[:, :],
                                       w1t[:, hcl * 1024 + c * 128:hcl * 1024 + (c + 1) * 128],
                                       self.actT[:, c * SEQ + tok0:c * SEQ + tok0 + 512],
                                       c == 0, c == NCH - 1) for c in range(NCH)]
                            vA = self.mmg(itemsA, reads=("actT",))
                            self.wused(wi1, vA)
                            bB = self.pacq()
                            self.wrtrk[wi3].rd("tensor")
                            itemsB = [(bB, self.pap(bB)[:, :],
                                       w3t[:, hcl * 1024 + c * 128:hcl * 1024 + (c + 1) * 128],
                                       self.actT[:, c * SEQ + tok0:c * SEQ + tok0 + 512],
                                       c == 0, c == NCH - 1) for c in range(NCH)]
                            vB = self.mmg(itemsB, reads=("actT",))
                            self.wused(wi3, vB)
                            sv = self.op("scalar", lambda bA=bA: nc.scalar.activation(
                                self.t1[:, 0:512], self.pap(bA)[:, :], AF.Silu),
                                rpsum=(bA,), writes=("t1",))
                            bld.wait("vector", "scalar", sv)
                            self.op("vector", lambda bB=bB, hc=hc, half=half:
                                    nc.vector.tensor_mul(
                                        self.harena[:, hc * 1024 + half * 512:
                                                    hc * 1024 + (half + 1) * 512],
                                        self.t1[:, 0:512], self.pap(bB)[:, :]),
                                    rpsum=(bB,), reads=("t1",), writes=("harena",))
                for tokblk in range(2):
                    tok0 = blk * 1024 + tokblk * 512
                    tile0 = tok0 // 128
                    banks = self.pacq(8)
                    for g in range(NG):
                        wi2 = self.wload(wdn[e, g], ncols=2048)
                        w2t = self.wrot[wi2]
                        items = []
                        for hcl in range(2):
                            hc = g * 2 + hcl
                            for st in range(4):
                                for do in range(2):
                                    items.append((
                                        banks[st * 2 + do],
                                        self.pap(banks[st * 2 + do])[:, :],
                                        self.harena[:, hc * 1024 + tokblk * 512 +
                                                    st * 128:hc * 1024 + tokblk * 512 +
                                                    (st + 1) * 128],
                                        w2t[:, hcl * 1024 + do * 512:
                                            hcl * 1024 + (do + 1) * 512],
                                        hc == 0, hc == HC - 1))
                        self.wrtrk[wi2].rd("tensor")
                        v = self.mmg(items, reads=("harena",))
                        self.wused(wi2, v)
                    for st in range(4):
                        for do in range(2):
                            b = banks[st * 2 + do]
                            tile = tile0 + st
                            if kind == "l0":
                                comb = self.router[:, tile * 16 + 8 + e:tile * 16 + 9 + e]
                            else:
                                comb = (ones_col if e == 7 else
                                        self.router[:, tile * 16 + 8 + e:tile * 16 + 9 + e])
                            self.op("vector", lambda b=b, tile=tile, do=do, comb=comb:
                                    nc.vector.scalar_tensor_tensor(
                                        self.sE[:, tile * D + do * 512:
                                                tile * D + (do + 1) * 512],
                                        self.pap(b)[:, :], comb,
                                        self.sE[:, tile * D + do * 512:
                                                tile * D + (do + 1) * 512],
                                        ALU.mult, ALU.add),
                                    rpsum=(b,), reads=("router", "small"),
                                    writes=("sE",))
        bld.barrier()

    def emit_combine(self, seq, layer):
        nc, bld = self.nc, self.bld
        src = self.w["x"][seq] if layer == 0 else self.x1[seq]
        dst = self.x1[seq] if layer == 0 else self.y[seq]
        xb_ = [self.x0, self.x1b]
        xn_ = ["x0", "x1b"]
        for i in range(TT):
            xb, xnm = xb_[i % 2], xn_[i % 2]
            self.dma_into("sync", xnm, xb[:], src[i * 128:(i + 1) * 128, :])
            if layer == 0:
                self.trk["stats"].rd("scalar")
                self.op("scalar", lambda xb=xb, i=i: nc.scalar.activation(
                    self.t0[:, 0:D], xb[:], AF.Identity,
                    scale=self.stats[:, 2 * i:2 * i + 1],
                    bias=self.stats[:, 2 * i + 1:2 * i + 2]),
                    reads=(xnm, "stats"), writes=("t0",))
                self.op("vector", lambda i=i: nc.vector.tensor_add(
                    self.t0[:, 0:D], self.t0[:, 0:D], self.sE[:, i * D:(i + 1) * D]),
                    reads=("sE", "t0"), writes=("t0",))
                gv = self.op("scalar", lambda: nc.scalar.activation(
                    self.t1[:, 0:D], self.t0[:, 0:D], AF.Silu),
                    reads=("t0",), writes=("t1",))
            else:
                gv = self.op("scalar", lambda i=i: nc.scalar.activation(
                    self.t1[:, 0:D], self.sE[:, i * D:(i + 1) * D], AF.Silu),
                    reads=("sE",), writes=("t1",))
            self.op("vector", lambda i=i: nc.vector.tensor_mul(
                self.t1[:, 0:D], self.t1[:, 0:D], self.sS[:, i * D:(i + 1) * D]),
                reads=("sS", "t1"), writes=("t1",))
            ov = self.op("vector", lambda xb=xb: nc.vector.tensor_add(
                self.t0[:, 0:D], xb[:], self.t1[:, 0:D]),
                reads=(xnm, "t1"), writes=("t0",))
            self.dma_from("sync", "t0", dst[i * 128:(i + 1) * 128, :], self.t0[:, 0:D])
        bld.barrier()

    def emit_layer(self, seq, layer):
        x = self.w["x"]
        if layer == 0:
            self.emit_actT(seq, "ln", x[seq])
            self.maybe_dump(f"actT_{seq}_0", self.actT[:], "actT")
            if "attn0" in self.parts:
                self.emit_attention(seq, "l0")
                self.maybe_dump(f"sS_{seq}_0", self.sS[:], "sS")
            self.emit_actT(seq, "rms1", x[seq])
            if "inner" in self.parts:
                self.emit_attention(seq, "l0i")
                self.maybe_dump(f"sE_{seq}_0", self.sE[:], "sE")
            self.emit_actT(seq, "rms2", x[seq], router="l0" if "moe0" in self.parts else None)
            if "moe0" in self.parts:
                self.emit_moe(seq, "l0")
                self.maybe_dump(f"moe_{seq}_0", self.sE[:], "sE")
            self.emit_combine(seq, 0)
        else:
            self.emit_actT(seq, "ln_affine", self.x1[seq])
            if "attn1" in self.parts:
                self.emit_attention(seq, "l1")
                self.maybe_dump(f"sS_{seq}_1", self.sS[:], "sS")
            self.emit_actT(seq, "recompute_ln", self.x1[seq], router="l1" if "ffn1" in self.parts else None)
            if "ffn1" in self.parts:
                self.emit_moe(seq, "l1")
                self.maybe_dump(f"moe_{seq}_1", self.sE[:], "sE")
            self.emit_combine(seq, 1)

    def emit_final(self):
        bld = self.bld
        bld.barrier()
        for qn in ("sync", "gpsimd"):
            if bld.dcnt[qn]:
                bld.dwait(qn, qn, bld.dcnt[qn])


def _chunk(M):
    Din, F = M.shape
    c = Din // 128
    return np.ascontiguousarray(
        M.reshape(c, 128, F).transpose(1, 0, 2).reshape(128, c * F))


def _heads(M):
    H = M.shape[1] // 64
    return np.ascontiguousarray(
        np.stack([_chunk(M[:, h * 64:(h + 1) * 64]) for h in range(H)]))


def prep_inputs(inputs):
    f = np.float32
    T = lambda a: np.ascontiguousarray(np.asarray(a, f).T)
    w = {}
    w["WqH"] = _heads(T(inputs["L0_attn_Wq"]))
    w["WkH"] = _heads(T(inputs["L0_attn_Wk"]))
    w["WvH"] = _heads(T(inputs["L0_attn_Wv"]))
    w["WpT"] = _chunk(T(inputs["L0_attn_Wp"]))
    w["BqT"] = _chunk(T(inputs["L0_attn_Bq"]))
    w["BkT"] = _chunk(T(inputs["L0_attn_Bk"]))
    w["BvT"] = _chunk(T(inputs["L0_attn_Bv"]))
    w["AqT"] = T(inputs["L0_attn_Aq"])
    w["AkT"] = T(inputs["L0_attn_Ak"])
    w["AvT"] = T(inputs["L0_attn_Av"])
    w["g1"] = np.asarray(inputs["L0_ffn_g1"], f).reshape(1, D)
    w["g2"] = np.asarray(inputs["L0_ffn_g2"], f).reshape(1, D)
    w["fwqH"] = _heads(np.asarray(inputs["L0_ffn_wq"], f))
    w["fwkH"] = _heads(np.asarray(inputs["L0_ffn_wk"], f))
    w["fwvH"] = _heads(np.asarray(inputs["L0_ffn_wv"], f))
    w["fwo"] = _chunk(np.asarray(inputs["L0_ffn_wo"], f))
    w["gateT"] = np.ascontiguousarray(np.asarray(inputs["L0_ffn_gate"], f).T)
    w1 = np.asarray(inputs["L0_ffn_w1"], f)
    w3 = np.asarray(inputs["L0_ffn_w3"], f)
    w2 = np.asarray(inputs["L0_ffn_w2"], f)
    w["w1C"] = np.ascontiguousarray(
        w1.reshape(4, 8, 128, 12, 2, 128).transpose(0, 3, 2, 4, 1, 5).reshape(4, 12, 128, 2048))
    w["w3C"] = np.ascontiguousarray(
        w3.reshape(4, 8, 128, 12, 2, 128).transpose(0, 3, 2, 4, 1, 5).reshape(4, 12, 128, 2048))
    w["w2C"] = np.ascontiguousarray(
        w2.reshape(4, 12, 2, 128, 1024).transpose(0, 1, 3, 2, 4).reshape(4, 12, 128, 2048))
    w["l1wqH"] = _heads(T(inputs["L1_attn_w_q"]))
    w["l1wkvT"] = _chunk(T(inputs["L1_attn_w_kv"]))
    w["l1woT"] = _chunk(T(inputs["L1_attn_w_out"]))
    w["ln_g"] = np.asarray(inputs["L1_attn_ln_g"], f).reshape(1, D)
    w["ln_b"] = np.asarray(inputs["L1_attn_ln_b"], f).reshape(1, D)
    w["lno_g"] = np.asarray(inputs["L1_attn_lno_g"], f).reshape(1, D)
    w["lno_b"] = np.asarray(inputs["L1_attn_lno_b"], f).reshape(1, D)
    w["WrT"] = np.ascontiguousarray(np.asarray(inputs["L1_ffn_Wr"], f).T)
    Wu = np.concatenate([np.asarray(inputs["L1_ffn_Wu"], f),
                         np.asarray(inputs["L1_ffn_Wu_s"], f)[None]], 0)
    Wg = np.concatenate([np.asarray(inputs["L1_ffn_Wg"], f),
                         np.asarray(inputs["L1_ffn_Wg_s"], f)[None]], 0)
    Wd = np.concatenate([np.asarray(inputs["L1_ffn_Wd"], f),
                         np.asarray(inputs["L1_ffn_Wd_s"], f)[None]], 0)
    w["WuC"] = np.ascontiguousarray(
        Wu.reshape(8, 8, 128, 16, 2, 128).transpose(0, 3, 2, 4, 1, 5).reshape(8, 16, 128, 2048))
    w["WgC"] = np.ascontiguousarray(
        Wg.reshape(8, 8, 128, 16, 2, 128).transpose(0, 3, 2, 4, 1, 5).reshape(8, 16, 128, 2048))
    w["WdC"] = np.ascontiguousarray(
        Wd.reshape(8, 16, 2, 128, 1024).transpose(0, 1, 3, 2, 4).reshape(8, 16, 128, 2048))
    return w


def kernel(**inputs):
    x = np.asarray(inputs["L0_attn_x"], np.float32)
    w = prep_inputs(inputs)
    mod = Mod()
    nc = mod.build()
    in_maps = []
    for core in range(8):
        m = dict(w)
        m["x"] = np.ascontiguousarray(x[2 * core:2 * core + 2])
        in_maps.append(m)
    res = run_bass_kernel_spmd(nc, in_maps, core_ids=list(range(8)))
    return np.concatenate([r["y"] for r in res.results], axis=0)


# version 10
# speedup vs baseline: 1.3402x; 1.0563x over previous
import math
from contextlib import ExitStack

import numpy as np
import concourse.bass as bass
import concourse.mybir as mybir
from concourse.bass_utils import run_bass_kernel_spmd

F32 = mybir.dt.float32
BF16 = mybir.dt.bfloat16
I32 = mybir.dt.int32
AF = mybir.ActivationFunctionType
ALU = mybir.AluOpType
AX = mybir.AxisListType
PI = float(np.pi)

SEQ, D = 2048, 1024
TT, NCH = SEQ // 128, D // 128
NEG = -1e30
EPS5 = 1e-5
EPSR = 1.1920928955078125e-07
ENGINES = ("sync", "gpsimd", "vector", "scalar", "tensor")


class Bld:
    THROTTLE = 4

    def __init__(self, nc, ctx):
        self.nc = nc
        self.q = {e: [] for e in ENGINES}
        self.sems = {e: ctx.enter_context(nc.semaphore(f"ms_{e}")) for e in ENGINES}
        self.dsems = {e: ctx.enter_context(nc.semaphore(f"dma_{e}"))
                      for e in ("sync", "gpsimd")}
        self.cnt = {e: 0 for e in ENGINES}
        self.icnt = {e: 0 for e in ENGINES}
        self.dcnt = {"sync": 0, "gpsimd": 0}
        self.lastw = {e: 0 for e in ENGINES}
        self.dlastw = {"sync": 0, "gpsimd": 0}
        self.sink = {}

    def mark(self, eng, fn):
        if self.cnt[eng] - self.lastw[eng] >= self.THROTTLE:
            self.wait(eng, eng, self.cnt[eng] - self.THROTTLE + 1)
        self.cnt[eng] += 1
        self.icnt[eng] = self.icnt.get(eng, 0) + 1
        v = self.cnt[eng]
        sem = self.sems[eng]
        self.q[eng].append(lambda _e, fn=fn, sem=sem: fn().then_inc(sem, 1))
        return v

    def run(self, eng, fn):
        self.icnt[eng] = self.icnt.get(eng, 0) + 1
        self.q[eng].append(lambda _e, fn=fn: fn())

    def wait(self, eng, on_eng, v):
        if v <= 0:
            return
        if eng == on_eng:
            if v <= self.lastw[eng]:
                return
            self.lastw[eng] = v
        sem = self.sems[on_eng]
        self.q[eng].append(lambda e, sem=sem, v=v: e.wait_ge(sem, v))

    def dma(self, queue, out, in_, **kw):
        if self.dcnt[queue] - self.dlastw[queue] >= 16 * self.THROTTLE:
            self.dwait(queue, queue, self.dcnt[queue] - 16 * (self.THROTTLE - 1))
        self.dcnt[queue] += 16
        v = self.dcnt[queue]
        sem = self.dsems[queue]
        self.q[queue].append(
            lambda e, out=out, in_=in_, kw=kw, sem=sem:
            e.dma_start(out=out, in_=in_, **kw).then_inc(sem, 16))
        return v

    def dwait(self, eng, queue, v):
        if v <= 0:
            return
        if eng == queue:
            if v <= self.dlastw[queue]:
                return
            self.dlastw[queue] = v
        sem = self.dsems[queue]
        self.q[eng].append(lambda e, sem=sem, v=v: e.wait_ge(sem, v))

    def barrier(self):
        nc = self.nc
        for e in ENGINES:
            self.cnt[e] += 1
        arrive = dict(self.cnt)
        for e in ENGINES:
            if e in self.dcnt and self.dcnt[e] > 0:
                self.dwait(e, e, self.dcnt[e])
            self.wait(e, e, arrive[e] - 1)
            sem = self.sems[e]
            snk = self.sink.get(e)
            if e == "vector":
                self.q[e].append(lambda _e, sem=sem, snk=snk:
                                 nc.vector.tensor_copy(snk[:], snk[:]).then_inc(sem, 1))
            elif e == "scalar":
                self.q[e].append(lambda _e, sem=sem, snk=snk:
                                 nc.scalar.copy(snk[:], snk[:]).then_inc(sem, 1))
            elif e == "gpsimd":
                self.q[e].append(lambda _e, sem=sem, snk=snk:
                                 nc.gpsimd.memset(snk[:], 0.0).then_inc(sem, 1))
            else:
                self.q[e].append(lambda e_, sem=sem: e_.nop().then_inc(sem, 1))
        for e in ENGINES:
            for o in ENGINES:
                if o != e:
                    self.wait(e, o, arrive[o])

    def replay(self, block):
        bq = self.q

        @block.sync
        def _(eng):
            for fn in bq["sync"]:
                fn(eng)

        @block.gpsimd
        def _(eng):
            for fn in bq["gpsimd"]:
                fn(eng)

        @block.vector
        def _(eng):
            for fn in bq["vector"]:
                fn(eng)

        @block.scalar
        def _(eng):
            for fn in bq["scalar"]:
                fn(eng)

        @block.tensor
        def _(eng):
            for fn in bq["tensor"]:
                fn(eng)


class Trk:

    def __init__(self, bld):
        self.bld = bld
        self.w = None
        self.r = {}

    def rd(self, eng):
        if self.w:
            k, v, idx = self.w
            if isinstance(k, tuple):
                self.bld.dwait(eng, k[1], v)
            elif k != eng:
                self.bld.wait(eng, k, v)
            elif eng != "tensor":
                self.bld.wait(eng, eng, v)

    def did_rd(self, key, v):
        if v > self.r.get(key, 0):
            self.r[key] = v

    def wr(self, eng):
        self.rd(eng)
        for k, v in self.r.items():
            if isinstance(k, tuple):
                self.bld.dwait(eng, k[1], v)
            elif k != eng:
                self.bld.wait(eng, k, v)

    def did_wr(self, key, v):
        idx = self.bld.icnt[key] if isinstance(key, str) else 0
        self.w = (key, v, idx)
        self.r = {}

    def dma_wr_pre(self, queue):
        if self.w:
            k, v = self.w[0], self.w[1]
            if isinstance(k, tuple):
                if k[1] != queue:
                    self.bld.dwait(queue, k[1], v)
            else:
                self.bld.wait(queue, k, v)
        for k, v in self.r.items():
            if isinstance(k, tuple):
                if k[1] != queue:
                    self.bld.dwait(queue, k[1], v)
            else:
                self.bld.wait(queue, k, v)


def cdiv(a, b):
    return (a + b - 1) // b


class Mod:
    def __init__(self, layers=(0, 1), dumps=(), parts=None, nheads=16,
                 outproj=True):
        self.layers = tuple(layers)
        self.dumps = dict(dumps)
        self.dump_aps = {}
        self.nheads = nheads
        self.outproj = outproj
        self.attn_stage = "full"
        self.parts = set(parts) if parts is not None else {
            "attn0", "inner", "moe0", "attn1", "ffn1"}

    def build(self):
        nc = bass.Bass("TRN2", target_bir_lowering=False)
        self.nc = nc
        ctx = ExitStack()
        self.declare_io(nc)
        with ctx:
            self.alloc_sbuf(nc, ctx)
            bld = Bld(nc, ctx)
            self.bld = bld
            bld.sink["vector"] = self.snkv
            bld.sink["scalar"] = self.snks
            bld.sink["gpsimd"] = self.snkg
            self.ptrk = [Trk(bld) for _ in range(8)]
            self.prr = 0
            names = ("actT sS sE harena kh vh qh sblk pblk ptblk oh opr "
                     "stats router tq tk tv lA lB t0 t1 ntbf x0 x1b small "
                     "bcast onesb iota masks ident onesf").split()
            self.trk = {n: Trk(bld) for n in names}
            self.wrtrk = [Trk(bld) for _ in range(4)]
            self.strk = [Trk(bld), Trk(bld)]
            self.trk["smallM"] = Trk(bld)
            self._wri = 0
            self.emit_init()
            for seq in range(2):
                for layer in self.layers:
                    self.emit_layer(seq, layer)
            self.emit_final()
            block = ctx.enter_context(nc.Block())
            bld.replay(block)
        return nc

    def declare_io(self, nc):
        d = {}

        def inp(name, shape):
            d[name] = nc.dram_tensor(name, list(shape), F32, kind="ExternalInput").ap()

        inp("x", (2, SEQ, D))
        inp("WqH", (16, 128, 512)); inp("WkH", (16, 128, 512))
        inp("WvH", (16, 128, 512))
        inp("WpT", (128, NCH * D))
        for w in ("BqT", "BkT", "BvT"):
            inp(w, (128, NCH * 8))
        for w in ("AqT", "AkT", "AvT"):
            inp(w, (8, D))
        inp("g1", (1, D)); inp("g2", (1, D))
        inp("fwqH", (16, 128, 512))
        inp("fwkH", (4, 128, 512)); inp("fwvH", (4, 128, 512))
        inp("fwo", (128, NCH * D))
        inp("gateT", (4, D))
        inp("w1C", (4, 12, 128, 2048)); inp("w3C", (4, 12, 128, 2048))
        inp("w2C", (4, 12, 128, 2048))
        inp("l1wqH", (16, 128, 512)); inp("l1wkvT", (128, NCH * 128))
        inp("l1woT", (128, NCH * D))
        inp("ln_g", (1, D)); inp("ln_b", (1, D))
        inp("lno_g", (1, D)); inp("lno_b", (1, D))
        inp("WrT", (8, D))
        inp("WuC", (8, 16, 128, 2048)); inp("WgC", (8, 16, 128, 2048))
        inp("WdC", (8, 16, 128, 2048))
        self.w = d
        self.y = nc.dram_tensor("y", [2, SEQ, D], F32, kind="ExternalOutput").ap()
        self.x1 = nc.dram_tensor("x1scr", [2, SEQ, D], F32, kind="Internal").ap()
        okind = "ExternalOutput" if "oscr" in self.dumps else "Internal"
        self.oscr = nc.dram_tensor("oscr", [8, 128, SEQ], BF16, kind=okind).ap()
        self.mask_d = nc.dram_tensor("mask_d", [128, 4096], BF16, kind="Internal").ap()
        self.ones_d = nc.dram_tensor("ones_d", [1, 2048], BF16, kind="Internal").ap()
        self.iota_d = nc.dram_tensor("iota_d", [128, 2048], F32, kind="Internal").ap()
        self.cs_d = nc.dram_tensor("cs_d", [32, 2048], BF16, kind="Internal").ap()
        self.sn_d = nc.dram_tensor("sn_d", [32, 2048], BF16, kind="Internal").ap()
        for name, shape in self.dumps.items():
            if shape is None:
                continue
            self.dump_aps[name] = nc.dram_tensor(
                "dump_" + name, list(shape), F32, kind="ExternalOutput").ap()

    def alloc_sbuf(self, nc, ctx):
        A = lambda name, shape, dt=BF16: ctx.enter_context(nc.sbuf_tensor(name, shape, dt))
        self.actT = A("actT", [128, NCH * SEQ])
        self.sS = A("sS", [128, TT * D])
        self.sE = A("sE", [128, TT * D])
        self.harena = A("harena", [128, 32768])
        self.wrot = [A(f"wrot{i}", [128, 2048]) for i in range(4)]
        self.bcast = A("bcast", [128, 6 * D])
        self.stats = A("stats", [128, 64], F32)
        self.router = A("router", [128, 256], F32)
        self.ident = A("ident", [128, 128])
        self.identf = A("identf", [128, 128], F32)
        self.t0 = A("t0", [128, 2 * D], F32)
        self.t1 = A("t1", [128, D], F32)
        self.ntbf = A("ntbf", [128, D])
        self.small = A("small", [128, 64], F32)
        self.consts = A("consts", [128, 4], F32)
        self.onesf = A("onesf", [1, 128], F32)
        self.snkv = A("snkv", [128, 8], F32)
        self.snks = A("snks", [128, 8], F32)
        self.snkg = A("snkg", [128, 8], F32)
        self.psum = [ctx.enter_context(nc.psum_tensor(f"pb{i}", [128, 512], F32))
                     for i in range(8)]
        hbf = self.harena
        hf32 = self.harena[:].bitcast(F32)
        self.sblk = hf32[:, 0:2048]
        self.x0 = hf32[:, 0:1024]
        self.x1b = hf32[:, 1024:2048]
        self.pblk = hbf[:, 4096:6144]
        self.ptblk = hbf[:, 6144:8192]
        self.oh = hbf[:, 8192:10240]
        self.opr = hbf[:, 10240:11264]
        self.KH0, self.QH0 = 11264, 14336
        self.kh = hbf[:, 11264:13312]
        self.vh = hbf[:, 13312:14336]
        self.qh = hbf[:, 14336:16384]
        self.tq = hbf[0:8, 16384:18432]
        self.tk = hbf[0:8, 18432:20480]
        self.tv = hbf[0:8, 20480:22528]
        self.cs_t = hbf[0:64, 16384:18432]
        self.sn_t = hbf[0:64, 18432:20480]
        self.lA = hbf[0:8, 22528:25600]
        self.lB = hbf[:, 25600:25792]
        self.iota = hf32[:, 12896:14944]
        self.masks = hbf[:, 29888:31936]
        self.masksB = hbf[:, 4096:6144]
        self.onesb = hbf[0:1, 0:512]
        self.wA = self.actT[:, 0:NCH * D]

    def _t(self, t):
        return self.trk[t] if isinstance(t, str) else t

    def pacq(self, n=1, avoid=None):
        ids = []
        for _ in range(n):
            i = self.prr
            self.prr = (self.prr + 1) % 8
            if avoid is not None and i == avoid:
                i = self.prr
                self.prr = (self.prr + 1) % 8
            self.ptrk[i].wr("tensor")
            ids.append(i)
        return ids if n > 1 else ids[0]

    def pap(self, i):
        return self.psum[i]

    def pbf(self, i):
        return self.psum[i][:].bitcast(BF16)

    def mmg(self, items, reads=()):
        bld = self.bld
        for t in reads:
            self._t(t).rd("tensor")
        for (bank, out, lhsT, rhs, st, sp) in items[:-1]:
            bld.run("tensor", lambda o=out, l=lhsT, r=rhs, s=st, p=sp:
                    self.nc.tensor.matmul(o, l, r, start=s, stop=p))
        bank, out, lhsT, rhs, st, sp = items[-1]
        v = bld.mark("tensor", lambda o=out, l=lhsT, r=rhs, s=st, p=sp:
                     self.nc.tensor.matmul(o, l, r, start=s, stop=p))
        for b in {it[0] for it in items}:
            self.ptrk[b].did_wr("tensor", v)
        for t in reads:
            self._t(t).did_rd("tensor", v)
        return v

    def op(self, eng, fn, rpsum=(), reads=(), writes=()):
        bld = self.bld
        for b in rpsum:
            self.ptrk[b].rd(eng)
        for t in reads:
            self._t(t).rd(eng)
        for t in writes:
            self._t(t).wr(eng)
        v = bld.mark(eng, fn)
        for b in rpsum:
            self.ptrk[b].did_rd(eng, v)
        for t in reads:
            self._t(t).did_rd(eng, v)
        for t in writes:
            self._t(t).did_wr(eng, v)
        return v

    def dma_into(self, queue, trkname, dst_ap, src_ap):
        t = self._t(trkname)
        t.dma_wr_pre(queue)
        v = self.bld.dma(queue, dst_ap, src_ap)
        t.did_wr(("dma", queue), v)
        return v

    def dma_from(self, queue, trkname, dst_ap, src_ap):
        t = self._t(trkname)
        if t.w:
            k, v = t.w[0], t.w[1]
            if isinstance(k, tuple):
                self.bld.dwait(queue, k[1], v)
            else:
                self.bld.wait(queue, k, v)
        v = self.bld.dma(queue, dst_ap, src_ap)
        t.did_rd(("dma", queue), v)
        return v

    def wload(self, src_ap, ncols=4096):
        i = self._wri
        self._wri = (i + 1) % 4
        t = self.wrtrk[i]
        t.dma_wr_pre("gpsimd")
        v = self.bld.dma("gpsimd", self.wrot[i][:, 0:ncols], src_ap)
        t.did_wr(("dma", "gpsimd"), v)
        return i

    def wuse(self, i):
        self.wrtrk[i].rd("tensor")
        return self.wrot[i]

    def wused(self, i, v):
        self.wrtrk[i].did_rd("tensor", v)

    def maybe_dump(self, name, sbuf_ap, trkname):
        if name not in self.dumps:
            return
        self.bld.barrier()
        self.dma_from("gpsimd", trkname, self.dump_aps[name], sbuf_ap)
        self.bld.barrier()

    def emit_init(self):
        nc, bld = self.nc, self.bld
        bld.mark("vector", lambda: nc.vector.memset(self.snkv[:], 0.0))
        bld.mark("scalar", lambda: nc.scalar.memzero(self.snks[:]))
        bld.mark("gpsimd", lambda: nc.gpsimd.memset(self.snkg[:], 0.0))
        self.op("gpsimd", lambda: nc.gpsimd.memset(self.ident[:], 0.0),
                writes=("ident",))
        self.op("gpsimd", lambda: nc.gpsimd.affine_select(
            out=self.ident[:], in_=self.ident[:], compare_op=ALU.not_equal,
            fill=1.0, base=0, pattern=[[-1, 128]], channel_multiplier=1),
            reads=("ident",), writes=("ident",))
        self.op("gpsimd", lambda: nc.gpsimd.memset(self.identf[:], 0.0),
                writes=("onesf",))
        self.op("gpsimd", lambda: nc.gpsimd.affine_select(
            out=self.identf[:], in_=self.identf[:], compare_op=ALU.not_equal,
            fill=1.0, base=0, pattern=[[-1, 128]], channel_multiplier=1),
            reads=("onesf",), writes=("onesf",))
        self.op("gpsimd", lambda: nc.gpsimd.memset(self.harena[0:1, 0:2048], 1.0),
                writes=("kh",))
        self.dma_from("sync", "kh", self.ones_d, self.harena[0:1, 0:2048])
        for m in range(4):
            sl = self.masks[:, m * 512:(m + 1) * 512]
            self.op("gpsimd", lambda sl=sl: nc.gpsimd.memset(sl, 0.0),
                    writes=("masks",))
            self.op("gpsimd", lambda sl=sl, m=m: nc.gpsimd.affine_select(
                out=sl, in_=sl, compare_op=ALU.is_ge, fill=NEG,
                base=128 * m, pattern=[[-1, 512]], channel_multiplier=1),
                reads=("masks",), writes=("masks",))
            s2 = self.masksB[:, m * 512:(m + 1) * 512]
            self.op("gpsimd", lambda s2=s2: nc.gpsimd.memset(s2, 0.0),
                    writes=("pblk",))
            self.op("gpsimd", lambda s2=s2, m=m: nc.gpsimd.affine_select(
                out=s2, in_=s2, compare_op=ALU.is_ge, fill=NEG,
                base=-128 * m, pattern=[[1, 512]], channel_multiplier=-1),
                reads=("pblk",), writes=("pblk",))
        self.dma_from("sync", "masks", self.mask_d[:, 0:2048], self.masks)
        self.dma_from("sync", "pblk", self.mask_d[:, 2048:4096], self.masksB)
        self.op("gpsimd", lambda: nc.gpsimd.iota(
            self.iota, pattern=[[-1, 2048]], base=0, channel_multiplier=1,
            allow_small_or_imprecise_dtypes=True), writes=("iota",))
        self.dma_from("sync", "iota", self.iota_d, self.iota)
        vo = self.op("vector", lambda: nc.vector.memset(self.onesb[:], 1.0),
                     writes=("onesb",))
        bld.mark("vector", lambda: nc.vector.memset(self.consts[:, 0:1], EPS5))
        bld.mark("vector", lambda: nc.vector.memset(self.consts[:, 1:2], EPSR))
        bld.mark("vector", lambda: nc.vector.memset(self.consts[:, 2:3], 0.0))
        bld.mark("vector", lambda: nc.vector.memset(self.onesf[:], 1.0))
        import os as _os
        _vecs = [] if _os.environ.get("SKIP_BCAST") else ["g1", "g2", "ln_g", "ln_b", "lno_g", "lno_b"]
        for i, name in enumerate(_vecs):
            self.dma_into("sync", "t0", self.t0[0:1, 0:D], self.w[name])
            self.op("vector", lambda: nc.vector.tensor_copy(
                self.ntbf[0:1, :], self.t0[0:1, 0:D]), reads=("t0",), writes=("ntbf",))
            for half in range(2):
                b = self.pacq()
                self.mmg([(b, self.pap(b)[:, 0:512], self.onesb[0:1, 0:128],
                           self.ntbf[0:1, half * 512:(half + 1) * 512], True, True)],
                         reads=("ntbf", "onesb"))
                self.op("vector", lambda b=b, i=i, half=half: nc.vector.tensor_copy(
                    self.bcast[:, i * D + half * 512:i * D + (half + 1) * 512],
                    self.pap(b)[:, 0:512]), rpsum=(b,), writes=("bcast",))
        if not _os.environ.get("SKIP_ROPE"):
            self.emit_rope_tables()
            self.dma_from("sync", "tq", self.cs_d, self.cs_t[0:32, :])
            self.dma_from("sync", "tk", self.sn_d, self.sn_t[0:32, :])
        bld.barrier()

    def emit_rope_tables(self):
        nc, bld = self.nc, self.bld
        sm, t0, t1 = self.small, self.t0, self.t1
        self.op("gpsimd", lambda: nc.gpsimd.iota(
            sm[0:32, 0:1], pattern=[[0, 1]], base=0, channel_multiplier=1,
            allow_small_or_imprecise_dtypes=True), writes=("small",))
        self.op("scalar", lambda: nc.scalar.activation(
            sm[0:32, 1:2], sm[0:32, 0:1], AF.Exp,
            scale=-math.log(10000.0) / 32.0), reads=("small",), writes=("small",))
        self.op("gpsimd", lambda: nc.gpsimd.iota(
            t0[0:32, 0:2048], pattern=[[1, 2048]], base=0, channel_multiplier=0,
            allow_small_or_imprecise_dtypes=True), writes=("t0",))
        self.op("vector", lambda: nc.vector.tensor_scalar_mul(
            t0[0:32, 0:2048], t0[0:32, 0:2048], sm[0:32, 1:2]),
            reads=("small", "t0"), writes=("t0",))
        for half in range(4):
            F = t0[0:32, half * 512:(half + 1) * 512]
            self.op("vector", lambda F=F: nc.vector.tensor_scalar_mul(
                t1[0:32, 0:512], F, 1.0 / (2 * PI)), reads=("t0",), writes=("t1",))
            self.op("vector", lambda: nc.vector.tensor_copy(
                t1[0:32, 512:1024].bitcast(I32)[:, 0:512], t1[0:32, 0:512]),
                reads=("t1",), writes=("t1",))
            self.op("vector", lambda: nc.vector.tensor_copy(
                t1[0:32, 0:512], t1[0:32, 512:1024].bitcast(I32)[:, 0:512]),
                reads=("t1",), writes=("t1",))
            self.op("vector", lambda F=F: nc.vector.scalar_tensor_tensor(
                t1[0:32, 512:1024], t1[0:32, 0:512], -2 * PI, F,
                ALU.mult, ALU.add), reads=("t1", "t0"), writes=("t1",))
            self.op("vector", lambda: nc.vector.tensor_scalar(
                t1[0:32, 0:512], t1[0:32, 512:1024], PI, None, ALU.is_gt),
                reads=("t1",), writes=("t1",))
            self.op("vector", lambda F=F: nc.vector.scalar_tensor_tensor(
                F, t1[0:32, 0:512], -2 * PI, t1[0:32, 512:1024],
                ALU.mult, ALU.add), reads=("t1",), writes=("t0",))
            self.op("scalar", lambda F=F, half=half: nc.scalar.activation(
                self.sn_t[0:32, half * 512:(half + 1) * 512], F, AF.Sin),
                reads=("t0",), writes=("tk",))
            self.op("vector", lambda: nc.vector.tensor_scalar_add(
                t1[0:32, 0:512], t1[0:32, 512:1024], PI / 2),
                reads=("t1",), writes=("t1",))
            self.op("vector", lambda: nc.vector.tensor_scalar(
                t1[0:32, 512:1024], t1[0:32, 0:512], PI, None, ALU.is_gt),
                reads=("t1",), writes=("t1",))
            self.op("vector", lambda: nc.vector.scalar_tensor_tensor(
                t1[0:32, 0:512], t1[0:32, 512:1024], -2 * PI, t1[0:32, 0:512],
                ALU.mult, ALU.add), reads=("t1",), writes=("t1",))
            self.op("scalar", lambda half=half: nc.scalar.activation(
                self.cs_t[0:32, half * 512:(half + 1) * 512],
                t1[0:32, 0:512], AF.Sin), reads=("t1",), writes=("tq",))

    def tile_stats(self, u, utrk, eps, rms, store_i=None):
        nc, bld = self.nc, self.bld
        sm = self.small
        if not rms:
            self.op("vector", lambda: nc.vector.reduce_sum(
                sm[:, 0:1], u[:, 0:D], axis=AX.X, negate=True),
                reads=(utrk,), writes=("small",))
            vv = self.op("vector", lambda: nc.vector.tensor_scalar_mul(
                sm[:, 0:1], sm[:, 0:1], 1.0 / D), writes=("small",))
            bld.wait("scalar", "vector", vv)
        self.op("scalar", lambda rms=rms: nc.scalar.activation(
            self.t1[:, 0:D], u[:, 0:D], AF.Square,
            bias=self.consts[:, 2:3] if rms else sm[:, 0:1],
            accum_out=sm[:, 1:2]), reads=(utrk, "small"), writes=("t1", "small"))
        epscol = self.consts[:, 0:1] if eps == EPS5 else self.consts[:, 1:2]
        bld.run("scalar", lambda epscol=epscol: nc.scalar.activation(
            sm[:, 2:3], sm[:, 1:2], AF.Ln, bias=epscol, scale=1.0 / D))
        s2 = bld.mark("scalar", lambda: nc.scalar.activation(
            sm[:, 3:4], sm[:, 2:3], AF.Exp, scale=-0.5))
        self.trk["small"].did_wr("scalar", s2)
        if rms:
            z = self.op("vector", lambda: nc.vector.memset(sm[:, 4:5], 0.0),
                        writes=("small",))
            bld.wait("scalar", "vector", z)
        else:
            bv = self.op("vector", lambda: nc.vector.tensor_mul(
                sm[:, 4:5], sm[:, 0:1], sm[:, 3:4]), reads=("small",), writes=("small",))
            bld.wait("scalar", "vector", bv)
        if store_i is not None:
            self.op("vector", lambda i=store_i: nc.vector.tensor_copy(
                self.stats[:, 2 * i:2 * i + 2], self.small[:, 3:5]),
                reads=("small",), writes=("stats",))

    def transpose_tile(self, i):
        nc = self.nc
        for c in range(NCH):
            b = self.pacq()
            self.trk["ntbf"].rd("tensor")
            v = self.bld.mark("tensor", lambda b=b, c=c: nc.tensor.transpose(
                self.pbf(b)[:, 0:128], self.ntbf[:, c * 128:(c + 1) * 128],
                self.ident[:]))
            self.ptrk[b].did_wr("tensor", v)
            self.trk["ntbf"].did_rd("tensor", v)
            self.op("vector", lambda b=b, c=c, i=i: nc.vector.tensor_copy(
                self.actT[:, c * SEQ + i * 128:c * SEQ + (i + 1) * 128],
                self.pbf(b)[:, 0:128]), rpsum=(b,), writes=("actT",))

    def emit_actT(self, seq, mode, src, router=None):
        nc, bld = self.nc, self.bld
        xb_ = [self.x0, self.x1b]
        xn_ = ["x0", "x1b"]
        if router is not None:
            nE = 4 if router == "l0" else 8
            hf32 = self.harena[:].bitcast(F32)
            gg = hf32[:, 2048:2048 + nE * 1024]
            g2b = hf32[:, 2048 + nE * 1024:2048 + (nE + 1) * 1024]
            ggt = self.trk.setdefault("gg", Trk(bld))
            rows = self.w["gateT"] if router == "l0" else self.w["WrT"]
            if router == "l0":
                self.dma_into("sync", "t1", self.t1[0:1, :], self.w["g2"])
                for half in range(2):
                    b = self.pacq()
                    self.mmg([(b, self.pap(b)[:, :], self.onesf[:],
                               self.t1[0:1, half * 512:(half + 1) * 512],
                               True, True)], reads=("onesf", "t1"))
                    self.op("vector", lambda b=b, half=half: nc.vector.tensor_copy(
                        g2b[:, half * 512:(half + 1) * 512], self.pap(b)[:, :]),
                        rpsum=(b,), writes=(ggt,))
            for e in range(nE):
                self.dma_into("sync", "t1", self.t1[0:1, :], rows[e])
                for half in range(2):
                    b = self.pacq()
                    self.mmg([(b, self.pap(b)[:, :], self.onesf[:],
                               self.t1[0:1, half * 512:(half + 1) * 512],
                               True, True)], reads=("onesf", "t1"))
                    sl = gg[:, e * 1024 + half * 512:e * 1024 + (half + 1) * 512]
                    self.op("vector", lambda b=b, sl=sl: nc.vector.tensor_copy(
                        sl, self.pap(b)[:, :]), rpsum=(b,), writes=(ggt,))
                if router == "l0":
                    self.op("vector", lambda e=e: nc.vector.tensor_mul(
                        gg[:, e * 1024:(e + 1) * 1024],
                        gg[:, e * 1024:(e + 1) * 1024], g2b),
                        reads=(ggt,), writes=(ggt,))
        for i in range(TT):
            xb, xnm = xb_[i % 2], xn_[i % 2]
            self.dma_into("sync", xnm, xb[:], src[i * 128:(i + 1) * 128, :])
            if mode == "ln":
                self.tile_stats(xb, xnm, EPS5, rms=False, store_i=i)
                self.op("scalar", lambda xb=xb: nc.scalar.activation(
                    self.ntbf[:], xb[:], AF.Identity, scale=self.small[:, 3:4],
                    bias=self.small[:, 4:5]), reads=(xnm, "small"), writes=("ntbf",))
            elif mode == "ln_affine":
                self.tile_stats(xb, xnm, EPS5, rms=False, store_i=i)
                self.op("scalar", lambda xb=xb: nc.scalar.activation(
                    self.t0[:, 0:D], xb[:], AF.Identity, scale=self.small[:, 3:4],
                    bias=self.small[:, 4:5]), reads=(xnm, "small"), writes=("t0",))
                self.tile_stats(self.t0, "t0", EPS5, rms=False)
                self.op("scalar", lambda: nc.scalar.activation(
                    self.t0[:, D:2 * D], self.t0[:, 0:D], AF.Identity,
                    scale=self.small[:, 3:4], bias=self.small[:, 4:5]),
                    reads=("t0", "small"), writes=("t0",))
                self.op("vector", lambda: nc.vector.tensor_mul(
                    self.t0[:, D:2 * D], self.t0[:, D:2 * D],
                    self.bcast[:, 2 * D:3 * D]), reads=("bcast", "t0"), writes=("t0",))
                self.op("vector", lambda: nc.vector.tensor_add(
                    self.ntbf[:], self.t0[:, D:2 * D], self.bcast[:, 3 * D:4 * D]),
                    reads=("bcast", "t0"), writes=("ntbf",))
            elif mode == "recompute_ln":
                self.trk["stats"].rd("scalar")
                if router is None:
                    self.op("scalar", lambda xb=xb, i=i: nc.scalar.activation(
                        self.ntbf[:], xb[:], AF.Identity,
                        scale=self.stats[:, 2 * i:2 * i + 1],
                        bias=self.stats[:, 2 * i + 1:2 * i + 2]),
                        reads=(xnm, "stats"), writes=("ntbf",))
                else:
                    self.op("scalar", lambda xb=xb, i=i: nc.scalar.activation(
                        self.t0[:, 0:D], xb[:], AF.Identity,
                        scale=self.stats[:, 2 * i:2 * i + 1],
                        bias=self.stats[:, 2 * i + 1:2 * i + 2]),
                        reads=(xnm, "stats"), writes=("t0",))
                    self.op("vector", lambda: nc.vector.tensor_copy(
                        self.ntbf[:], self.t0[:, 0:D]), reads=("t0",),
                        writes=("ntbf",))
            elif mode in ("rms1", "rms2"):
                self.trk["stats"].rd("scalar")
                self.op("scalar", lambda xb=xb, i=i: nc.scalar.activation(
                    self.t0[:, 0:D], xb[:], AF.Identity,
                    scale=self.stats[:, 2 * i:2 * i + 1],
                    bias=self.stats[:, 2 * i + 1:2 * i + 2]),
                    reads=(xnm, "stats"), writes=("t0",))
                if mode == "rms2":
                    self.op("vector", lambda i=i: nc.vector.tensor_add(
                        self.t0[:, 0:D], self.t0[:, 0:D],
                        self.sE[:, i * D:(i + 1) * D]),
                        reads=("sE", "t0"), writes=("t0",))
                self.tile_stats(self.t0, "t0", EPSR, rms=True)
                gofs = 0 if mode == "rms1" else D
                self.op("scalar", lambda: nc.scalar.activation(
                    self.t0[:, D:2 * D], self.t0[:, 0:D], AF.Identity,
                    scale=self.small[:, 3:4]), reads=("t0", "small"), writes=("t0",))
                self.op("vector", lambda gofs=gofs: nc.vector.tensor_mul(
                    self.ntbf[:], self.t0[:, D:2 * D], self.bcast[:, gofs:gofs + D]),
                    reads=("bcast", "t0"), writes=("ntbf",))
            else:
                raise ValueError(mode)
            if router is not None:
                u = self.t0[:, D:2 * D] if mode == "rms2" else self.t0[:, 0:D]
                for e in range(nE):
                    self.op("vector", lambda u=u, e=e, i=i:
                            nc.vector.scalar_tensor_tensor(
                                self.t1[:, 0:D], u, 1.0,
                                gg[:, e * 1024:(e + 1) * 1024],
                                ALU.mult, ALU.mult,
                                accum_out=self.router[:, i * 16 + e:i * 16 + e + 1]),
                            reads=("t0", ggt), writes=("t1", "router"))
            self.transpose_tile(i)
        bld.barrier()

    def emit_attention(self, seq, kind):
        nc, bld = self.nc, self.bld
        H = 16
        self.dma_into("sync", "masks", self.masks, self.mask_d[:, 0:2048])
        self.dma_into("sync", "pblk", self.masksB, self.mask_d[:, 2048:4096])
        KB = 70 if kind == "l0" else 67
        hbf = self.harena
        if kind == "l0":
            for r in range(3):
                self.dma_into("gpsimd", "kh",
                              hbf[67 + r:68 + r, self.KH0:self.KH0 + 2048],
                              self.ones_d)
                self.dma_into("gpsimd", "qh",
                              hbf[64 + r:65 + r, self.QH0:self.QH0 + 2048],
                              self.ones_d)
        else:
            for r in range(3):
                self.dma_into("gpsimd", "kh",
                              hbf[64 + r:65 + r, self.KH0:self.KH0 + 2048],
                              self.ones_d)
        if kind == "l0":
            qsrc, ksrc, vsrc = self.w["WqH"], self.w["WkH"], self.w["WvH"]
            qscale, kscale = 0.125, None
            self.dma_into("sync", "iota", self.iota, self.iota_d)
        elif kind == "l0i":
            qsrc, ksrc, vsrc = self.w["fwqH"], self.w["fwkH"], self.w["fwvH"]
            qscale, kscale = 0.125, None
            self.dma_into("sync", "tq", self.cs_t[0:32, :], self.cs_d)
            self.dma_into("sync", "tk", self.sn_t[0:32, :], self.sn_d)
        else:
            qsrc = self.w["l1wqH"]
            sqs = math.sqrt(64.0 ** -0.5)
            qscale, kscale = (64.0 ** -0.5) * sqs, sqs

        if self.attn_stage == "init":
            bld.barrier()
            return
        if kind == "l0":
            self.dma_into("gpsimd", "lA", self.lA[:, 0:D], self.w["AqT"])
            self.dma_into("gpsimd", "lA", self.lA[:, D:2 * D], self.w["AkT"])
            self.dma_into("gpsimd", "lA", self.lA[:, 2 * D:3 * D], self.w["AvT"])
            self.dma_into("gpsimd", "lB", self.lB[:, 0:64], self.w["BqT"])
            self.dma_into("gpsimd", "lB", self.lB[:, 64:128], self.w["BkT"])
            self.dma_into("gpsimd", "lB", self.lB[:, 128:192], self.w["BvT"])
            if self.attn_stage == "lora_dma":
                bld.barrier()
                return
            if self.attn_stage.startswith("lora_mm"):
                import os as _os
                var = self.attn_stage
                b = self.pacq()
                if var == "lora_mm_bare":
                    self.mmg([(b, self.pap(b)[0:8, :], self.lB[:, 0:8],
                               self.actT[:, 0:512], True, True)],
                             reads=("lB", "actT"))
                    bld.barrier()
                    return
                if var == "lora_mm_nt2":
                    self.mmg([(b, self.pap(b)[0:8, :], self.ntbf[:, 0:8],
                               self.actT[:, 0:512], True, True)],
                             reads=("ntbf", "actT"))
                    bld.barrier()
                    return
                if var == "lora_mm_rhs":
                    self.mmg([(b, self.pap(b)[0:8, :], self.lB[:, 0:8],
                               self.ntbf[:, 0:512], True, True)],
                             reads=("lB", "ntbf"))
                    bld.barrier()
                    return
                nmm = 1 if var == "lora_mm1" else NCH
                items = [(b, self.pap(b)[0:8, :],
                          self.lB[:, c * 8:(c + 1) * 8],
                          self.actT[:, c * SEQ + 0:c * SEQ + 512],
                          c == 0, c == nmm - 1) for c in range(nmm)]
                self.mmg(items, reads=("lB", "actT"))
                if var == "lora_mm_v":
                    self.op("vector", lambda b=b: nc.vector.tensor_copy(
                        self.tq[0:8, 0:512], self.pap(b)[0:8, :]),
                        rpsum=(b,), writes=("tq",))
                elif var == "lora_mm_nt":
                    self.op("scalar", lambda b=b: nc.scalar.activation(
                        self.ntbf[0:8, 0:512], self.pap(b)[0:8, :],
                        AF.Copy, scale=2.0), rpsum=(b,), writes=("ntbf",))
                else:
                    self.op("scalar", lambda b=b: nc.scalar.activation(
                        self.tq[0:8, 0:512], self.pap(b)[0:8, :],
                        AF.Copy, scale=2.0), rpsum=(b,), writes=("tq",))
                bld.barrier()
                return
            for ti, (tbuf, tn, bofs) in enumerate(
                    [(self.tq, "tq", 0), (self.tk, "tk", 64), (self.tv, "tv", 128)]):
                for tb in range(4):
                    b = self.pacq()
                    items = [(b, self.pap(b)[0:8, :],
                              self.lB[:, bofs + c * 8:bofs + (c + 1) * 8],
                              self.actT[:, c * SEQ + tb * 512:c * SEQ + (tb + 1) * 512],
                              c == 0, c == NCH - 1) for c in range(NCH)]
                    self.mmg(items, reads=("lB", "actT"))
                    self.op("scalar", lambda b=b, tbuf=tbuf, tb=tb: nc.scalar.activation(
                        tbuf[0:8, tb * 512:(tb + 1) * 512], self.pap(b)[0:8, :],
                        AF.Copy, scale=2.0), rpsum=(b,), writes=(tn,))

        if self.attn_stage == "lora":
            bld.barrier()
            return
        slopes = [2.0 ** (-0.5 * (h + 1)) for h in range(H)]

        for h in range(self.nheads):
            newkv = (kind == "l0") or (kind == "l0i" and h % 4 == 0) or \
                    (kind == "l1" and h == 0)
            if newkv:
                kv = {"l0": h, "l0i": h // 4, "l1": 0}[kind]
                if kind == "l1":
                    wik = self.wload(self.w["l1wkvT"], ncols=NCH * 128)
                    wiv = wik
                    ksl = lambda wt, c: wt[:, c * 128:c * 128 + 64]
                    vsl = lambda wt, c: wt[:, c * 128 + 64:c * 128 + 128]
                else:
                    wik = self.wload(ksrc[kv], ncols=512)
                    wiv = self.wload(vsrc[kv], ncols=512)
                    ksl = lambda wt, c: wt[:, c * 64:(c + 1) * 64]
                    vsl = ksl
                wkt = self.wuse(wik)
                for tb in range(4):
                    b = self.pacq()
                    items = [(b, self.pap(b)[0:64, :], ksl(wkt, c),
                              self.actT[:, c * SEQ + tb * 512:c * SEQ + (tb + 1) * 512],
                              c == 0, False) for c in range(NCH)]
                    if kind == "l0":
                        items.append((b, self.pap(b)[0:64, :],
                                      self.lA[:, D + h * 64:D + (h + 1) * 64],
                                      self.tk[:, tb * 512:(tb + 1) * 512],
                                      False, True))
                    else:
                        items[-1] = items[-1][:5] + (True,)
                    v = self.mmg(items, reads=("actT", "lA", "tk"))
                    self.wused(wik, v)
                    if kscale is None:
                        self.op("vector", lambda b=b, tb=tb: nc.vector.tensor_copy(
                            self.kh[0:64, tb * 512:(tb + 1) * 512],
                            self.pap(b)[0:64, :]), rpsum=(b,), writes=("kh",))
                    else:
                        self.op("scalar", lambda b=b, tb=tb: nc.scalar.activation(
                            self.kh[0:64, tb * 512:(tb + 1) * 512],
                            self.pap(b)[0:64, :], AF.Copy, scale=kscale),
                            rpsum=(b,), writes=("kh",))
                wvt = self.wuse(wiv)
                for tt in range(TT):
                    b = self.pacq()
                    items = [(b, self.pap(b)[:, 0:64],
                              self.actT[:, c * SEQ + tt * 128:c * SEQ + (tt + 1) * 128],
                              vsl(wvt, c), c == 0, False) for c in range(NCH)]
                    if kind == "l0":
                        items.append((b, self.pap(b)[:, 0:64],
                                      self.tv[:, tt * 128:(tt + 1) * 128],
                                      self.lA[:, 2 * D + h * 64:2 * D + (h + 1) * 64],
                                      False, True))
                    else:
                        items[-1] = items[-1][:5] + (True,)
                    v = self.mmg(items, reads=("actT", "lA", "tv"))
                    self.wused(wiv, v)
                    self.op("vector", lambda b=b, tt=tt: nc.vector.tensor_copy(
                        self.vh[:, tt * 64:(tt + 1) * 64], self.pap(b)[:, 0:64]),
                        rpsum=(b,), writes=("vh",))
                if kind == "l0i":
                    self.rope("kh")
            wiq = self.wload(qsrc[h], ncols=512)
            wqt = self.wuse(wiq)
            for tb in range(4):
                b = self.pacq()
                items = [(b, self.pap(b)[0:64, :],
                          wqt[:, c * 64:(c + 1) * 64],
                          self.actT[:, c * SEQ + tb * 512:c * SEQ + (tb + 1) * 512],
                          c == 0, False) for c in range(NCH)]
                if kind == "l0":
                    items.append((b, self.pap(b)[0:64, :],
                                  self.lA[:, h * 64:(h + 1) * 64],
                                  self.tq[:, tb * 512:(tb + 1) * 512], False, True))
                else:
                    items[-1] = items[-1][:5] + (True,)
                v = self.mmg(items, reads=("actT", "lA", "tq"))
                self.wused(wiq, v)
                self.op("scalar", lambda b=b, tb=tb: nc.scalar.activation(
                    self.qh[0:64, tb * 512:(tb + 1) * 512], self.pap(b)[0:64, :],
                    AF.Copy, scale=qscale), rpsum=(b,), writes=("qh",))
            if kind == "l0i":
                self.rope("qh")
            if self.attn_stage == "proj":
                continue

            hbf = self.harena
            sl0 = self.harena[:].bitcast(F32)[0:1, 0:2048]
            if kind == "l0":
                slope = slopes[h]
                self.op("scalar", lambda slope=slope: nc.scalar.activation(
                    self.t0[0:1, 0:2048], self.iota[0:1, :], AF.Identity,
                    scale=slope), reads=("iota",), writes=("t0",))
                self.op("vector", lambda: nc.vector.tensor_copy(
                    self.ptblk[0:1, :], self.t0[0:1, 0:2048]),
                    reads=("t0",), writes=("ptblk",))
                self.dma_from("gpsimd", "ptblk",
                              self.harena[64:65, self.KH0:self.KH0 + 2048].bitcast(F32).bitcast(BF16)
                              if False else hbf[64:65, self.KH0:self.KH0 + 2048],
                              self.ptblk[0:1, :])
                self.op("vector", lambda: nc.vector.tensor_sub(
                    sl0, self.t0[0:1, 0:2048], self.ptblk[0:1, :]),
                    reads=("t0", "ptblk"), writes=("sblk",))
                self.op("vector", lambda: nc.vector.tensor_copy(
                    self.ptblk[0:1, :], sl0), reads=("sblk",), writes=("ptblk",))
                self.dma_from("gpsimd", "ptblk",
                              hbf[65:66, self.KH0:self.KH0 + 2048], self.ptblk[0:1, :])
                self.op("vector", lambda: nc.vector.tensor_sub(
                    self.t0[0:1, 0:2048], sl0, self.ptblk[0:1, :]),
                    reads=("sblk", "ptblk"), writes=("t0",))
                self.op("vector", lambda: nc.vector.tensor_copy(
                    self.ptblk[0:1, :], self.t0[0:1, 0:2048]),
                    reads=("t0",), writes=("ptblk",))
                self.dma_from("gpsimd", "ptblk",
                              hbf[66:67, self.KH0:self.KH0 + 2048], self.ptblk[0:1, :])
                self.trk["kh"].did_wr(("dma", "gpsimd"), self.bld.dcnt["gpsimd"])

            KA = 67 if kind == "l0" else 64
            sm = self.small
            for qt in range(TT):
                nk512 = qt // 4 + 1
                pt = self.strk[qt % 2]
                c0 = 32 + (qt % 2) * 16
                ksp, mm_ = qt // 4, qt % 4
                w = (mm_ + 1) * 128
                banks = [self.pacq() for _ in range(nk512)]
                items = [(banks[ks], self.pap(banks[ks])[:, 0:(w if ks == ksp else 512)],
                          self.qh[0:KA, qt * 128:(qt + 1) * 128],
                          self.kh[0:KA, ks * 512:ks * 512 + (w if ks == ksp else 512)],
                          True, True)
                         for ks in range(nk512)]
                self.mmg(items, reads=("qh", "kh"))
                b = banks[ksp]
                self.op("vector", lambda b=b, mm_=mm_, w=w: nc.vector.tensor_add(
                    self.pap(b)[:, w - 128:w], self.pap(b)[:, w - 128:w],
                    self.masks[:, mm_ * 512 + w - 128:mm_ * 512 + w]),
                    rpsum=(b,), reads=("masks",), writes=())
                self.ptrk[b].did_wr("vector", self.bld.cnt["vector"])
                for ks in range(nk512):
                    dst = (sm[:, c0 + 4:c0 + 5] if nk512 == 1
                           else sm[:, c0 + ks:c0 + 1 + ks])
                    wk = w if ks == ksp else 512
                    self.op("vector", lambda b=banks[ks], dst=dst, wk=wk:
                            nc.vector.reduce_max(dst,
                                                 self.pap(b)[:, 0:wk], axis=AX.X,
                                                 negate=True),
                            rpsum=(banks[ks],), writes=(pt,))
                if nk512 > 1:
                    self.op("vector", lambda n=nk512, c0=c0: nc.vector.tensor_reduce(
                        sm[:, c0 + 4:c0 + 5], sm[:, c0:c0 + n], axis=AX.X,
                        op=ALU.min), reads=(pt,), writes=(pt,))
                for ks in range(nk512):
                    b = banks[ks]
                    acc = (sm[:, c0 + 9:c0 + 10] if nk512 == 1
                           else sm[:, c0 + 5 + ks:c0 + 6 + ks])
                    wk = w if ks == ksp else 512
                    ev = self.op("scalar", lambda b=b, acc=acc, c0=c0, wk=wk:
                                 nc.scalar.activation(
                                     self.pap(b)[:, 0:wk], self.pap(b)[:, 0:wk],
                                     AF.Exp,
                                     bias=sm[:, c0 + 4:c0 + 5],
                                     accum_out=acc),
                                 rpsum=(b,), reads=(pt,), writes=(pt,))
                    self.ptrk[b].did_wr("scalar", ev)
                if nk512 > 1:
                    self.op("vector", lambda n=nk512, c0=c0: nc.vector.tensor_reduce(
                        sm[:, c0 + 9:c0 + 10], sm[:, c0 + 5:c0 + 5 + n], axis=AX.X,
                        op=ALU.add), reads=(pt,), writes=(pt,))
                lv = self.op("scalar", lambda c0=c0: nc.scalar.activation(
                    sm[:, c0 + 10:c0 + 11], sm[:, c0 + 9:c0 + 10], AF.Ln),
                    reads=(pt,), writes=(pt,))
                self.op("vector", lambda qt=qt, c0=c0: nc.vector.tensor_sub(
                    sm[:, 8 + qt:9 + qt], sm[:, c0 + 4:c0 + 5],
                    sm[:, c0 + 10:c0 + 11]),
                    reads=(pt,), writes=("smallM",))
            if self.attn_stage == "passA":
                continue

            bt = self.pacq()
            self.trk["smallM"].rd("tensor")
            vtr = bld.mark("tensor", lambda bt=bt: nc.tensor.transpose(
                self.pap(bt)[0:16, 0:128], sm[:, 8:24], self.identf[:]))
            self.ptrk[bt].did_wr("tensor", vtr)
            self.trk["smallM"].did_rd("tensor", vtr)
            t1b = self.t1[:].bitcast(BF16)
            self.op("vector", lambda bt=bt: nc.vector.tensor_copy(
                self.t1[0:16, 0:128], self.pap(bt)[0:16, 0:128]),
                rpsum=(bt,), writes=("t1",))
            self.op("vector", lambda: nc.vector.tensor_copy(
                t1b[0:16, 1024:1152], self.t1[0:16, 0:128]),
                reads=("t1",), writes=("t1",))
            self.op("vector", lambda: nc.vector.tensor_sub(
                self.t1[0:16, 128:256], self.t1[0:16, 0:128], t1b[0:16, 1024:1152]),
                reads=("t1",), writes=("t1",))
            self.op("vector", lambda: nc.vector.tensor_copy(
                t1b[0:16, 1152:1280], self.t1[0:16, 128:256]),
                reads=("t1",), writes=("t1",))
            self.op("vector", lambda: nc.vector.tensor_sub(
                self.t1[0:16, 256:384], self.t1[0:16, 128:256], t1b[0:16, 1152:1280]),
                reads=("t1",), writes=("t1",))
            self.op("vector", lambda: nc.vector.tensor_copy(
                t1b[0:16, 1280:1408], self.t1[0:16, 256:384]),
                reads=("t1",), writes=("t1",))
            qrow0 = 67 if kind == "l0" else 64
            for r, c0 in ((0, 1024), (1, 1152), (2, 1280)):
                self.dma_from("gpsimd", "t1",
                              hbf[qrow0 + r:qrow0 + r + 1, self.QH0:self.QH0 + 2048],
                              t1b[0:16, c0:c0 + 128])
            self.trk["qh"].did_wr(("dma", "gpsimd"), self.bld.dcnt["gpsimd"])

            for qb in range(4):
                kmax = 4 * (qb + 1)
                bo = self.pacq()
                prev = None
                for kc in range(kmax):
                    b = self.pacq(avoid=bo)
                    i = kc - 4 * qb
                    i0 = max(i, 0) * 128
                    self.mmg([(b, self.pap(b)[:, 0:512 - i0],
                               self.kh[0:KB, kc * 128:(kc + 1) * 128],
                               self.qh[0:KB, qb * 512 + i0:(qb + 1) * 512],
                               True, True)], reads=("qh", "kh"))
                    if i >= 0:
                        self.op("vector", lambda b=b, i=i, i0=i0: nc.vector.tensor_add(
                            self.pap(b)[:, 0:128], self.pap(b)[:, 0:128],
                            self.masksB[:, i * 512 + i0:i * 512 + i0 + 128]),
                            rpsum=(b,), reads=("pblk",), writes=())
                        self.ptrk[b].did_wr("vector", self.bld.cnt["vector"])
                    pts = self.ptblk[:, (kc % 4) * 512:(kc % 4 + 1) * 512]
                    if i0 > 0:
                        self.op("gpsimd", lambda pts=pts, i0=i0: nc.gpsimd.memset(
                            pts[:, 0:i0], 0.0), writes=("ptblk",))
                    ev = self.op("scalar", lambda b=b, pts=pts, i0=i0:
                                 nc.scalar.activation(
                                     pts[:, i0:512], self.pap(b)[:, 0:512 - i0],
                                     AF.Exp), rpsum=(b,),
                                 writes=("ptblk",))
                    if prev is not None:
                        self.mmg([prev], reads=("vh", "ptblk"))
                    prev = (bo, self.pap(bo)[0:64, :],
                            self.vh[:, kc * 64:(kc + 1) * 64],
                            pts, kc == 0, kc == kmax - 1)
                self.mmg([prev], reads=("vh", "ptblk"))
                self.op("vector", lambda b=bo, h=h, qb=qb: nc.vector.tensor_copy(
                    self.oh[(h % 2) * 64:(h % 2) * 64 + 64,
                            qb * 512:(qb + 1) * 512],
                    self.pap(b)[0:64, :]), rpsum=(bo,), writes=("oh",))
            if h % 2 == 1 and self.attn_stage == "full":
                self.dma_from("sync", "oh", self.oscr[h // 2], self.oh[:])

        if not self.outproj:
            bld.barrier()
            return
        wname = {"l0": "WpT", "l0i": "fwo", "l1": "l1woT"}[kind]
        self.dma_into("gpsimd", "actT", self.wA, self.w[wname])
        groups = [[qt] for qt in range(TT)]
        for grp in groups:
            g0, gl = grp[0], len(grp)
            for pair in range(8):
                self.dma_into("sync", "opr",
                              self.opr[:, pair * 128:pair * 128 + gl * 128],
                              self.oscr[pair][:, g0 * 128:(g0 + gl) * 128])
            for qt in grp:
                for do in range(2):
                    b = self.pacq()
                    items = [(b, self.pap(b)[:, :],
                              self.opr[:, pair * 128 + (qt - g0) * 128:
                                        pair * 128 + (qt - g0 + 1) * 128],
                              self.wA[:, pair * D + do * 512:pair * D + do * 512 + 512],
                              pair == 0, pair == 7) for pair in range(8)]
                    self.mmg(items, reads=("opr", "actT"))
                    if kind == "l0":
                        self.op("scalar", lambda b=b, qt=qt, do=do: nc.scalar.activation(
                            self.sS[:, qt * D + do * 512:qt * D + (do + 1) * 512],
                            self.pap(b)[:, :], AF.Copy), rpsum=(b,), writes=("sS",))
                    elif kind == "l0i":
                        self.op("scalar", lambda b=b, qt=qt, do=do: nc.scalar.activation(
                            self.sE[:, qt * D + do * 512:qt * D + (do + 1) * 512],
                            self.pap(b)[:, :], AF.Copy), rpsum=(b,), writes=("sE",))
                    else:
                        self.op("vector", lambda b=b, do=do: nc.vector.tensor_copy(
                            self.t0[:, do * 512:(do + 1) * 512], self.pap(b)[:, :]),
                            rpsum=(b,), writes=("t0",))
                if kind == "l1":
                    self.tile_stats(self.t0, "t0", EPS5, rms=False)
                    self.op("scalar", lambda: nc.scalar.activation(
                        self.t1[:, 0:D], self.t0[:, 0:D], AF.Identity,
                        scale=self.small[:, 3:4], bias=self.small[:, 4:5]),
                        reads=("t0", "small"), writes=("t1",))
                    self.op("vector", lambda: nc.vector.tensor_mul(
                        self.t1[:, 0:D], self.t1[:, 0:D], self.bcast[:, 4 * D:5 * D]),
                        reads=("bcast", "t1"), writes=("t1",))
                    self.op("vector", lambda qt=qt: nc.vector.tensor_add(
                        self.sS[:, qt * D:(qt + 1) * D], self.t1[:, 0:D],
                        self.bcast[:, 5 * D:6 * D]),
                        reads=("bcast", "t1"), writes=("sS",))
        bld.barrier()

    def rope(self, bufname):
        nc = self.nc
        buf = getattr(self, bufname)
        hbf = self.harena
        x2c = self.ptblk[0:32, :]
        p2 = hbf[0:32, 0:2048]
        p3 = hbf[0:32, 2048:4096]
        p4 = self.t0[:].bitcast(BF16)[0:32, 0:2048]
        cs, sn = self.cs_t[0:32, :], self.sn_t[0:32, :]
        self.op("vector", lambda: nc.vector.tensor_copy(x2c, buf[32:64, :]),
                reads=(bufname,), writes=("ptblk",))
        self.op("vector", lambda: nc.vector.tensor_mul(p2, x2c, sn),
                reads=("ptblk", "tk"), writes=("sblk",))
        self.op("vector", lambda: nc.vector.tensor_mul(p3, x2c, cs),
                reads=("ptblk", "tq"), writes=("sblk",))
        self.op("vector", lambda: nc.vector.tensor_mul(p4, buf[0:32, :], sn),
                reads=(bufname, "tk"), writes=("t0",))
        self.op("vector", lambda: nc.vector.tensor_mul(x2c, buf[0:32, :], cs),
                reads=(bufname, "tq"), writes=("ptblk",))
        self.op("vector", lambda: nc.vector.tensor_sub(buf[0:32, :], x2c, p2),
                reads=("ptblk", "sblk"), writes=(bufname,))
        self.op("vector", lambda: nc.vector.tensor_add(buf[32:64, :], p3, p4),
                reads=("sblk", "t0"), writes=(bufname,))

    def emit_router(self, kind):
        nc, bld = self.nc, self.bld
        nE = 4 if kind == "l0" else 8
        for i in range(TT):
            base = i * 16
            pr = self.router[:, base:base + 8]
            cb = self.router[:, base + 8:base + 16]
            sm = self.small
            mv = self.op("vector", lambda pr=pr, nE=nE: nc.vector.reduce_max(
                sm[:, 0:1], pr[:, 0:nE], axis=AX.X, negate=True),
                reads=("router",), writes=("small",))
            bld.wait("scalar", "vector", mv)
            ev = self.op("scalar", lambda nE=nE, pr=pr: nc.scalar.activation(
                pr[:, 0:nE], pr[:, 0:nE], AF.Exp,
                bias=sm[:, 0:1], accum_out=sm[:, 1:2]),
                reads=("router", "small"), writes=("router", "small"))
            bld.wait("vector", "scalar", ev)
            self.op("vector", lambda: nc.vector.reciprocal(
                sm[:, 2:3], sm[:, 1:2]), reads=("small",), writes=("small",))
            self.op("vector", lambda pr=pr, nE=nE: nc.vector.tensor_scalar_mul(
                pr[:, 0:nE], pr[:, 0:nE], sm[:, 2:3]),
                reads=("small", "router"), writes=("router",))
            if kind == "l0":
                self.op("vector", lambda pr=pr: nc.vector.reduce_max(
                    sm[:, 0:1], pr[:, 0:4], axis=AX.X), reads=("router",),
                    writes=("small",))
                self.op("vector", lambda pr=pr: nc.vector.tensor_scalar(
                    sm[:, 8:12], pr[:, 0:4], sm[:, 0:1], None, ALU.is_ge),
                    reads=("router", "small"), writes=("small",))
                self.cumsum_mask(sm, 8, 4, 16)
                self.op("vector", lambda pr=pr: nc.vector.scalar_tensor_tensor(
                    sm[:, 20:24], sm[:, 16:20], -1e38, pr[:, 0:4], ALU.mult, ALU.add),
                    reads=("router",), writes=("small",))
                self.op("vector", lambda: nc.vector.reduce_max(
                    sm[:, 1:2], sm[:, 20:24], axis=AX.X), writes=("small",))
                self.op("vector", lambda: nc.vector.tensor_scalar(
                    sm[:, 8:12], sm[:, 20:24], sm[:, 1:2], None, ALU.is_ge),
                    writes=("small",))
                self.cumsum_mask(sm, 8, 4, 24)
                self.op("vector", lambda: nc.vector.tensor_add(
                    sm[:, 16:20], sm[:, 16:20], sm[:, 24:28]), writes=("small",))
                self.op("vector", lambda: nc.vector.tensor_add(
                    sm[:, 2:3], sm[:, 0:1], sm[:, 1:2]), writes=("small",))
                self.op("vector", lambda: nc.vector.reciprocal(
                    sm[:, 3:4], sm[:, 2:3]), writes=("small",))
                self.op("vector", lambda pr=pr: nc.vector.tensor_mul(
                    sm[:, 20:24], pr[:, 0:4], sm[:, 16:20]),
                    reads=("router",), writes=("small",))
                self.op("vector", lambda cb=cb: nc.vector.tensor_scalar_mul(
                    cb[:, 0:4], sm[:, 20:24], sm[:, 3:4]),
                    reads=("small",), writes=("router",))
            else:
                self.op("vector", lambda pr=pr: nc.vector.reduce_max(
                    sm[:, 0:1], pr[:, 0:8], axis=AX.X), reads=("router",),
                    writes=("small",))
                self.op("vector", lambda pr=pr: nc.vector.tensor_scalar(
                    sm[:, 8:16], pr[:, 0:8], sm[:, 0:1], None, ALU.is_ge),
                    reads=("router", "small"), writes=("small",))
                self.cumsum_mask(sm, 8, 8, 16)
                self.op("vector", lambda cb=cb: nc.vector.tensor_scalar(
                    cb[:, 0:8], sm[:, 16:24], sm[:, 0:1], 2.0, ALU.mult, ALU.mult),
                    reads=("small",), writes=("router",))
                self.op("vector", lambda cb=cb: nc.vector.memset(cb[:, 7:8], 1.0),
                        writes=("router",))

    def cumsum_mask(self, sm, src, n, dst):
        nc = self.nc
        self.op("vector", lambda: nc.vector.tensor_copy(
            sm[:, 32:32 + n], sm[:, src:src + n]), writes=("small",))
        cur = 32
        oth = 40
        sh = 1
        while sh < n:
            self.op("vector", lambda cur=cur, oth=oth, sh=sh: nc.vector.tensor_copy(
                sm[:, oth:oth + sh], sm[:, cur:cur + sh]), writes=("small",))
            self.op("vector", lambda cur=cur, oth=oth, sh=sh: nc.vector.tensor_add(
                sm[:, oth + sh:oth + n], sm[:, cur + sh:cur + n], sm[:, cur:cur + n - sh]),
                writes=("small",))
            cur, oth = oth, cur
            sh *= 2
        self.op("vector", lambda cur=cur: nc.vector.tensor_scalar(
            sm[:, 48:48 + n], sm[:, cur:cur + n], 1.0, None, ALU.is_equal),
            writes=("small",))
        self.op("vector", lambda dst=dst: nc.vector.tensor_mul(
            sm[:, dst:dst + n], sm[:, src:src + n], sm[:, 48:48 + n]),
            writes=("small",))

    def emit_moe(self, seq, kind):
        nc, bld = self.nc, self.bld
        self.emit_router(kind)
        if kind == "l0":
            nE, HC, NG = 4, 24, 12
            wup, wgt, wdn = self.w["w1C"], self.w["w3C"], self.w["w2C"]
        else:
            nE, HC, NG = 8, 32, 16
            wup, wgt, wdn = self.w["WgC"], self.w["WuC"], self.w["WdC"]
            z = self.op("gpsimd", lambda: nc.gpsimd.memset(self.sE[:], 0.0),
                        writes=("sE",))
        ones_col = self.small[:, 63:64]
        oz = self.op("vector", lambda: nc.vector.memset(ones_col, 1.0),
                     writes=("small",))
        for e in range(nE):
            for blk in range(2):
                for g in range(NG):
                    wi1 = self.wload(wup[e, g], ncols=2048)
                    wi3 = self.wload(wgt[e, g], ncols=2048)
                    w1t = self.wrot[wi1]
                    w3t = self.wrot[wi3]
                    for hcl in range(2):
                        hc = g * 2 + hcl
                        for half in range(2):
                            tok0 = blk * 1024 + half * 512
                            bA = self.pacq()
                            self.wrtrk[wi1].rd("tensor")
                            itemsA = [(bA, self.pap(bA)[:, :],
                                       w1t[:, hcl * 1024 + c * 128:hcl * 1024 + (c + 1) * 128],
                                       self.actT[:, c * SEQ + tok0:c * SEQ + tok0 + 512],
                                       c == 0, c == NCH - 1) for c in range(NCH)]
                            vA = self.mmg(itemsA, reads=("actT",))
                            self.wused(wi1, vA)
                            bB = self.pacq()
                            self.wrtrk[wi3].rd("tensor")
                            itemsB = [(bB, self.pap(bB)[:, :],
                                       w3t[:, hcl * 1024 + c * 128:hcl * 1024 + (c + 1) * 128],
                                       self.actT[:, c * SEQ + tok0:c * SEQ + tok0 + 512],
                                       c == 0, c == NCH - 1) for c in range(NCH)]
                            vB = self.mmg(itemsB, reads=("actT",))
                            self.wused(wi3, vB)
                            sv = self.op("scalar", lambda bA=bA: nc.scalar.activation(
                                self.t1[:, 0:512], self.pap(bA)[:, :], AF.Silu),
                                rpsum=(bA,), writes=("t1",))
                            bld.wait("vector", "scalar", sv)
                            self.op("vector", lambda bB=bB, hc=hc, half=half:
                                    nc.vector.tensor_mul(
                                        self.harena[:, hc * 1024 + half * 512:
                                                    hc * 1024 + (half + 1) * 512],
                                        self.t1[:, 0:512], self.pap(bB)[:, :]),
                                    rpsum=(bB,), reads=("t1",), writes=("harena",))
                for tokblk in range(2):
                    tok0 = blk * 1024 + tokblk * 512
                    tile0 = tok0 // 128
                    banks = self.pacq(8)
                    for g in range(NG):
                        wi2 = self.wload(wdn[e, g], ncols=2048)
                        w2t = self.wrot[wi2]
                        items = []
                        for hcl in range(2):
                            hc = g * 2 + hcl
                            for st in range(4):
                                for do in range(2):
                                    items.append((
                                        banks[st * 2 + do],
                                        self.pap(banks[st * 2 + do])[:, :],
                                        self.harena[:, hc * 1024 + tokblk * 512 +
                                                    st * 128:hc * 1024 + tokblk * 512 +
                                                    (st + 1) * 128],
                                        w2t[:, hcl * 1024 + do * 512:
                                            hcl * 1024 + (do + 1) * 512],
                                        hc == 0, hc == HC - 1))
                        self.wrtrk[wi2].rd("tensor")
                        v = self.mmg(items, reads=("harena",))
                        self.wused(wi2, v)
                    for st in range(4):
                        for do in range(2):
                            b = banks[st * 2 + do]
                            tile = tile0 + st
                            if kind == "l0":
                                comb = self.router[:, tile * 16 + 8 + e:tile * 16 + 9 + e]
                            else:
                                comb = (ones_col if e == 7 else
                                        self.router[:, tile * 16 + 8 + e:tile * 16 + 9 + e])
                            self.op("vector", lambda b=b, tile=tile, do=do, comb=comb:
                                    nc.vector.scalar_tensor_tensor(
                                        self.sE[:, tile * D + do * 512:
                                                tile * D + (do + 1) * 512],
                                        self.pap(b)[:, :], comb,
                                        self.sE[:, tile * D + do * 512:
                                                tile * D + (do + 1) * 512],
                                        ALU.mult, ALU.add),
                                    rpsum=(b,), reads=("router", "small"),
                                    writes=("sE",))
        bld.barrier()

    def emit_combine(self, seq, layer):
        nc, bld = self.nc, self.bld
        src = self.w["x"][seq] if layer == 0 else self.x1[seq]
        dst = self.x1[seq] if layer == 0 else self.y[seq]
        xb_ = [self.x0, self.x1b]
        xn_ = ["x0", "x1b"]
        for i in range(TT):
            xb, xnm = xb_[i % 2], xn_[i % 2]
            self.dma_into("sync", xnm, xb[:], src[i * 128:(i + 1) * 128, :])
            if layer == 0:
                self.trk["stats"].rd("scalar")
                self.op("scalar", lambda xb=xb, i=i: nc.scalar.activation(
                    self.t0[:, 0:D], xb[:], AF.Identity,
                    scale=self.stats[:, 2 * i:2 * i + 1],
                    bias=self.stats[:, 2 * i + 1:2 * i + 2]),
                    reads=(xnm, "stats"), writes=("t0",))
                self.op("vector", lambda i=i: nc.vector.tensor_add(
                    self.t0[:, 0:D], self.t0[:, 0:D], self.sE[:, i * D:(i + 1) * D]),
                    reads=("sE", "t0"), writes=("t0",))
                gv = self.op("scalar", lambda: nc.scalar.activation(
                    self.t1[:, 0:D], self.t0[:, 0:D], AF.Silu),
                    reads=("t0",), writes=("t1",))
            else:
                gv = self.op("scalar", lambda i=i: nc.scalar.activation(
                    self.t1[:, 0:D], self.sE[:, i * D:(i + 1) * D], AF.Silu),
                    reads=("sE",), writes=("t1",))
            self.op("vector", lambda i=i: nc.vector.tensor_mul(
                self.t1[:, 0:D], self.t1[:, 0:D], self.sS[:, i * D:(i + 1) * D]),
                reads=("sS", "t1"), writes=("t1",))
            ov = self.op("vector", lambda xb=xb: nc.vector.tensor_add(
                self.t0[:, 0:D], xb[:], self.t1[:, 0:D]),
                reads=(xnm, "t1"), writes=("t0",))
            self.dma_from("sync", "t0", dst[i * 128:(i + 1) * 128, :], self.t0[:, 0:D])
        bld.barrier()

    def emit_layer(self, seq, layer):
        x = self.w["x"]
        if layer == 0:
            self.emit_actT(seq, "ln", x[seq])
            self.maybe_dump(f"actT_{seq}_0", self.actT[:], "actT")
            if "attn0" in self.parts:
                self.emit_attention(seq, "l0")
                self.maybe_dump(f"sS_{seq}_0", self.sS[:], "sS")
            self.emit_actT(seq, "rms1", x[seq])
            if "inner" in self.parts:
                self.emit_attention(seq, "l0i")
                self.maybe_dump(f"sE_{seq}_0", self.sE[:], "sE")
            self.emit_actT(seq, "rms2", x[seq], router="l0" if "moe0" in self.parts else None)
            if "moe0" in self.parts:
                self.emit_moe(seq, "l0")
                self.maybe_dump(f"moe_{seq}_0", self.sE[:], "sE")
            self.emit_combine(seq, 0)
        else:
            self.emit_actT(seq, "ln_affine", self.x1[seq])
            if "attn1" in self.parts:
                self.emit_attention(seq, "l1")
                self.maybe_dump(f"sS_{seq}_1", self.sS[:], "sS")
            self.emit_actT(seq, "recompute_ln", self.x1[seq], router="l1" if "ffn1" in self.parts else None)
            if "ffn1" in self.parts:
                self.emit_moe(seq, "l1")
                self.maybe_dump(f"moe_{seq}_1", self.sE[:], "sE")
            self.emit_combine(seq, 1)

    def emit_final(self):
        bld = self.bld
        bld.barrier()
        for qn in ("sync", "gpsimd"):
            if bld.dcnt[qn]:
                bld.dwait(qn, qn, bld.dcnt[qn])


def _chunk(M):
    Din, F = M.shape
    c = Din // 128
    return np.ascontiguousarray(
        M.reshape(c, 128, F).transpose(1, 0, 2).reshape(128, c * F))


def _heads(M):
    H = M.shape[1] // 64
    return np.ascontiguousarray(
        np.stack([_chunk(M[:, h * 64:(h + 1) * 64]) for h in range(H)]))


def prep_inputs(inputs):
    f = np.float32
    T = lambda a: np.ascontiguousarray(np.asarray(a, f).T)
    w = {}
    w["WqH"] = _heads(T(inputs["L0_attn_Wq"]))
    w["WkH"] = _heads(T(inputs["L0_attn_Wk"]))
    w["WvH"] = _heads(T(inputs["L0_attn_Wv"]))
    w["WpT"] = _chunk(T(inputs["L0_attn_Wp"]))
    w["BqT"] = _chunk(T(inputs["L0_attn_Bq"]))
    w["BkT"] = _chunk(T(inputs["L0_attn_Bk"]))
    w["BvT"] = _chunk(T(inputs["L0_attn_Bv"]))
    w["AqT"] = T(inputs["L0_attn_Aq"])
    w["AkT"] = T(inputs["L0_attn_Ak"])
    w["AvT"] = T(inputs["L0_attn_Av"])
    w["g1"] = np.asarray(inputs["L0_ffn_g1"], f).reshape(1, D)
    w["g2"] = np.asarray(inputs["L0_ffn_g2"], f).reshape(1, D)
    w["fwqH"] = _heads(np.asarray(inputs["L0_ffn_wq"], f))
    w["fwkH"] = _heads(np.asarray(inputs["L0_ffn_wk"], f))
    w["fwvH"] = _heads(np.asarray(inputs["L0_ffn_wv"], f))
    w["fwo"] = _chunk(np.asarray(inputs["L0_ffn_wo"], f))
    w["gateT"] = np.ascontiguousarray(np.asarray(inputs["L0_ffn_gate"], f).T)
    w1 = np.asarray(inputs["L0_ffn_w1"], f)
    w3 = np.asarray(inputs["L0_ffn_w3"], f)
    w2 = np.asarray(inputs["L0_ffn_w2"], f)
    w["w1C"] = np.ascontiguousarray(
        w1.reshape(4, 8, 128, 12, 2, 128).transpose(0, 3, 2, 4, 1, 5).reshape(4, 12, 128, 2048))
    w["w3C"] = np.ascontiguousarray(
        w3.reshape(4, 8, 128, 12, 2, 128).transpose(0, 3, 2, 4, 1, 5).reshape(4, 12, 128, 2048))
    w["w2C"] = np.ascontiguousarray(
        w2.reshape(4, 12, 2, 128, 1024).transpose(0, 1, 3, 2, 4).reshape(4, 12, 128, 2048))
    w["l1wqH"] = _heads(T(inputs["L1_attn_w_q"]))
    w["l1wkvT"] = _chunk(T(inputs["L1_attn_w_kv"]))
    w["l1woT"] = _chunk(T(inputs["L1_attn_w_out"]))
    w["ln_g"] = np.asarray(inputs["L1_attn_ln_g"], f).reshape(1, D)
    w["ln_b"] = np.asarray(inputs["L1_attn_ln_b"], f).reshape(1, D)
    w["lno_g"] = np.asarray(inputs["L1_attn_lno_g"], f).reshape(1, D)
    w["lno_b"] = np.asarray(inputs["L1_attn_lno_b"], f).reshape(1, D)
    w["WrT"] = np.ascontiguousarray(np.asarray(inputs["L1_ffn_Wr"], f).T)
    Wu = np.concatenate([np.asarray(inputs["L1_ffn_Wu"], f),
                         np.asarray(inputs["L1_ffn_Wu_s"], f)[None]], 0)
    Wg = np.concatenate([np.asarray(inputs["L1_ffn_Wg"], f),
                         np.asarray(inputs["L1_ffn_Wg_s"], f)[None]], 0)
    Wd = np.concatenate([np.asarray(inputs["L1_ffn_Wd"], f),
                         np.asarray(inputs["L1_ffn_Wd_s"], f)[None]], 0)
    w["WuC"] = np.ascontiguousarray(
        Wu.reshape(8, 8, 128, 16, 2, 128).transpose(0, 3, 2, 4, 1, 5).reshape(8, 16, 128, 2048))
    w["WgC"] = np.ascontiguousarray(
        Wg.reshape(8, 8, 128, 16, 2, 128).transpose(0, 3, 2, 4, 1, 5).reshape(8, 16, 128, 2048))
    w["WdC"] = np.ascontiguousarray(
        Wd.reshape(8, 16, 2, 128, 1024).transpose(0, 1, 3, 2, 4).reshape(8, 16, 128, 2048))
    return w


def kernel(**inputs):
    x = np.asarray(inputs["L0_attn_x"], np.float32)
    w = prep_inputs(inputs)
    mod = Mod()
    nc = mod.build()
    in_maps = []
    for core in range(8):
        m = dict(w)
        m["x"] = np.ascontiguousarray(x[2 * core:2 * core + 2])
        in_maps.append(m)
    res = run_bass_kernel_spmd(nc, in_maps, core_ids=list(range(8)))
    return np.concatenate([r["y"] for r in res.results], axis=0)
